# version 15
# baseline (speedup 1.0000x reference)
from contextlib import ExitStack

import numpy as np

import concourse.bass as bass
import concourse.mybir as mybir
from concourse.bass_utils import run_bass_kernel_spmd

F32 = mybir.dt.float32
BF16 = mybir.dt.bfloat16
AF = mybir.ActivationFunctionType
ALU = mybir.AluOpType
AX = mybir.AxisListType

T = 4096
C = 1024
H = 16
NT = T // 128
NCC = C // 128
NPAIR = H // 2
EPS = 1e-5
NEG_BIG = -1.0e33

ENGINES = ("sync", "gpsimd", "scalar", "vector", "tensor")

DEBUG_OUTPUTS = False


class _Op:
    __slots__ = ("eng", "fn", "deps", "step", "idx", "val", "awaited")

    def __init__(self, eng, fn, deps, step, idx):
        self.eng = eng
        self.fn = fn
        self.deps = deps
        self.step = step
        self.idx = idx
        self.val = None
        self.awaited = False


class Buf:
    __slots__ = ("name", "w", "rs")

    def __init__(self, name):
        self.name = name
        self.w = None
        self.rs = []


class Prog:
    def __init__(self):
        self.ops = []
        self._last_dma = {}

    def add(self, eng, fn, reads=(), writes=(), deps=(), dma=False):
        d = []
        if dma:
            prev = self._last_dma.get(eng)
            if prev is not None:
                d.append(prev)
        for b in reads:
            if b.w is not None:
                d.append(b.w)
        for b in writes:
            d.extend(b.rs)
            if b.w is not None:
                d.append(b.w)
        d.extend(x for x in deps if x is not None)
        seen = set()
        dd = []
        for x in d:
            if id(x) not in seen:
                seen.add(id(x))
                dd.append(x)
        op = _Op(eng, fn, dd, 16 if dma else 1, len(self.ops))
        if dma:
            self._last_dma[eng] = op
        for x in dd:
            assert x.idx < op.idx
        for b in reads:
            b.rs.append(op)
        for b in writes:
            b.w = op
            b.rs = []
        self.ops.append(op)
        return op

    def emit(self, block, sems):
        for op in self.ops:
            if op.step == 16:
                op.awaited = True
            for dep in op.deps:
                if dep.eng != op.eng or dep.step == 16:
                    dep.awaited = True
        cnt = {e: 0 for e in ENGINES}
        for op in self.ops:
            if op.awaited:
                cnt[op.eng] += op.step
                op.val = cnt[op.eng]

        def make_runner(eng_name):
            def run(eng):
                waited = {}
                for op in self.ops:
                    if op.eng != eng_name:
                        continue
                    need = {}
                    for dep in op.deps:
                        if dep.eng == eng_name and dep.step != 16:
                            continue
                        if dep.val is None:
                            continue
                        s = dep.eng
                        if need.get(s, 0) < dep.val:
                            need[s] = dep.val
                    for s, v in need.items():
                        if waited.get(s, 0) < v:
                            eng.wait_ge(sems[s], v)
                            waited[s] = v
                    ins = op.fn(eng)
                    if op.awaited:
                        assert ins is not None, "awaited op returned None"
                        ins.then_inc(sems[op.eng], op.step)
            return run

        block.sync(make_runner("sync"))
        block.gpsimd(make_runner("gpsimd"))
        block.scalar(make_runner("scalar"))
        block.vector(make_runner("vector"))
        block.tensor(make_runner("tensor"))


def _ceil(a, b):
    return -(-a // b)


def build_program():
    nc = bass.Bass("TRN2", target_bir_lowering=False)

    x_h = nc.dram_tensor("x", [T, C], F32, kind="ExternalInput")
    wq_h = nc.dram_tensor("wq", [H, C, 64], F32, kind="ExternalInput")
    wk_h = nc.dram_tensor("wk", [H, C, 64], F32, kind="ExternalInput")
    wv_h = nc.dram_tensor("wv", [H, C, 64], F32, kind="ExternalInput")
    wp_h = nc.dram_tensor("wp", [C, C], F32, kind="ExternalInput")
    bp_h = nc.dram_tensor("bp", [C], F32, kind="ExternalInput")
    rel_h = nc.dram_tensor("rel", [33 * H], F32, kind="ExternalInput")
    wr_h = nc.dram_tensor("wr", [C, C], F32, kind="ExternalInput")
    wkf_h = nc.dram_tensor("wkf", [C, C], F32, kind="ExternalInput")
    wvf_h = nc.dram_tensor("wvf", [C, C], F32, kind="ExternalInput")
    mur_h = nc.dram_tensor("mur", [C], F32, kind="ExternalInput")
    muk_h = nc.dram_tensor("muk", [C], F32, kind="ExternalInput")

    spillA_h = nc.dram_tensor("spillA", [NPAIR, 128, T], BF16, kind="Internal")
    spillX_h = nc.dram_tensor("spillX", [T, C], F32, kind="Internal")
    out_h = nc.dram_tensor("out", [T, C], F32, kind="ExternalOutput")
    dbg = {}
    if DEBUG_OUTPUTS:
        dbg["xnT"] = nc.dram_tensor("dbg_xnT", [128, NCC * T], BF16, kind="ExternalOutput")
        dbg["attnT"] = nc.dram_tensor("dbg_attnT", [NPAIR, 128, T], BF16, kind="ExternalOutput")
        dbg["xnew"] = nc.dram_tensor("dbg_xnew", [T, C], F32, kind="ExternalOutput")
        dbg["st"] = nc.dram_tensor("dbg_st", [128, 64], F32, kind="ExternalOutput")
        dbg["x0"] = nc.dram_tensor("dbg_x0", [128, C], F32, kind="ExternalOutput")
        dbg["pg"] = nc.dram_tensor("dbg_pg", [128, 4 * T], BF16, kind="ExternalOutput")
        dbg["dg"] = nc.dram_tensor("dbg_dg", [128, 512], BF16, kind="ExternalOutput")
        dbg["nm"] = nc.dram_tensor("dbg_nm", [128, 64], F32, kind="ExternalOutput")
        dbg["bd"] = nc.dram_tensor("dbg_bd", [128, 256], BF16, kind="ExternalOutput")
        dbg["qa"] = nc.dram_tensor("dbg_qa", [128, T], BF16, kind="ExternalOutput")
        dbg["qp"] = nc.dram_tensor("dbg_qp", [128, T], BF16, kind="ExternalOutput")
        dbg["kp"] = nc.dram_tensor("dbg_kp", [128, T], BF16, kind="ExternalOutput")
        dbg["kr"] = nc.dram_tensor("dbg_kr", [128, 1], BF16, kind="ExternalOutput")

    prog = Prog()
    add = prog.add

    with ExitStack() as st:
        def sb(name, shape, dt):
            return st.enter_context(nc.sbuf_tensor(name, shape, dt))

        ident = sb("ident", [128, 128], BF16)
        ones_f = sb("ones_f", [128, 128], F32)
        zeros_f = sb("zeros_f", [128, 128], F32)
        causal_f = sb("causal_f", [128, 128], F32)
        rel8 = sb("rel8", [128, 528], F32)
        bdiag = sb("bdiag", [128, 16 * 128], BF16)
        bsub = sb("bsub", [128, 16 * 128], BF16)
        diag4 = sb("diag4", [128, 4 * 128], BF16)
        bp_b = sb("bp_b", [128, C], F32)
        mu_sb = sb("mu_sb", [128, 32], F32)
        negb16 = sb("negb16", [128, 16], F32)
        stats = sb("stats", [128, 64], F32)
        negm = sb("negm", [128, 2 * NT], F32)
        lparts = sb("lparts", [128, 24], F32)
        gtmp = sb("gtmp", [128, 2], F32)
        krb = sb("krb", [128, 1], BF16)
        epsc = sb("epsc", [128, 1], F32)
        gdiff = sb("gdiff", [128, 272], F32)
        gdiff2 = sb("gdiff2", [128, 272], F32)

        xt = sb("xt", [128, 2 * C], F32)
        ysb = sb("ysb", [128, 2 * C], F32)
        y2 = sb("y2", [128, 2 * C], F32)
        sqd = sb("sqd", [128, C], F32)

        U1 = sb("U1", [128, 32832], BF16)
        U2 = sb("U2", [128, 46592], BF16)

        def u1(off, n):
            return U1[:, off:off + n]

        def u2(off, n):
            return U2[:, off:off + n]

        qpair = u2(0, T)
        kpair = u2(T, T)
        vgrp = u2(2 * T, 2 * T + 128)
        pgrpT = u2(16512, 16384)
        masks = u2(16512, 33 * 128)
        qabs = u2(20608, T)
        attnT = u2(32896, T)
        wq_g = u2(36992, 2048)
        wk_g = u2(39040, 2048)
        wv_g = u2(41088, 2048)
        xn1t = u2(43136, 2048)
        tmp_b = [u2(43136, 512), u2(43648, 512)]
        lrecip = u2(44160, 512)
        wp_sb = u2(0, 8192)
        wrT = u2(8192, 8192)
        wkT = u2(16384, 8192)
        wvT = u2(24576, 8192)
        aslab = u2(32768, 8192)
        slab = u2(40960, 2048)
        dscr = u2(45696, 512)
        xrT = u2(0, T)
        xkT = u2(T, T)
        kact = u2(32768, T)
        r_sb = u2(36864, T)

        xnT = u1(0, NCC * T)
        xn2T = u1(0, NCC * (T + 8))

        PA = st.enter_context(nc.psum_tensor("PA", [128, 1024], F32))
        PB = st.enter_context(nc.psum_tensor("PB", [128, 1024], F32))
        PC = st.enter_context(nc.psum_tensor("PC", [128, 512], F32))
        PD = st.enter_context(nc.psum_tensor("PD", [128, 512], F32))
        PE_ = st.enter_context(nc.psum_tensor("PE", [128, 512], F32))
        PF = st.enter_context(nc.psum_tensor("PF", [128, 512], F32))

        sems = {e: st.enter_context(nc.semaphore("sem_" + e)) for e in ENGINES}

        b_ident = Buf("ident"); b_ones = Buf("ones"); b_zeros = Buf("zeros")
        b_causal = Buf("causal"); b_rel8 = Buf("rel8")
        b_bdiag = [Buf(f"bd{h}") for h in range(H)]
        b_bsub = [Buf(f"bs{h}") for h in range(H)]
        b_masks = Buf("masks"); b_bp = Buf("bp"); b_mu = Buf("mu")
        b_negb = Buf("negb")
        b_stats = [Buf(f"st{i}") for i in range(4)]
        b_negm = [Buf("negm0"), Buf("negm1")]
        b_lparts = Buf("lparts"); b_gtmp = Buf("gtmp")
        b_diag4 = [Buf(f"dg{i}") for i in range(4)]
        b_xnT = Buf("xnT"); b_qabs = Buf("qabs")
        b_pgT = [Buf(f"pgT{j}") for j in range(NT)]
        b_negmg = Buf("negmg"); b_linv = Buf("linv"); b_lrec = Buf("lrec")
        b_qpair = Buf("qpair"); b_kpair = Buf("kpair"); b_vgrp = Buf("vgrp")
        b_attnT = Buf("attnT")
        b_wqg = Buf("wqg"); b_wkg = Buf("wkg"); b_wvg = Buf("wvg")
        b_xn1t = [Buf("xn1t0"), Buf("xn1t1")]
        b_xn2T = Buf("xn2T"); b_wp = Buf("wp")
        b_wrT = Buf("wrT"); b_wkTb = Buf("wkT"); b_wvTb = Buf("wvT")
        b_slab = [Buf("slab0"), Buf("slab1")]
        b_aslab = [Buf("aslab0"), Buf("aslab1")]
        b_xrT = Buf("xrT"); b_xkT = Buf("xkT"); b_kact = Buf("kact")
        b_rsb = Buf("rsb"); b_dscr = Buf("dscr")
        b_xt = [Buf("xt0"), Buf("xt1")]
        b_ysb = [Buf("ysb0"), Buf("ysb1")]
        b_y2 = [Buf("y20"), Buf("y21")]; b_sqd = Buf("sqd")
        b_krmax = Buf("krmax"); b_eps = Buf("eps")
        b_gdiff = Buf("gdiff"); b_gdiff2 = Buf("gdiff2")
        b_PA = Buf("PA"); b_PB = Buf("PB"); b_PC = Buf("PC")
        b_PD = Buf("PD"); b_PE = Buf("PE"); b_PF = Buf("PF")
        b_spillA = [Buf(f"spA{p}") for p in range(NPAIR)]
        b_spillX = [Buf(f"spX{i}") for i in range(NT)]
        b_out = [Buf(f"out{i}") for i in range(NT)]

        add("vector", lambda e: e.memset(ones_f[:, :], 1.0), writes=[b_ones])
        add("vector", lambda e: e.memset(epsc[:, :], EPS), writes=[b_eps])
        add("vector", lambda e: e.memset(zeros_f[:, :], 0.0), writes=[b_zeros])
        add("vector", lambda e: e.memset(diag4[:, 0:128], 1.0), writes=[b_diag4[0]])
        add("gpsimd", lambda e: e.affine_select(
            ident[:, :], diag4[:, 0:128], pattern=[[1, 128]], base=0,
            channel_multiplier=-1, compare_op=ALU.is_equal, fill=0.0),
            reads=[b_diag4[0]], writes=[b_ident])
        add("vector", lambda e: e.memset(diag4[:, 128:256], 1.0),
            writes=[b_diag4[1]])
        add("vector", lambda e: e.memset(ysb[:, 0:128], NEG_BIG),
            writes=[b_ysb[0]])
        add("gpsimd", lambda e: e.affine_select(
            causal_f[:, :], ysb[:, 0:128], pattern=[[1, 128]], base=-1,
            channel_multiplier=-1, compare_op=ALU.is_ge, fill=0.0),
            reads=[b_ysb[0]], writes=[b_causal])

        def _mask_op(slot, base):
            def f(e):
                return e.affine_select(
                    masks[:, 128 * slot:128 * slot + 128], diag4[:, 0:128],
                    pattern=[[1, 128]], base=base, channel_multiplier=-1,
                    compare_op=ALU.is_equal, fill=0.0)
            return f
        for d in range(17):
            add("gpsimd", _mask_op(d, d), reads=[b_diag4[0]],
                writes=[b_masks] + b_pgT[0:9])
        for d in range(1, 17):
            add("gpsimd", _mask_op(17 + d - 1, d - 128), reads=[b_diag4[0]],
                writes=[b_masks] + b_pgT[0:9])

        add("sync", lambda e: e.dma_start(
            out=rel8[:, :], in_=rel_h.ap().rearrange("(one c) -> one c", one=1)
            .partition_broadcast(128)), writes=[b_rel8], dma=True)
        add("sync", lambda e: e.dma_start(
            out=bp_b[:, :], in_=bp_h.ap().rearrange("(one c) -> one c", one=1)
            .partition_broadcast(128)), writes=[b_bp], dma=True)
        add("gpsimd", lambda e: e.dma_start(
            out=mu_sb[:, 0:8], in_=mur_h.ap().rearrange("(cc p) -> p cc", p=128),
            allow_slow_non_contiguous=True), writes=[b_mu], dma=True)
        add("gpsimd", lambda e: e.dma_start(
            out=mu_sb[:, 8:16], in_=muk_h.ap().rearrange("(cc p) -> p cc", p=128),
            allow_slow_non_contiguous=True), writes=[b_mu], dma=True)
        add("vector", lambda e: e.tensor_tensor(
            mu_sb[:, 16:32], ones_f[:, 0:16], mu_sb[:, 0:16], ALU.subtract),
            reads=[b_mu, b_ones], writes=[b_mu])

        add("vector", lambda e: e.tensor_scalar(
            rel8[:, :], rel8[:, :], 8.0, None, ALU.mult),
            reads=[b_rel8], writes=[b_rel8])
        add("vector", lambda e: e.reduce_max(
            stats[:, 63:64], rel8[:, :], axis=AX.X, apply_absolute_value=True),
            reads=[b_rel8], writes=[b_stats[0]])
        add("scalar", lambda e: e.mul(
            stats[:, 62:63], stats[:, 63:64], 0.25),
            reads=[b_stats[0]], writes=[b_stats[0]])
        add("vector", lambda e: e.tensor_scalar(
            negb16[:, :], ones_f[:, 0:16], stats[:, 62:63], None, ALU.mult),
            reads=[b_ones, b_stats[0]], writes=[b_negb])

        for d in range(17):
            add("vector", (lambda d=d: lambda e: e.tensor_tensor(
                gdiff[:, 16 * d:16 * d + 16], rel8[:, (16 - d) * H:(16 - d) * H + 16],
                rel8[:, 0:16], ALU.subtract))(),
                reads=[b_rel8], writes=[b_gdiff])
        add("scalar", lambda e: e.copy(gdiff2[:, :], gdiff[:, :]),
            reads=[b_gdiff], writes=[b_gdiff2])
        for h in range(H):
            add("vector", (lambda h=h: lambda e: e.tensor_copy(
                bdiag[:, 128 * h:128 * h + 128], causal_f[:, :]))(),
                reads=[b_causal], writes=[b_bdiag[h]])
            add("vector", (lambda h=h: lambda e: e.memset(
                bsub[:, 128 * h:128 * h + 128], 0.0))(),
                writes=[b_bsub[h]])
            for d in range(17):
                add("vector", (lambda h=h, d=d: lambda e: e.scalar_tensor_tensor(
                    bdiag[:, 128 * h:128 * h + 128],
                    masks[:, 128 * d:128 * d + 128], gdiff2[:, 16 * d + h:16 * d + h + 1],
                    bdiag[:, 128 * h:128 * h + 128], ALU.mult, ALU.add))(),
                    reads=[b_masks, b_gdiff2, b_bdiag[h]] + b_pgT[0:9],
                    writes=[b_bdiag[h]])
                if d >= 1:
                    add("vector", (lambda h=h, d=d: lambda e: e.scalar_tensor_tensor(
                        bsub[:, 128 * h:128 * h + 128],
                        masks[:, 128 * (17 + d - 1):128 * (17 + d - 1) + 128],
                        gdiff2[:, 16 * d + h:16 * d + h + 1],
                        bsub[:, 128 * h:128 * h + 128], ALU.mult, ALU.add))(),
                        reads=[b_masks, b_gdiff2, b_bsub[h]] + b_pgT[0:9],
                        writes=[b_bsub[h]])

        def ln_stats_apply(xin_ap, b_xin, out_ap, b_out_buf, sidx,
                           dump_ap=None, b_dump=None):
            s0 = 12 * sidx
            add("vector", lambda e: e.reduce_sum(
                stats[:, s0:s0 + 1], xin_ap, axis=AX.X),
                reads=[b_xin], writes=[b_stats[sidx]])
            add("vector", lambda e: e.scalar_tensor_tensor(
                dump_ap, xin_ap, 1.0, xin_ap, ALU.mult, ALU.mult,
                accum_out=stats[:, s0 + 1:s0 + 2]),
                reads=[b_xin], writes=[b_dump, b_stats[sidx]])
            add("scalar", lambda e: e.mul(
                stats[:, s0 + 2:s0 + 3], stats[:, s0:s0 + 1], 1.0 / C),
                reads=[b_stats[sidx]], writes=[b_stats[sidx]])
            add("scalar", lambda e: e.mul(
                stats[:, s0 + 3:s0 + 4], stats[:, s0:s0 + 1], 1.0 / C),
                reads=[b_stats[sidx]], writes=[b_stats[sidx]])
            add("vector", lambda e: e.tensor_scalar(
                stats[:, s0 + 4:s0 + 5], stats[:, s0 + 2:s0 + 3],
                stats[:, s0 + 3:s0 + 4], None, ALU.mult),
                reads=[b_stats[sidx]], writes=[b_stats[sidx]])
            add("gpsimd", lambda e: e.tensor_copy(
                stats[:, s0 + 5:s0 + 6], stats[:, s0 + 4:s0 + 5]),
                reads=[b_stats[sidx]], writes=[b_stats[sidx]])
            add("vector", lambda e: e.scalar_tensor_tensor(
                stats[:, s0 + 6:s0 + 7], stats[:, s0 + 1:s0 + 2], 1.0 / C,
                stats[:, s0 + 5:s0 + 6], ALU.mult, ALU.subtract),
                reads=[b_stats[sidx]], writes=[b_stats[sidx]])
            add("scalar", lambda e: e.activation(
                stats[:, s0 + 7:s0 + 8], stats[:, s0 + 6:s0 + 7], AF.Ln,
                bias=epsc[:, :], scale=1.0),
                reads=[b_stats[sidx], b_eps], writes=[b_stats[sidx]])
            add("vector", lambda e: e.tensor_copy(
                stats[:, s0 + 8:s0 + 9], stats[:, s0 + 7:s0 + 8]),
                reads=[b_stats[sidx]], writes=[b_stats[sidx]])
            add("scalar", lambda e: e.activation(
                stats[:, s0 + 9:s0 + 10], stats[:, s0 + 8:s0 + 9], AF.Exp,
                bias=0.0, scale=-0.5),
                reads=[b_stats[sidx]], writes=[b_stats[sidx]])
            add("vector", lambda e: e.tensor_scalar(
                out_ap, xin_ap, stats[:, s0 + 2:s0 + 3], stats[:, s0 + 9:s0 + 10],
                ALU.subtract, ALU.mult),
                reads=[b_xin, b_stats[sidx]], writes=[b_out_buf])

        def ln_pair(specs):
            for (xin_ap, b_xin, _o, _bo, sidx, dump_ap, b_dump) in specs:
                s0 = 12 * sidx
                add("vector", (lambda xin_ap=xin_ap, s0=s0: lambda e:
                               e.reduce_sum(stats[:, s0:s0 + 1], xin_ap,
                                            axis=AX.X))(),
                    reads=[b_xin], writes=[b_stats[sidx]])
            for (xin_ap, b_xin, _o, _bo, sidx, dump_ap, b_dump) in specs:
                s0 = 12 * sidx
                add("vector", (lambda xin_ap=xin_ap, s0=s0, dump_ap=dump_ap:
                               lambda e: e.scalar_tensor_tensor(
                    dump_ap, xin_ap, 1.0, xin_ap, ALU.mult, ALU.mult,
                    accum_out=stats[:, s0 + 1:s0 + 2]))(),
                    reads=[b_xin], writes=[b_dump, b_stats[sidx]])
            for (_x, _bx, _o, _bo, sidx, _d, _bd) in specs:
                s0 = 12 * sidx
                add("vector", (lambda s0=s0: lambda e: e.tensor_scalar(
                    stats[:, s0 + 2:s0 + 3], stats[:, s0:s0 + 1], 1.0 / C,
                    None, ALU.mult))(),
                    reads=[b_stats[sidx]], writes=[b_stats[sidx]])
            for (_x, _bx, _o, _bo, sidx, _d, _bd) in specs:
                s0 = 12 * sidx
                add("vector", (lambda s0=s0: lambda e: e.scalar_tensor_tensor(
                    stats[:, s0 + 3:s0 + 4], stats[:, s0:s0 + 1], 1.0 / C,
                    stats[:, s0 + 2:s0 + 3], ALU.mult, ALU.mult))(),
                    reads=[b_stats[sidx]], writes=[b_stats[sidx]])
            for (_x, _bx, _o, _bo, sidx, _d, _bd) in specs:
                s0 = 12 * sidx
                add("gpsimd", (lambda s0=s0: lambda e: e.tensor_copy(
                    stats[:, s0 + 7:s0 + 8], stats[:, s0 + 1:s0 + 2]))(),
                    reads=[b_stats[sidx]], writes=[b_stats[sidx]])
            for (_x, _bx, _o, _bo, sidx, _d, _bd) in specs:
                s0 = 12 * sidx
                add("vector", (lambda s0=s0: lambda e: e.scalar_tensor_tensor(
                    stats[:, s0 + 4:s0 + 5], stats[:, s0 + 7:s0 + 8], 1.0 / C,
                    stats[:, s0 + 3:s0 + 4], ALU.mult, ALU.subtract))(),
                    reads=[b_stats[sidx]], writes=[b_stats[sidx]])
            for (_x, _bx, _o, _bo, sidx, _d, _bd) in specs:
                s0 = 12 * sidx
                add("scalar", (lambda s0=s0: lambda e: e.activation(
                    stats[:, s0 + 5:s0 + 6], stats[:, s0 + 4:s0 + 5], AF.Ln,
                    bias=epsc[:, :], scale=1.0))(),
                    reads=[b_stats[sidx], b_eps], writes=[b_stats[sidx]])
            for (_x, _bx, _o, _bo, sidx, _d, _bd) in specs:
                s0 = 12 * sidx
                add("scalar", (lambda s0=s0: lambda e: e.activation(
                    stats[:, s0 + 6:s0 + 7], stats[:, s0 + 5:s0 + 6], AF.Exp,
                    bias=0.0, scale=-0.5))(),
                    reads=[b_stats[sidx]], writes=[b_stats[sidx]])
            for (xin_ap, b_xin, out_ap, b_out_buf, sidx, _d, _bd) in specs:
                s0 = 12 * sidx
                add("vector", (lambda xin_ap=xin_ap, out_ap=out_ap, s0=s0:
                               lambda e: e.tensor_scalar(
                    out_ap, xin_ap, stats[:, s0 + 2:s0 + 3],
                    stats[:, s0 + 6:s0 + 7], ALU.subtract, ALU.mult))(),
                    reads=[b_xin, b_stats[sidx]], writes=[b_out_buf])


        xnT3 = xnT.rearrange("p (cc t) -> p cc t", cc=NCC)
        PSP = {0: (PA, b_PA), 1: (PB, b_PB)}

        for k2 in range(NT // 2):
            iA, iB = 2 * k2, 2 * k2 + 1
            tl = []
            for i, par in ((iA, 0), (iB, 1)):
                ps, b_ps = PSP[par]
                xin = xt[:, C * par:C * par + C]
                add("sync", (lambda i=i, xin=xin: lambda e: e.dma_start(
                    out=xin, in_=x_h[128 * i:128 * i + 128, :]))(),
                    writes=[b_xt[par]], dma=True)
                tl.append((i, par, ps, b_ps, xin))
            ln_pair([(xin, b_xt[par], xn1t[:, C * par:C * par + C],
                      b_xn1t[par], par, y2[:, C * par:C * par + C], b_y2[par])
                     for (i, par, ps, b_ps, xin) in tl])
            for (i, par, ps, b_ps, xin) in tl:
                x1 = xn1t[:, C * par:C * par + C]
                for cc in range(NCC):
                    add("tensor", (lambda cc=cc, x1=x1, ps=ps: lambda e: e.matmul(
                        ps[:, 128 * cc:128 * cc + 128],
                        x1[:, 128 * cc:128 * cc + 128], ident[:, :],
                        start=True, stop=True))(),
                        reads=[b_xn1t[par], b_ident], writes=[b_ps])
                add("vector", (lambda i=i, ps=ps: lambda e: e.tensor_copy(
                    xnT3[:, :, 128 * i:128 * i + 128],
                    ps[:, :].rearrange("p (cc t) -> p cc t", cc=NCC)))(),
                    reads=[b_ps], writes=[b_xnT])

        if DEBUG_OUTPUTS:
            add("sync", lambda e: e.dma_start(out=dbg["xnT"].ap(), in_=xnT),
                reads=[b_xnT], dma=True)

        wq3 = wq_g.rearrange("p (pp cc x) -> p pp cc x", pp=2, cc=NCC)
        wk3 = wk_g.rearrange("p (pp cc x) -> p pp cc x", pp=2, cc=NCC)
        wv3 = wv_g.rearrange("p (cc x) -> p cc x", cc=NCC)
        vg4 = vgrp.rearrange("p (t m e) -> p t m e", m=4, e=65)
        add("vector", lambda e: e.memset(vg4[:, :, :, 64:65], 1.0),
            writes=[b_vgrp])

        def wload(dst3_sel, src_h, head):
            def f(e):
                return e.dma_start(
                    out=dst3_sel,
                    in_=src_h.ap()[head].rearrange("(cc p) d -> p cc d", p=128))
            return f

        for g in range(4):
            pairs = (2 * g, 2 * g + 1)
            for pp, p in enumerate(pairs):
                for h01 in range(2):
                    h = 2 * p + h01
                    add("gpsimd", wload(wq3[:, pp, :, 64 * h01:64 * h01 + 64], wq_h, h),
                        writes=[b_wqg], dma=True)
                    add("gpsimd", wload(wk3[:, pp, :, 64 * h01:64 * h01 + 64], wk_h, h),
                        writes=[b_wkg], dma=True)
            for m in range(4):
                add("gpsimd", wload(wv3[:, :, 64 * m:64 * m + 64], wv_h, 4 * g + m),
                    writes=[b_wvg], dma=True)

            for ti in range(NT):
                ps, b_ps = (PC, b_PC) if ti % 2 == 0 else (PD, b_PD)
                for cc in range(NCC):
                    add("tensor", (lambda ti=ti, cc=cc, ps=ps: lambda e: e.matmul(
                        ps[:, 0:256],
                        xnT[:, T * cc + 128 * ti: T * cc + 128 * ti + 128],
                        wv_g[:, 256 * cc:256 * cc + 256],
                        start=(cc == 0), stop=(cc == NCC - 1)))(),
                        reads=[b_xnT, b_wvg], writes=[b_ps])
                add("vector", (lambda ti=ti, ps=ps: lambda e: e.tensor_copy(
                    vg4[:, ti, :, 0:64],
                    ps[:, 0:256].rearrange("p (m e) -> p m e", m=4)))(),
                    reads=[b_ps], writes=[b_vgrp])

            for pp, p in enumerate(pairs):
                for which, dst, b_dst, wsb3, b_w in (
                        (0, qpair, b_qpair, wq3, b_wqg),
                        (1, kpair, b_kpair, wk3, b_wkg)):
                    for tc in range(8):
                        ps, b_ps = (PA, b_PA) if tc % 2 == 0 else (PB, b_PB)
                        for cc in range(NCC):
                            add("tensor", (lambda pp=pp, tc=tc, cc=cc, ps=ps,
                                           wsb3=wsb3: lambda e: e.matmul(
                                ps[:, 0:512],
                                wsb3[:, pp, cc, :],
                                xnT[:, T * cc + 512 * tc: T * cc + 512 * tc + 512],
                                start=(cc == 0), stop=(cc == NCC - 1)))(),
                                reads=[b_xnT, b_w], writes=[b_ps])
                        add("vector", (lambda tc=tc, ps=ps, dst=dst: lambda e:
                            e.tensor_copy(dst[:, 512 * tc:512 * tc + 512],
                                          ps[:, 0:512]))(),
                            reads=[b_ps], writes=[b_dst])

                add("vector", lambda e: e.tensor_scalar(
                    qabs.bitcast(mybir.dt.uint16), qpair.bitcast(mybir.dt.uint16),
                    0x7FFF, None, ALU.bitwise_and),
                    reads=[b_qpair], writes=[b_qabs] + b_pgT[8:16])
                add("vector", lambda e: e.reduce_max(
                    stats[:, 61:62], kpair[:, :], axis=AX.X,
                    apply_absolute_value=True),
                    reads=[b_kpair], writes=[b_krmax])
                add("scalar", lambda e: e.copy(krb[:, :], stats[:, 61:62]),
                    reads=[b_krmax], writes=[b_krmax])
                if DEBUG_OUTPUTS and p == 0:
                    add("gpsimd", lambda e: e.dma_start(out=dbg["qa"].ap(), in_=qabs),
                        reads=[b_qabs], dma=True)
                    add("gpsimd", lambda e: e.dma_start(out=dbg["qp"].ap(), in_=qpair),
                        reads=[b_qpair], dma=True)
                    add("gpsimd", lambda e: e.dma_start(out=dbg["kp"].ap(), in_=kpair),
                        reads=[b_kpair], dma=True)
                    add("gpsimd", lambda e: e.dma_start(out=dbg["kr"].ap(), in_=krb[:, :]),
                        reads=[b_krmax], dma=True)
                for h01 in range(2):
                    rows = slice(64 * h01, 64 * h01 + 64)
                    for qg in range(8):
                        seg = 8 * h01 + qg
                        psM, b_psM = (PC, b_PC) if seg % 2 == 0 else (PD, b_PD)
                        add("tensor", (lambda rows=rows, qg=qg, psM=psM:
                                       lambda e: e.matmul(
                            psM[0:1, 0:512], krb[rows, 0:1],
                            qabs[rows, 512 * qg:512 * qg + 512],
                            start=True, stop=True))(),
                            reads=[b_krmax, b_qabs], writes=[b_psM])
                        add("vector", (lambda seg=seg, psM=psM: lambda e:
                                       e.reduce_max(
                            lparts[0:1, seg:seg + 1], psM[0:1, 0:512],
                            axis=AX.X))(),
                            reads=[b_psM], writes=[b_lparts])
                for h01 in range(2):
                    add("vector", (lambda h01=h01: lambda e: e.reduce_max(
                        lparts[0:1, 16 + h01:17 + h01],
                        lparts[0:1, 8 * h01:8 * h01 + 8], axis=AX.X))(),
                        reads=[b_lparts], writes=[b_lparts])
                for h01 in range(2):
                    add("scalar", (lambda h01=h01: lambda e: e.copy(
                        diag4[0:1, 256 + h01:257 + h01],
                        lparts[0:1, 16 + h01:17 + h01]))(),
                        reads=[b_lparts], writes=[b_diag4[2]])
                    psM, b_psM = (PC, b_PC) if h01 == 0 else (PD, b_PD)
                    add("tensor", (lambda h01=h01, psM=psM: lambda e: e.matmul(
                        psM[:, 0:1], diag4[0:1, 128:256],
                        diag4[0:1, 256 + h01:257 + h01],
                        start=True, stop=True))(),
                        reads=[b_diag4[1], b_diag4[2]], writes=[b_psM])
                    add("vector", (lambda h01=h01, psM=psM: lambda e:
                                   e.scalar_tensor_tensor(
                        negm[:, 16 + h01:17 + h01], psM[:, 0:1], 0.125,
                        negb16[:, 0:1], ALU.mult, ALU.add))(),
                        reads=[b_psM, b_negb], writes=[b_negmg])
                for h01 in range(2):
                    add("vector", (lambda h01=h01: lambda e: e.tensor_tensor(
                        negm[:, h01:h01 + 1], zeros_f[:, 0:1],
                        negm[:, 16 + h01:17 + h01], ALU.subtract))(),
                        reads=[b_zeros, b_negmg], writes=[b_negmg])

                pend_tail = [None]

                def emit_seg(h01, qg, pp=pp, p=p):
                    h = 2 * p + h01
                    rows = slice(64 * h01, 64 * h01 + 64)
                    m = 2 * pp + h01
                    seg = 8 * h01 + qg
                    jmax = 4 * qg + 3
                    psV, b_psV = (PE_, b_PE) if seg % 2 == 0 else (PF, b_PF)
                    tb = tmp_b[qg % 2]

                    def emit_s(j):
                        ii0 = max(0, j - 4 * qg)
                        r = j % 5
                        ps, b_ps = ((PA, b_PA), (PA, b_PA), (PB, b_PB),
                                    (PB, b_PB), (PC, b_PC))[r]
                        c0 = 512 * (r % 2) if r < 4 else 0
                        blks = []
                        if 0 <= j - 4 * qg <= 3:
                            blks.append((j - 4 * qg, bdiag, b_bdiag[h]))
                        if 0 <= j - 4 * qg + 1 <= 3:
                            blks.append((j - 4 * qg + 1, bsub, b_bsub[h]))
                        add("tensor", (lambda j=j, ii0=ii0, c0=c0, ps=ps,
                                       rows=rows, qg=qg,
                                       st=(len(blks) == 0):
                                       lambda e: e.matmul(
                            ps[:, c0 + 128 * ii0:c0 + 512],
                            kpair[rows, 128 * j:128 * j + 128],
                            qpair[rows, 512 * qg + 128 * ii0:512 * qg + 512],
                            start=True, stop=st))(),
                            reads=[b_qpair, b_kpair], writes=[b_ps])
                        for bi, (ii, btile, bbuf) in enumerate(blks):
                            add("tensor", (lambda ii=ii, c0=c0, ps=ps, h=h,
                                           btile=btile,
                                           st=(bi == len(blks) - 1):
                                           lambda e: e.matmul(
                                ps[:, c0 + 128 * ii:c0 + 128 * ii + 128],
                                btile[:, 128 * h:128 * h + 128], ident[:, :],
                                start=False, stop=st))(),
                                reads=[bbuf, b_ident], writes=[b_ps])

                    def emit_exp(j0, ii0):
                        r = j0 % 5
                        ps, b_ps = ((PA, b_PA), (PA, b_PA), (PB, b_PB),
                                    (PB, b_PB), (PC, b_PC))[r]
                        c0 = 512 * (r % 2) if r < 4 else 0
                        add("scalar", (lambda j0=j0, ii0=ii0, c0=c0,
                                       ps=ps, h01=h01: lambda e: e.activation(
                            pgrpT[:, 512 * j0 + 128 * ii0:512 * j0 + 512],
                            ps[:, c0 + 128 * ii0:c0 + 512], AF.Exp,
                            bias=negm[:, h01:h01 + 1], scale=0.125))(),
                            reads=[b_ps, b_negmg], writes=[b_pgT[j0]])

                    def emit_AV(j):
                        ii0 = max(0, j - 4 * qg)
                        add("tensor", (lambda j=j, ii0=ii0, m=m, psV=psV,
                                       jmax=jmax: lambda e: e.matmul(
                            psV[0:65, 128 * ii0:512],
                            vg4[:, j, m, :],
                            pgrpT[:, 512 * j + 128 * ii0:512 * j + 512],
                            start=(j == 0), stop=(j == jmax)))(),
                            reads=[b_vgrp, b_pgT[j]], writes=[b_psV])

                    def tail():
                        add("scalar", (lambda psV=psV: lambda e: e.copy(
                            ysb[64:65, 0:512], psV[64:65, 0:512]))(),
                            reads=[b_psV], writes=[b_ysb[0]])
                        add("vector", lambda e: e.reciprocal(
                            ysb[64:65, 0:512], ysb[64:65, 0:512]),
                            reads=[b_ysb[0]], writes=[b_ysb[0]])
                        add("scalar", lambda e: e.copy(
                            lrecip[64:65, 0:512], ysb[64:65, 0:512]),
                            reads=[b_ysb[0]],
                            writes=[b_lrec, b_xn1t[1]])
                        add("tensor", lambda e: e.matmul(
                            PD[:, 0:512], diag4[64:65, 128:256],
                            lrecip[64:65, 0:512], start=True, stop=True),
                            reads=[b_diag4[1], b_lrec], writes=[b_PD])
                        add("scalar", lambda e: e.copy(
                            ysb[:, 512:1024], PD[:, 0:512]),
                            reads=[b_PD], writes=[b_ysb[1]])
                        if h01 == 0:
                            add("vector", (lambda qg=qg, psV=psV: lambda e:
                                           e.tensor_tensor(
                                attnT[0:64, 512 * qg:512 * qg + 512],
                                psV[0:64, 0:512], ysb[0:64, 512:1024],
                                ALU.mult))(),
                                reads=[b_psV, b_ysb[1]], writes=[b_attnT])
                        else:
                            add("vector", (lambda psV=psV, tb=tb: lambda e:
                                           e.tensor_tensor(
                                tb[0:64, 0:512], psV[0:64, 0:512],
                                ysb[0:64, 512:1024], ALU.mult))(),
                                reads=[b_psV, b_ysb[1]], writes=[b_xn1t[0]])
                            add("scalar", (lambda qg=qg, tb=tb: lambda e:
                                           e.dma_start(
                                out=attnT[64:128, 512 * qg:512 * qg + 512],
                                in_=tb[0:64, 0:512]))(),
                                reads=[b_xn1t[0]], writes=[b_attnT], dma=True)

                    for j in range(jmax + 1):
                        emit_s(j)
                        emit_exp(j, max(0, j - 4 * qg))
                        if j == 3 and pend_tail[0] is not None:
                            for f in pend_tail[0]:
                                f()
                            pend_tail[0] = None
                        if j >= 6:
                            emit_AV(j - 6)
                    if qg == 0:
                        for j in range(4):
                            emit_AV(j)
                        pend_tail[0] = [tail]
                    else:
                        for j in range(max(0, jmax - 5), jmax - 1):
                            emit_AV(j)
                        pend_tail[0] = [
                            (lambda j=jmax - 1: emit_AV(j)),
                            (lambda j=jmax: emit_AV(j)), tail]

                for h01 in range(2):
                    for qg in range(8):
                        emit_seg(h01, qg)
                for f in pend_tail[0]:
                    f()
                pend_tail[0] = None

                add("sync", (lambda p=p: lambda e: e.dma_start(
                    out=spillA_h.ap()[p], in_=attnT[:, :]))(),
                    reads=[b_attnT], writes=[b_spillA[p]], dma=True)
                if DEBUG_OUTPUTS:
                    add("sync", (lambda p=p: lambda e: e.dma_start(
                        out=dbg["attnT"].ap()[p], in_=attnT[:, :]))(),
                        reads=[b_attnT], dma=True)

        for p in range(NPAIR):
            add("gpsimd", (lambda p=p: lambda e: e.dma_start(
                out=wp_sb[:, C * p:C * p + C],
                in_=wp_h[128 * p:128 * p + 128, :]))(),
                writes=[b_wp, b_qpair, b_kpair], dma=True)

        aslab3 = aslab.rearrange("p (par pr x) -> p par pr x", par=2, pr=NPAIR)

        def emit_aslab_read(g, par):
            for p in range(NPAIR):
                add("sync", (lambda g=g, p=p, par=par: lambda e: e.dma_start(
                    out=aslab3[:, par, p, :],
                    in_=spillA_h.ap()[p][:, 512 * g:512 * g + 512]))(),
                    reads=[b_spillA[p]],
                    writes=[b_aslab[par], b_pgT[31],
                            b_attnT, b_wqg, b_wkg], dma=True)

        ffn_ws = [(wr_h, wrT, b_wrT, [b_vgrp]),
                  (wkf_h, wkT, b_wkTb, b_pgT[0:16] + [b_qabs, b_masks, b_vgrp]),
                  (wvf_h, wvT, b_wvTb, b_pgT[15:32] + [b_qabs])]

        def emit_ffn_wslab(s):
            wsrc, wdst, b_wdst, extra = ffn_ws[s // NCC]
            sl = s % NCC
            par = s % 2
            ps, b_ps = (PE_, b_PE) if par == 0 else (PF, b_PF)
            sb_ap = slab[:, C * par:C * par + C]
            add("gpsimd", (lambda sl=sl, wsrc=wsrc, sb_ap=sb_ap: lambda e:
                           e.dma_start(out=sb_ap,
                                       in_=wsrc[128 * sl:128 * sl + 128, :]))(),
                writes=[b_slab[par], b_wkg, b_wvg], dma=True)
            wdst3 = wdst.rearrange("p (cc x) -> p cc x", cc=NCC)
            for half in range(2):
                for cc in range(4 * half, 4 * half + 4):
                    add("tensor", (lambda cc=cc, half=half, sb_ap=sb_ap, ps=ps:
                                   lambda e: e.matmul(
                        ps[:, 128 * (cc - 4 * half):128 * (cc - 4 * half) + 128],
                        sb_ap[:, 128 * cc:128 * cc + 128], ident[:, :],
                        start=True, stop=True))(),
                        reads=[b_slab[par], b_ident], writes=[b_ps])
                add("vector", (lambda sl=sl, half=half, wdst3=wdst3, ps=ps:
                               lambda e: e.tensor_copy(
                    wdst3[:, 4 * half:4 * half + 4, 128 * sl:128 * sl + 128],
                    ps[:, 0:512].rearrange("p (cc x) -> p cc x", cc=4)))(),
                    reads=[b_ps], writes=[b_wdst] + extra)

        xn2T3 = xn2T.rearrange("p (cc t) -> p cc t", cc=NCC)
        add("vector", lambda e: e.memset(xn2T3[:, :, 0:1], 0.0),
            writes=[b_xn2T, b_xnT])

        emit_aslab_read(0, 0)
        _t3_pending = []

        for k in range(NT // 2):
            iA, iB = 2 * k, 2 * k + 1
            g = iA // 4
            if iA % 4 == 0 and g + 1 < 8:
                emit_aslab_read(g + 1, (g + 1) % 2)
            gpar = g % 2
            tiles = []
            for i, par, psY, b_psY in ((iA, 0, PA, b_PA), (iB, 1, PB, b_PB)):
                xin = xt[:, C * par:C * par + C]
                add("sync", (lambda i=i, xin=xin: lambda e: e.dma_start(
                    out=xin, in_=x_h[128 * i:128 * i + 128, :]))(),
                    writes=[b_xt[par]], dma=True)
                tiles.append((i, par, psY, b_psY, xin))
            for (i, par, psY, b_psY, xin) in tiles:
                for p in range(NPAIR):
                    for hf in range(2):
                        add("tensor", (lambda i=i, p=p, hf=hf, gpar=gpar,
                                       psY=psY: lambda e: e.matmul(
                            psY[:, 512 * hf:512 * hf + 512],
                            aslab3[:, gpar, p, 128 * (i % 4):128 * (i % 4) + 128],
                            wp_sb[:, C * p + 512 * hf:C * p + 512 * hf + 512],
                            start=(p == 0), stop=(p == NPAIR - 1)))(),
                            reads=[b_aslab[gpar], b_wp], writes=[b_psY])
            for (i, par, psY, b_psY, xin) in tiles:
                ysl = ysb[:, C * par:C * par + C]
                add("vector", (lambda psY=psY, ysl=ysl: lambda e:
                               e.tensor_tensor(ysl, psY[:, :], bp_b[:, :],
                                               ALU.add))(),
                    reads=[b_psY, b_bp], writes=[b_ysb[par]])
            ln_pair([(ysb[:, C * par:C * par + C], b_ysb[par],
                      y2[:, C * par:C * par + C], b_y2[par], par,
                      y2[:, C * par:C * par + C], b_y2[par])
                     for (i, par, psY, b_psY, xin) in tiles])
            for (i, par, psY, b_psY, xin) in tiles:
                y2l = y2[:, C * par:C * par + C]
                add("vector", (lambda xin=xin, y2l=y2l: lambda e:
                               e.tensor_tensor(xin, y2l, xin, ALU.add))(),
                    reads=[b_y2[par], b_xt[par]], writes=[b_xt[par]])
            for (i, par, psY, b_psY, xin) in tiles:
                add("gpsimd", (lambda i=i, xin=xin: lambda e: e.dma_start(
                    out=spillX_h[128 * i:128 * i + 128, :], in_=xin))(),
                    reads=[b_xt[par]], writes=[b_spillX[i]], dma=True)
                if DEBUG_OUTPUTS:
                    add("gpsimd", (lambda i=i, xin=xin: lambda e: e.dma_start(
                        out=dbg["xnew"][128 * i:128 * i + 128, :], in_=xin))(),
                        reads=[b_xt[par]], dma=True)
            while _t3_pending:
                _t3_pending.pop(0)()
            ln_pair([(xt[:, C * par:C * par + C], b_xt[par],
                      xn1t[:, C * par:C * par + C], b_xn1t[par], 2 + par,
                      y2[:, C * par:C * par + C], b_y2[par])
                     for (i, par, psY, b_psY, xin) in tiles])
            for (i, par, psY, b_psY, xin) in tiles:
                x2 = xn1t[:, C * par:C * par + C]

                def _t3(i=i, par=par, x2=x2):
                    psT, b_psT = (PC, b_PC) if par == 0 else (PD, b_PD)
                    for half in range(2):
                        for cc in range(4 * half, 4 * half + 4):
                            add("tensor", (lambda cc=cc, half=half, x2=x2,
                                           psT=psT: lambda e: e.matmul(
                                psT[:, 128 * (cc - 4 * half):
                                    128 * (cc - 4 * half) + 128],
                                x2[:, 128 * cc:128 * cc + 128], ident[:, :],
                                start=True, stop=True))(),
                                reads=[b_xn1t[par], b_ident], writes=[b_psT])
                        add("vector", (lambda i=i, half=half, psT=psT: lambda e:
                                       e.tensor_copy(
                            xn2T3[:, 4 * half:4 * half + 4,
                                  1 + 128 * i:1 + 128 * i + 128],
                            psT[:, 0:512].rearrange("p (cc x) -> p cc x",
                                                    cc=4)))(),
                            reads=[b_psT], writes=[b_xn2T])
                _t3_pending.append(_t3)
            if iA < 24:
                emit_ffn_wslab(iA)
            if iB < 24:
                emit_ffn_wslab(iB)
        for _fn in _t3_pending:
            _fn()
        _t3_pending = []

        xrT3 = xrT.rearrange("p (cc t) -> p cc t", cc=NCC)
        xkT3 = xkT.rearrange("p (cc t) -> p cc t", cc=NCC)
        kact3 = kact.rearrange("p (dd t) -> p dd t", dd=NCC)
        wrT3 = wrT.rearrange("p (cc x) -> p cc x", cc=NCC)
        wkT3 = wkT.rearrange("p (cc x) -> p cc x", cc=NCC)
        wvT3 = wvT.rearrange("p (dd x) -> p dd x", dd=NCC)

        def emit_mix(ch):
            t0 = 512 * ch
            for cc in range(NCC):
                prev = xn2T3[:, cc, t0:t0 + 512]
                cur = xn2T3[:, cc, t0 + 1:t0 + 513]
                add("vector", (lambda prev=prev, cur=cur: lambda e:
                               e.tensor_tensor(dscr[:, :], cur, prev,
                                               ALU.subtract))(),
                    reads=[b_xn2T], writes=[b_dscr, b_wvg])
                add("vector", (lambda cc=cc, prev=prev: lambda e:
                               e.scalar_tensor_tensor(
                    xrT3[:, cc, :], dscr[:, :], mu_sb[:, cc:cc + 1], prev,
                    ALU.mult, ALU.add))(),
                    reads=[b_dscr, b_mu, b_xn2T], writes=[b_xrT, b_wp])
                add("vector", (lambda cc=cc, prev=prev: lambda e:
                               e.scalar_tensor_tensor(
                    xkT3[:, cc, :], dscr[:, :], mu_sb[:, 8 + cc:8 + cc + 1], prev,
                    ALU.mult, ALU.add))(),
                    reads=[b_dscr, b_mu, b_xn2T], writes=[b_xkT, b_wp])

        def emit_rsig(ch):
            for sub in range(4):
                tpar = sub % 2
                psR, b_psR = (PA, b_PA) if tpar == 0 else (PB, b_PB)
                for hf in range(2):
                    for cc in range(NCC):
                        add("tensor", (lambda sub=sub, hf=hf, cc=cc, psR=psR:
                                       lambda e: e.matmul(
                            psR[:, 512 * hf:512 * hf + 512],
                            xrT3[:, cc, 128 * sub:128 * sub + 128],
                            wrT3[:, cc, 512 * hf:512 * hf + 512],
                            start=(cc == 0), stop=(cc == NCC - 1)))(),
                            reads=[b_xrT, b_wrT], writes=[b_psR])
                add("scalar", (lambda psR=psR: lambda e: e.activation(
                    sqd[:, :], psR[:, :], AF.Exp, bias=0.0, scale=-1.0))(),
                    reads=[b_psR], writes=[b_sqd])
                add("vector", lambda e: e.tensor_scalar(
                    sqd[:, :], sqd[:, :], 1.0, None, ALU.add),
                    reads=[b_sqd], writes=[b_sqd])
                add("vector", lambda e: e.reciprocal(sqd[:, :], sqd[:, :]),
                    reads=[b_sqd], writes=[b_sqd])
                add("scalar", (lambda sub=sub: lambda e: e.copy(
                    r_sb[:, C * sub:C * sub + C], sqd[:, :]))(),
                    reads=[b_sqd], writes=[b_rsb, b_aslab[0], b_aslab[1]])

        def emit_kpath(ch):
            for dd in range(NCC):
                ps, b_ps = (PC, b_PC) if dd % 2 == 0 else (PD, b_PD)
                for cc in range(NCC):
                    add("tensor", (lambda dd=dd, cc=cc, ps=ps: lambda e: e.matmul(
                        ps[:, 0:512], wkT3[:, cc, 128 * dd:128 * dd + 128],
                        xkT3[:, cc, :], start=(cc == 0), stop=(cc == NCC - 1)))(),
                        reads=[b_wkTb, b_xkT], writes=[b_ps])
                add("vector", (lambda ps=ps: lambda e: e.tensor_relu(
                    ps[:, 0:512], ps[:, 0:512]))(),
                    reads=[b_ps], writes=[b_ps])
                add("scalar", (lambda dd=dd, ps=ps: lambda e: e.activation(
                    kact3[:, dd, :], ps[:, 0:512], AF.Square, bias=0.0,
                    scale=1.0))(),
                    reads=[b_ps], writes=[b_kact, b_aslab[0], b_aslab[1]])

        def emit_vtail(ch):
            for sp in range(2):
                tl = []
                for sub in (2 * sp, 2 * sp + 1):
                    tpar = sub % 2
                    i = 4 * ch + sub
                    xin = xt[:, C * tpar:C * tpar + C]
                    add("gpsimd", (lambda i=i, xin=xin: lambda e: e.dma_start(
                        out=xin, in_=spillX_h[128 * i:128 * i + 128, :]))(),
                        reads=[b_spillX[i]], writes=[b_xt[tpar]], dma=True)
                    tl.append((sub, tpar, i, xin))
                for (sub, tpar, i, xin) in tl:
                    for hf in range(2):
                        psV, b_psV = (PE_, b_PE) if hf == 0 else (PF, b_PF)
                        for dd in range(NCC):
                            add("tensor", (lambda sub=sub, hf=hf, dd=dd, psV=psV:
                                           lambda e: e.matmul(
                                psV[:, 0:512],
                                kact3[:, dd, 128 * sub:128 * sub + 128],
                                wvT3[:, dd, 512 * hf:512 * hf + 512],
                                start=(dd == 0), stop=(dd == NCC - 1)))(),
                                reads=[b_kact, b_wvTb], writes=[b_psV])
                        add("vector", (lambda sub=sub, hf=hf, psV=psV, tpar=tpar:
                                       lambda e: e.tensor_tensor(
                            ysb[:, C * tpar + 512 * hf:C * tpar + 512 * hf + 512],
                            psV[:, 0:512],
                            r_sb[:, C * sub + 512 * hf:C * sub + 512 * hf + 512],
                            ALU.mult))(),
                            reads=[b_psV, b_rsb], writes=[b_ysb[tpar]])
                ln_pair([(ysb[:, C * tpar:C * tpar + C], b_ysb[tpar],
                          y2[:, C * tpar:C * tpar + C], b_y2[tpar], tpar,
                          y2[:, C * tpar:C * tpar + C], b_y2[tpar])
                         for (sub, tpar, i, xin) in tl])
                for (sub, tpar, i, xin) in tl:
                    y2l = y2[:, C * tpar:C * tpar + C]
                    add("vector", (lambda xin=xin, y2l=y2l: lambda e:
                                   e.tensor_tensor(
                        xin, y2l, xin, ALU.add))(),
                        reads=[b_y2[tpar], b_xt[tpar]], writes=[b_xt[tpar]])
                    add("sync", (lambda i=i, xin=xin: lambda e: e.dma_start(
                        out=out_h[128 * i:128 * i + 128, :], in_=xin))(),
                        reads=[b_xt[tpar]], writes=[b_out[i]], dma=True)

        emit_mix(0)
        for ch in range(8):
            emit_rsig(ch)
            emit_kpath(ch)
            if ch + 1 < 8:
                emit_mix(ch + 1)
            emit_vtail(ch)

        add("sync", lambda e: e.nop(), deps=[b_out[NT - 1].w])

        with nc.Block() as block:
            prog.emit(block, sems)

    return nc


_NC_CACHE = None


def kernel(L0_attn_x, L0_attn_Wq, L0_attn_Wk, L0_attn_Wv, L0_attn_Wp,
           L0_attn_bp, L0_attn_rel_table, L0_ffn_W_r, L0_ffn_W_k, L0_ffn_W_v,
           L0_ffn_mu_r, L0_ffn_mu_k):
    global _NC_CACHE
    cc = np.ascontiguousarray
    x = cc(np.asarray(L0_attn_x, dtype=np.float32))
    ins = {
        "wq": cc(np.asarray(L0_attn_Wq, dtype=np.float32)),
        "wk": cc(np.asarray(L0_attn_Wk, dtype=np.float32)),
        "wv": cc(np.asarray(L0_attn_Wv, dtype=np.float32)),
        "wp": cc(np.asarray(L0_attn_Wp, dtype=np.float32)),
        "bp": cc(np.asarray(L0_attn_bp, dtype=np.float32)),
        "rel": cc(np.asarray(L0_attn_rel_table, dtype=np.float32)).reshape(-1),
        "wr": cc(np.asarray(L0_ffn_W_r, dtype=np.float32)),
        "wkf": cc(np.asarray(L0_ffn_W_k, dtype=np.float32)),
        "wvf": cc(np.asarray(L0_ffn_W_v, dtype=np.float32)),
        "mur": cc(np.asarray(L0_ffn_mu_r, dtype=np.float32)),
        "muk": cc(np.asarray(L0_ffn_mu_k, dtype=np.float32)),
    }
    if _NC_CACHE is None:
        _NC_CACHE = build_program()
    nc = _NC_CACHE
    in_maps = [dict(ins, x=cc(x[b])) for b in range(8)]
    res = run_bass_kernel_spmd(nc, in_maps, core_ids=list(range(8)))
    return np.stack([res.results[b]["out"] for b in range(8)], axis=0).astype(np.float32)


# version 16
# speedup vs baseline: 1.0782x; 1.0782x over previous
from contextlib import ExitStack

import numpy as np

import concourse.bass as bass
import concourse.mybir as mybir
from concourse.bass_utils import run_bass_kernel_spmd

F32 = mybir.dt.float32
BF16 = mybir.dt.bfloat16
AF = mybir.ActivationFunctionType
ALU = mybir.AluOpType
AX = mybir.AxisListType

T = 4096
C = 1024
H = 16
NT = T // 128
NCC = C // 128
NPAIR = H // 2
EPS = 1e-5
NEG_BIG = -1.0e33

ENGINES = ("sync", "gpsimd", "scalar", "vector", "tensor")

DEBUG_OUTPUTS = False


class _Op:
    __slots__ = ("eng", "fn", "deps", "step", "idx", "val", "awaited")

    def __init__(self, eng, fn, deps, step, idx):
        self.eng = eng
        self.fn = fn
        self.deps = deps
        self.step = step
        self.idx = idx
        self.val = None
        self.awaited = False


class Buf:
    __slots__ = ("name", "w", "rs")

    def __init__(self, name):
        self.name = name
        self.w = None
        self.rs = []


class Prog:
    def __init__(self):
        self.ops = []
        self._last_dma = {}

    def add(self, eng, fn, reads=(), writes=(), deps=(), dma=False):
        d = []
        if dma:
            prev = self._last_dma.get(eng)
            if prev is not None:
                d.append(prev)
        for b in reads:
            if b.w is not None:
                d.append(b.w)
        for b in writes:
            d.extend(b.rs)
            if b.w is not None:
                d.append(b.w)
        d.extend(x for x in deps if x is not None)
        seen = set()
        dd = []
        for x in d:
            if id(x) not in seen:
                seen.add(id(x))
                dd.append(x)
        op = _Op(eng, fn, dd, 16 if dma else 1, len(self.ops))
        if dma:
            self._last_dma[eng] = op
        for x in dd:
            assert x.idx < op.idx
        for b in reads:
            b.rs.append(op)
        for b in writes:
            b.w = op
            b.rs = []
        self.ops.append(op)
        return op

    def emit(self, block, sems):
        for op in self.ops:
            if op.step == 16:
                op.awaited = True
            for dep in op.deps:
                if dep.eng != op.eng or dep.step == 16:
                    dep.awaited = True
        cnt = {e: 0 for e in ENGINES}
        for op in self.ops:
            if op.awaited:
                cnt[op.eng] += op.step
                op.val = cnt[op.eng]

        def make_runner(eng_name):
            def run(eng):
                waited = {}
                for op in self.ops:
                    if op.eng != eng_name:
                        continue
                    need = {}
                    for dep in op.deps:
                        if dep.eng == eng_name and dep.step != 16:
                            continue
                        if dep.val is None:
                            continue
                        s = dep.eng
                        if need.get(s, 0) < dep.val:
                            need[s] = dep.val
                    for s, v in need.items():
                        if waited.get(s, 0) < v:
                            eng.wait_ge(sems[s], v)
                            waited[s] = v
                    ins = op.fn(eng)
                    if op.awaited:
                        assert ins is not None, "awaited op returned None"
                        ins.then_inc(sems[op.eng], op.step)
            return run

        block.sync(make_runner("sync"))
        block.gpsimd(make_runner("gpsimd"))
        block.scalar(make_runner("scalar"))
        block.vector(make_runner("vector"))
        block.tensor(make_runner("tensor"))


def _ceil(a, b):
    return -(-a // b)


def build_program():
    nc = bass.Bass("TRN2", target_bir_lowering=False)

    x_h = nc.dram_tensor("x", [T, C], F32, kind="ExternalInput")
    wq_h = nc.dram_tensor("wq", [H, C, 64], F32, kind="ExternalInput")
    wk_h = nc.dram_tensor("wk", [H, C, 64], F32, kind="ExternalInput")
    wv_h = nc.dram_tensor("wv", [H, C, 64], F32, kind="ExternalInput")
    wp_h = nc.dram_tensor("wp", [C, C], F32, kind="ExternalInput")
    bp_h = nc.dram_tensor("bp", [C], F32, kind="ExternalInput")
    rel_h = nc.dram_tensor("rel", [33 * H], F32, kind="ExternalInput")
    wr_h = nc.dram_tensor("wr", [C, C], F32, kind="ExternalInput")
    wkf_h = nc.dram_tensor("wkf", [C, C], F32, kind="ExternalInput")
    wvf_h = nc.dram_tensor("wvf", [C, C], F32, kind="ExternalInput")
    mur_h = nc.dram_tensor("mur", [C], F32, kind="ExternalInput")
    muk_h = nc.dram_tensor("muk", [C], F32, kind="ExternalInput")

    spillA_h = nc.dram_tensor("spillA", [NPAIR, 128, T], BF16, kind="Internal")
    spillX_h = nc.dram_tensor("spillX", [T, C], F32, kind="Internal")
    out_h = nc.dram_tensor("out", [T, C], F32, kind="ExternalOutput")
    dbg = {}
    if DEBUG_OUTPUTS:
        dbg["xnT"] = nc.dram_tensor("dbg_xnT", [128, NCC * T], BF16, kind="ExternalOutput")
        dbg["attnT"] = nc.dram_tensor("dbg_attnT", [NPAIR, 128, T], BF16, kind="ExternalOutput")
        dbg["xnew"] = nc.dram_tensor("dbg_xnew", [T, C], F32, kind="ExternalOutput")
        dbg["st"] = nc.dram_tensor("dbg_st", [128, 64], F32, kind="ExternalOutput")
        dbg["x0"] = nc.dram_tensor("dbg_x0", [128, C], F32, kind="ExternalOutput")
        dbg["pg"] = nc.dram_tensor("dbg_pg", [128, 4 * T], BF16, kind="ExternalOutput")
        dbg["dg"] = nc.dram_tensor("dbg_dg", [128, 512], BF16, kind="ExternalOutput")
        dbg["nm"] = nc.dram_tensor("dbg_nm", [128, 64], F32, kind="ExternalOutput")
        dbg["bd"] = nc.dram_tensor("dbg_bd", [128, 256], BF16, kind="ExternalOutput")
        dbg["qa"] = nc.dram_tensor("dbg_qa", [128, T], BF16, kind="ExternalOutput")
        dbg["qp"] = nc.dram_tensor("dbg_qp", [128, T], BF16, kind="ExternalOutput")
        dbg["kp"] = nc.dram_tensor("dbg_kp", [128, T], BF16, kind="ExternalOutput")
        dbg["kr"] = nc.dram_tensor("dbg_kr", [128, 1], BF16, kind="ExternalOutput")

    prog = Prog()
    add = prog.add

    with ExitStack() as st:
        def sb(name, shape, dt):
            return st.enter_context(nc.sbuf_tensor(name, shape, dt))

        ident = sb("ident", [128, 128], BF16)
        ones_f = sb("ones_f", [128, 128], F32)
        zeros_f = sb("zeros_f", [128, 128], F32)
        causal_f = sb("causal_f", [128, 128], F32)
        rel8 = sb("rel8", [128, 528], F32)
        bdiag = sb("bdiag", [128, 16 * 128], BF16)
        bsub = sb("bsub", [128, 16 * 128], BF16)
        diag4 = sb("diag4", [128, 4 * 128], BF16)
        bp_b = sb("bp_b", [128, C], F32)
        mu_sb = sb("mu_sb", [128, 32], F32)
        negb16 = sb("negb16", [128, 16], F32)
        stats = sb("stats", [128, 64], F32)
        negm = sb("negm", [128, 2 * NT], F32)
        lparts = sb("lparts", [128, 24], F32)
        gtmp = sb("gtmp", [128, 2], F32)
        krb = sb("krb", [128, 1], BF16)
        epsc = sb("epsc", [128, 1], F32)
        gdiff = sb("gdiff", [128, 272], F32)
        gdiff2 = sb("gdiff2", [128, 272], F32)

        xt = sb("xt", [128, 2 * C], F32)
        ysb = sb("ysb", [128, 2 * C], F32)
        y2 = sb("y2", [128, 2 * C], F32)
        sqd = sb("sqd", [128, C], F32)

        U1 = sb("U1", [128, 32832], BF16)
        U2 = sb("U2", [128, 46592], BF16)

        def u1(off, n):
            return U1[:, off:off + n]

        def u2(off, n):
            return U2[:, off:off + n]

        qpair = u2(0, T)
        kpair = u2(T, T)
        vgrp = u2(2 * T, 2 * T + 128)
        pgrpT = u2(16512, 16384)
        masks = u2(16512, 33 * 128)
        qabs = u2(20608, T)
        attnT = u2(32896, T)
        wq_g = u2(36992, 2048)
        wk_g = u2(39040, 2048)
        wv_g = u2(41088, 2048)
        xn1t = u2(43136, 2048)
        tmp_b = [u2(43136, 512), u2(43648, 512)]
        lrecip = u2(44160, 512)
        wp_sb = u2(0, 8192)
        wrT = u2(8192, 8192)
        wkT = u2(16384, 8192)
        wvT = u2(24576, 8192)
        aslab = u2(32768, 8192)
        slab = u2(40960, 2048)
        dscr = u2(45696, 512)
        xrT = u2(0, T)
        xkT = u2(T, T)
        kact = u2(32768, T)
        r_sb = u2(36864, T)

        xnT = u1(0, NCC * T)
        xn2T = u1(0, NCC * (T + 8))

        PA = st.enter_context(nc.psum_tensor("PA", [128, 1024], F32))
        PB = st.enter_context(nc.psum_tensor("PB", [128, 1024], F32))
        PC = st.enter_context(nc.psum_tensor("PC", [128, 512], F32))
        PD = st.enter_context(nc.psum_tensor("PD", [128, 512], F32))
        PE_ = st.enter_context(nc.psum_tensor("PE", [128, 512], F32))
        PF = st.enter_context(nc.psum_tensor("PF", [128, 512], F32))

        sems = {e: st.enter_context(nc.semaphore("sem_" + e)) for e in ENGINES}

        b_ident = Buf("ident"); b_ones = Buf("ones"); b_zeros = Buf("zeros")
        b_causal = Buf("causal"); b_rel8 = Buf("rel8")
        b_bdiag = [Buf(f"bd{h}") for h in range(H)]
        b_bsub = [Buf(f"bs{h}") for h in range(H)]
        b_masks = Buf("masks"); b_bp = Buf("bp"); b_mu = Buf("mu")
        b_negb = Buf("negb")
        b_stats = [Buf(f"st{i}") for i in range(4)]
        b_negm = [Buf("negm0"), Buf("negm1")]
        b_lparts = Buf("lparts"); b_gtmp = Buf("gtmp")
        b_diag4 = [Buf(f"dg{i}") for i in range(4)]
        b_xnT = Buf("xnT"); b_qabs = Buf("qabs")
        b_pgT = [Buf(f"pgT{j}") for j in range(NT)]
        b_negmg = Buf("negmg"); b_linv = Buf("linv"); b_lrec = Buf("lrec")
        b_qpair = Buf("qpair"); b_kpair = Buf("kpair"); b_vgrp = Buf("vgrp")
        b_attnT = Buf("attnT")
        b_wqg = Buf("wqg"); b_wkg = Buf("wkg"); b_wvg = Buf("wvg")
        b_xn1t = [Buf("xn1t0"), Buf("xn1t1")]
        b_xn2T = Buf("xn2T"); b_wp = Buf("wp")
        b_wrT = Buf("wrT"); b_wkTb = Buf("wkT"); b_wvTb = Buf("wvT")
        b_slab = [Buf("slab0"), Buf("slab1")]
        b_aslab = [Buf("aslab0"), Buf("aslab1")]
        b_xrT = Buf("xrT"); b_xkT = Buf("xkT"); b_kact = Buf("kact")
        b_rsb = Buf("rsb"); b_dscr = Buf("dscr")
        b_xt = [Buf("xt0"), Buf("xt1")]
        b_ysb = [Buf("ysb0"), Buf("ysb1")]
        b_y2 = [Buf("y20"), Buf("y21")]; b_sqd = Buf("sqd")
        b_krmax = Buf("krmax"); b_eps = Buf("eps")
        b_gdiff = Buf("gdiff"); b_gdiff2 = Buf("gdiff2")
        b_PA = Buf("PA"); b_PB = Buf("PB"); b_PC = Buf("PC")
        b_PD = Buf("PD"); b_PE = Buf("PE"); b_PF = Buf("PF")
        b_spillA = [Buf(f"spA{p}") for p in range(NPAIR)]
        b_spillX = [Buf(f"spX{i}") for i in range(NT)]
        b_out = [Buf(f"out{i}") for i in range(NT)]

        add("vector", lambda e: e.memset(ones_f[:, :], 1.0), writes=[b_ones])
        add("vector", lambda e: e.memset(epsc[:, :], EPS), writes=[b_eps])
        add("vector", lambda e: e.memset(zeros_f[:, :], 0.0), writes=[b_zeros])
        add("vector", lambda e: e.memset(diag4[:, 0:128], 1.0), writes=[b_diag4[0]])
        add("gpsimd", lambda e: e.affine_select(
            ident[:, :], diag4[:, 0:128], pattern=[[1, 128]], base=0,
            channel_multiplier=-1, compare_op=ALU.is_equal, fill=0.0),
            reads=[b_diag4[0]], writes=[b_ident])
        add("vector", lambda e: e.memset(diag4[:, 128:256], 1.0),
            writes=[b_diag4[1]])
        add("vector", lambda e: e.memset(ysb[:, 0:128], NEG_BIG),
            writes=[b_ysb[0]])
        add("gpsimd", lambda e: e.affine_select(
            causal_f[:, :], ysb[:, 0:128], pattern=[[1, 128]], base=-1,
            channel_multiplier=-1, compare_op=ALU.is_ge, fill=0.0),
            reads=[b_ysb[0]], writes=[b_causal])

        def _mask_op(slot, base):
            def f(e):
                return e.affine_select(
                    masks[:, 128 * slot:128 * slot + 128], diag4[:, 0:128],
                    pattern=[[1, 128]], base=base, channel_multiplier=-1,
                    compare_op=ALU.is_equal, fill=0.0)
            return f
        for d in range(17):
            add("gpsimd", _mask_op(d, d), reads=[b_diag4[0]],
                writes=[b_masks] + b_pgT[0:9])
        for d in range(1, 17):
            add("gpsimd", _mask_op(17 + d - 1, d - 128), reads=[b_diag4[0]],
                writes=[b_masks] + b_pgT[0:9])

        add("sync", lambda e: e.dma_start(
            out=rel8[:, :], in_=rel_h.ap().rearrange("(one c) -> one c", one=1)
            .partition_broadcast(128)), writes=[b_rel8], dma=True)
        add("sync", lambda e: e.dma_start(
            out=bp_b[:, :], in_=bp_h.ap().rearrange("(one c) -> one c", one=1)
            .partition_broadcast(128)), writes=[b_bp], dma=True)
        add("gpsimd", lambda e: e.dma_start(
            out=mu_sb[:, 0:8], in_=mur_h.ap().rearrange("(cc p) -> p cc", p=128),
            allow_slow_non_contiguous=True), writes=[b_mu], dma=True)
        add("gpsimd", lambda e: e.dma_start(
            out=mu_sb[:, 8:16], in_=muk_h.ap().rearrange("(cc p) -> p cc", p=128),
            allow_slow_non_contiguous=True), writes=[b_mu], dma=True)
        add("vector", lambda e: e.tensor_tensor(
            mu_sb[:, 16:32], ones_f[:, 0:16], mu_sb[:, 0:16], ALU.subtract),
            reads=[b_mu, b_ones], writes=[b_mu])

        add("vector", lambda e: e.tensor_scalar(
            rel8[:, :], rel8[:, :], 8.0, None, ALU.mult),
            reads=[b_rel8], writes=[b_rel8])
        add("vector", lambda e: e.reduce_max(
            stats[:, 63:64], rel8[:, :], axis=AX.X, apply_absolute_value=True),
            reads=[b_rel8], writes=[b_stats[0]])
        add("scalar", lambda e: e.mul(
            stats[:, 62:63], stats[:, 63:64], 0.25),
            reads=[b_stats[0]], writes=[b_stats[0]])
        add("vector", lambda e: e.tensor_scalar(
            negb16[:, :], ones_f[:, 0:16], stats[:, 62:63], None, ALU.mult),
            reads=[b_ones, b_stats[0]], writes=[b_negb])

        for d in range(17):
            add("vector", (lambda d=d: lambda e: e.tensor_tensor(
                gdiff[:, 16 * d:16 * d + 16], rel8[:, (16 - d) * H:(16 - d) * H + 16],
                rel8[:, 0:16], ALU.subtract))(),
                reads=[b_rel8], writes=[b_gdiff])
        add("scalar", lambda e: e.copy(gdiff2[:, :], gdiff[:, :]),
            reads=[b_gdiff], writes=[b_gdiff2])
        for h in range(H):
            add("vector", (lambda h=h: lambda e: e.tensor_copy(
                bdiag[:, 128 * h:128 * h + 128], causal_f[:, :]))(),
                reads=[b_causal], writes=[b_bdiag[h]])
            add("vector", (lambda h=h: lambda e: e.memset(
                bsub[:, 128 * h:128 * h + 128], 0.0))(),
                writes=[b_bsub[h]])
            for d in range(17):
                add("vector", (lambda h=h, d=d: lambda e: e.scalar_tensor_tensor(
                    bdiag[:, 128 * h:128 * h + 128],
                    masks[:, 128 * d:128 * d + 128], gdiff2[:, 16 * d + h:16 * d + h + 1],
                    bdiag[:, 128 * h:128 * h + 128], ALU.mult, ALU.add))(),
                    reads=[b_masks, b_gdiff2, b_bdiag[h]] + b_pgT[0:9],
                    writes=[b_bdiag[h]])
                if d >= 1:
                    add("vector", (lambda h=h, d=d: lambda e: e.scalar_tensor_tensor(
                        bsub[:, 128 * h:128 * h + 128],
                        masks[:, 128 * (17 + d - 1):128 * (17 + d - 1) + 128],
                        gdiff2[:, 16 * d + h:16 * d + h + 1],
                        bsub[:, 128 * h:128 * h + 128], ALU.mult, ALU.add))(),
                        reads=[b_masks, b_gdiff2, b_bsub[h]] + b_pgT[0:9],
                        writes=[b_bsub[h]])

        def ln_stats_apply(xin_ap, b_xin, out_ap, b_out_buf, sidx,
                           dump_ap=None, b_dump=None):
            s0 = 12 * sidx
            add("vector", lambda e: e.reduce_sum(
                stats[:, s0:s0 + 1], xin_ap, axis=AX.X),
                reads=[b_xin], writes=[b_stats[sidx]])
            add("vector", lambda e: e.scalar_tensor_tensor(
                dump_ap, xin_ap, 1.0, xin_ap, ALU.mult, ALU.mult,
                accum_out=stats[:, s0 + 1:s0 + 2]),
                reads=[b_xin], writes=[b_dump, b_stats[sidx]])
            add("scalar", lambda e: e.mul(
                stats[:, s0 + 2:s0 + 3], stats[:, s0:s0 + 1], 1.0 / C),
                reads=[b_stats[sidx]], writes=[b_stats[sidx]])
            add("scalar", lambda e: e.mul(
                stats[:, s0 + 3:s0 + 4], stats[:, s0:s0 + 1], 1.0 / C),
                reads=[b_stats[sidx]], writes=[b_stats[sidx]])
            add("vector", lambda e: e.tensor_scalar(
                stats[:, s0 + 4:s0 + 5], stats[:, s0 + 2:s0 + 3],
                stats[:, s0 + 3:s0 + 4], None, ALU.mult),
                reads=[b_stats[sidx]], writes=[b_stats[sidx]])
            add("gpsimd", lambda e: e.tensor_copy(
                stats[:, s0 + 5:s0 + 6], stats[:, s0 + 4:s0 + 5]),
                reads=[b_stats[sidx]], writes=[b_stats[sidx]])
            add("vector", lambda e: e.scalar_tensor_tensor(
                stats[:, s0 + 6:s0 + 7], stats[:, s0 + 1:s0 + 2], 1.0 / C,
                stats[:, s0 + 5:s0 + 6], ALU.mult, ALU.subtract),
                reads=[b_stats[sidx]], writes=[b_stats[sidx]])
            add("scalar", lambda e: e.activation(
                stats[:, s0 + 7:s0 + 8], stats[:, s0 + 6:s0 + 7], AF.Ln,
                bias=epsc[:, :], scale=1.0),
                reads=[b_stats[sidx], b_eps], writes=[b_stats[sidx]])
            add("vector", lambda e: e.tensor_copy(
                stats[:, s0 + 8:s0 + 9], stats[:, s0 + 7:s0 + 8]),
                reads=[b_stats[sidx]], writes=[b_stats[sidx]])
            add("scalar", lambda e: e.activation(
                stats[:, s0 + 9:s0 + 10], stats[:, s0 + 8:s0 + 9], AF.Exp,
                bias=0.0, scale=-0.5),
                reads=[b_stats[sidx]], writes=[b_stats[sidx]])
            add("vector", lambda e: e.tensor_scalar(
                out_ap, xin_ap, stats[:, s0 + 2:s0 + 3], stats[:, s0 + 9:s0 + 10],
                ALU.subtract, ALU.mult),
                reads=[b_xin, b_stats[sidx]], writes=[b_out_buf])

        def ln_pair(specs):
            for (xin_ap, b_xin, _o, _bo, sidx, dump_ap, b_dump) in specs:
                s0 = 12 * sidx
                add("vector", (lambda xin_ap=xin_ap, s0=s0: lambda e:
                               e.reduce_sum(stats[:, s0:s0 + 1], xin_ap,
                                            axis=AX.X))(),
                    reads=[b_xin], writes=[b_stats[sidx]])
            for (xin_ap, b_xin, _o, _bo, sidx, dump_ap, b_dump) in specs:
                s0 = 12 * sidx
                add("vector", (lambda xin_ap=xin_ap, s0=s0, dump_ap=dump_ap:
                               lambda e: e.scalar_tensor_tensor(
                    dump_ap, xin_ap, 1.0, xin_ap, ALU.mult, ALU.mult,
                    accum_out=stats[:, s0 + 1:s0 + 2]))(),
                    reads=[b_xin], writes=[b_dump, b_stats[sidx]])
            for (_x, _bx, _o, _bo, sidx, _d, _bd) in specs:
                s0 = 12 * sidx
                add("vector", (lambda s0=s0: lambda e: e.tensor_scalar(
                    stats[:, s0 + 2:s0 + 3], stats[:, s0:s0 + 1], 1.0 / C,
                    None, ALU.mult))(),
                    reads=[b_stats[sidx]], writes=[b_stats[sidx]])
            for (_x, _bx, _o, _bo, sidx, _d, _bd) in specs:
                s0 = 12 * sidx
                add("vector", (lambda s0=s0: lambda e: e.scalar_tensor_tensor(
                    stats[:, s0 + 3:s0 + 4], stats[:, s0:s0 + 1], 1.0 / C,
                    stats[:, s0 + 2:s0 + 3], ALU.mult, ALU.mult))(),
                    reads=[b_stats[sidx]], writes=[b_stats[sidx]])
            for (_x, _bx, _o, _bo, sidx, _d, _bd) in specs:
                s0 = 12 * sidx
                add("gpsimd", (lambda s0=s0: lambda e: e.tensor_copy(
                    stats[:, s0 + 7:s0 + 8], stats[:, s0 + 1:s0 + 2]))(),
                    reads=[b_stats[sidx]], writes=[b_stats[sidx]])
            for (_x, _bx, _o, _bo, sidx, _d, _bd) in specs:
                s0 = 12 * sidx
                add("vector", (lambda s0=s0: lambda e: e.scalar_tensor_tensor(
                    stats[:, s0 + 4:s0 + 5], stats[:, s0 + 7:s0 + 8], 1.0 / C,
                    stats[:, s0 + 3:s0 + 4], ALU.mult, ALU.subtract))(),
                    reads=[b_stats[sidx]], writes=[b_stats[sidx]])
            for (_x, _bx, _o, _bo, sidx, _d, _bd) in specs:
                s0 = 12 * sidx
                add("scalar", (lambda s0=s0: lambda e: e.activation(
                    stats[:, s0 + 5:s0 + 6], stats[:, s0 + 4:s0 + 5], AF.Ln,
                    bias=epsc[:, :], scale=1.0))(),
                    reads=[b_stats[sidx], b_eps], writes=[b_stats[sidx]])
            for (_x, _bx, _o, _bo, sidx, _d, _bd) in specs:
                s0 = 12 * sidx
                add("scalar", (lambda s0=s0: lambda e: e.activation(
                    stats[:, s0 + 6:s0 + 7], stats[:, s0 + 5:s0 + 6], AF.Exp,
                    bias=0.0, scale=-0.5))(),
                    reads=[b_stats[sidx]], writes=[b_stats[sidx]])
            for (xin_ap, b_xin, out_ap, b_out_buf, sidx, _d, _bd) in specs:
                s0 = 12 * sidx
                add("vector", (lambda xin_ap=xin_ap, out_ap=out_ap, s0=s0:
                               lambda e: e.tensor_scalar(
                    out_ap, xin_ap, stats[:, s0 + 2:s0 + 3],
                    stats[:, s0 + 6:s0 + 7], ALU.subtract, ALU.mult))(),
                    reads=[b_xin, b_stats[sidx]], writes=[b_out_buf])


        xnT3 = xnT.rearrange("p (cc t) -> p cc t", cc=NCC)
        PSP = {0: (PA, b_PA), 1: (PB, b_PB)}

        for k2 in range(NT // 2):
            iA, iB = 2 * k2, 2 * k2 + 1
            tl = []
            for i, par in ((iA, 0), (iB, 1)):
                ps, b_ps = PSP[par]
                xin = xt[:, C * par:C * par + C]
                add("sync", (lambda i=i, xin=xin: lambda e: e.dma_start(
                    out=xin, in_=x_h[128 * i:128 * i + 128, :]))(),
                    writes=[b_xt[par]], dma=True)
                tl.append((i, par, ps, b_ps, xin))
            ln_pair([(xin, b_xt[par], xn1t[:, C * par:C * par + C],
                      b_xn1t[par], par, y2[:, C * par:C * par + C], b_y2[par])
                     for (i, par, ps, b_ps, xin) in tl])
            for (i, par, ps, b_ps, xin) in tl:
                x1 = xn1t[:, C * par:C * par + C]
                for cc in range(NCC):
                    add("tensor", (lambda cc=cc, x1=x1, ps=ps: lambda e: e.matmul(
                        ps[:, 128 * cc:128 * cc + 128],
                        x1[:, 128 * cc:128 * cc + 128], ident[:, :],
                        start=True, stop=True))(),
                        reads=[b_xn1t[par], b_ident], writes=[b_ps])
                add("vector", (lambda i=i, ps=ps: lambda e: e.tensor_copy(
                    xnT3[:, :, 128 * i:128 * i + 128],
                    ps[:, :].rearrange("p (cc t) -> p cc t", cc=NCC)))(),
                    reads=[b_ps], writes=[b_xnT])

        if DEBUG_OUTPUTS:
            add("sync", lambda e: e.dma_start(out=dbg["xnT"].ap(), in_=xnT),
                reads=[b_xnT], dma=True)

        wq3 = wq_g.rearrange("p (pp cc x) -> p pp cc x", pp=2, cc=NCC)
        wk3 = wk_g.rearrange("p (pp cc x) -> p pp cc x", pp=2, cc=NCC)
        wv3 = wv_g.rearrange("p (cc x) -> p cc x", cc=NCC)
        vg4 = vgrp.rearrange("p (t m e) -> p t m e", m=4, e=65)
        add("vector", lambda e: e.memset(vg4[:, :, :, 64:65], 1.0),
            writes=[b_vgrp])

        def wload(dst3_sel, src_h, head):
            def f(e):
                return e.dma_start(
                    out=dst3_sel,
                    in_=src_h.ap()[head].rearrange("(cc p) d -> p cc d", p=128))
            return f

        for g in range(4):
            pairs = (2 * g, 2 * g + 1)
            for pp, p in enumerate(pairs):
                for h01 in range(2):
                    h = 2 * p + h01
                    add("gpsimd", wload(wq3[:, pp, :, 64 * h01:64 * h01 + 64], wq_h, h),
                        writes=[b_wqg], dma=True)
                    add("gpsimd", wload(wk3[:, pp, :, 64 * h01:64 * h01 + 64], wk_h, h),
                        writes=[b_wkg], dma=True)
            for m in range(4):
                add("gpsimd", wload(wv3[:, :, 64 * m:64 * m + 64], wv_h, 4 * g + m),
                    writes=[b_wvg], dma=True)

            for ti in range(NT):
                ps, b_ps = (PC, b_PC) if ti % 2 == 0 else (PD, b_PD)
                for cc in range(NCC):
                    add("tensor", (lambda ti=ti, cc=cc, ps=ps: lambda e: e.matmul(
                        ps[:, 0:256],
                        xnT[:, T * cc + 128 * ti: T * cc + 128 * ti + 128],
                        wv_g[:, 256 * cc:256 * cc + 256],
                        start=(cc == 0), stop=(cc == NCC - 1)))(),
                        reads=[b_xnT, b_wvg], writes=[b_ps])
                add("vector", (lambda ti=ti, ps=ps: lambda e: e.tensor_copy(
                    vg4[:, ti, :, 0:64],
                    ps[:, 0:256].rearrange("p (m e) -> p m e", m=4)))(),
                    reads=[b_ps], writes=[b_vgrp])

            for pp, p in enumerate(pairs):
                for which, dst, b_dst, wsb3, b_w in (
                        (0, qpair, b_qpair, wq3, b_wqg),
                        (1, kpair, b_kpair, wk3, b_wkg)):
                    for tc in range(8):
                        ps, b_ps = (PA, b_PA) if tc % 2 == 0 else (PB, b_PB)
                        for cc in range(NCC):
                            add("tensor", (lambda pp=pp, tc=tc, cc=cc, ps=ps,
                                           wsb3=wsb3: lambda e: e.matmul(
                                ps[:, 0:512],
                                wsb3[:, pp, cc, :],
                                xnT[:, T * cc + 512 * tc: T * cc + 512 * tc + 512],
                                start=(cc == 0), stop=(cc == NCC - 1)))(),
                                reads=[b_xnT, b_w], writes=[b_ps])
                        add("vector", (lambda tc=tc, ps=ps, dst=dst: lambda e:
                            e.tensor_copy(dst[:, 512 * tc:512 * tc + 512],
                                          ps[:, 0:512]))(),
                            reads=[b_ps], writes=[b_dst])

                add("vector", lambda e: e.tensor_scalar(
                    qabs.bitcast(mybir.dt.uint16), qpair.bitcast(mybir.dt.uint16),
                    0x7FFF, None, ALU.bitwise_and),
                    reads=[b_qpair], writes=[b_qabs] + b_pgT[8:16])
                add("vector", lambda e: e.reduce_max(
                    stats[:, 61:62], kpair[:, :], axis=AX.X,
                    apply_absolute_value=True),
                    reads=[b_kpair], writes=[b_krmax])
                add("scalar", lambda e: e.copy(krb[:, :], stats[:, 61:62]),
                    reads=[b_krmax], writes=[b_krmax])
                if DEBUG_OUTPUTS and p == 0:
                    add("gpsimd", lambda e: e.dma_start(out=dbg["qa"].ap(), in_=qabs),
                        reads=[b_qabs], dma=True)
                    add("gpsimd", lambda e: e.dma_start(out=dbg["qp"].ap(), in_=qpair),
                        reads=[b_qpair], dma=True)
                    add("gpsimd", lambda e: e.dma_start(out=dbg["kp"].ap(), in_=kpair),
                        reads=[b_kpair], dma=True)
                    add("gpsimd", lambda e: e.dma_start(out=dbg["kr"].ap(), in_=krb[:, :]),
                        reads=[b_krmax], dma=True)
                for h01 in range(2):
                    rows = slice(64 * h01, 64 * h01 + 64)
                    for qg in range(8):
                        seg = 8 * h01 + qg
                        psM, b_psM = (PC, b_PC) if seg % 2 == 0 else (PD, b_PD)
                        add("tensor", (lambda rows=rows, qg=qg, psM=psM:
                                       lambda e: e.matmul(
                            psM[0:1, 0:512], krb[rows, 0:1],
                            qabs[rows, 512 * qg:512 * qg + 512],
                            start=True, stop=True))(),
                            reads=[b_krmax, b_qabs], writes=[b_psM])
                        add("vector", (lambda seg=seg, psM=psM: lambda e:
                                       e.reduce_max(
                            lparts[0:1, seg:seg + 1], psM[0:1, 0:512],
                            axis=AX.X))(),
                            reads=[b_psM], writes=[b_lparts])
                for h01 in range(2):
                    add("vector", (lambda h01=h01: lambda e: e.reduce_max(
                        lparts[0:1, 16 + h01:17 + h01],
                        lparts[0:1, 8 * h01:8 * h01 + 8], axis=AX.X))(),
                        reads=[b_lparts], writes=[b_lparts])
                for h01 in range(2):
                    add("scalar", (lambda h01=h01: lambda e: e.copy(
                        diag4[0:1, 256 + h01:257 + h01],
                        lparts[0:1, 16 + h01:17 + h01]))(),
                        reads=[b_lparts], writes=[b_diag4[2]])
                    psM, b_psM = (PC, b_PC) if h01 == 0 else (PD, b_PD)
                    add("tensor", (lambda h01=h01, psM=psM: lambda e: e.matmul(
                        psM[:, 0:1], diag4[0:1, 128:256],
                        diag4[0:1, 256 + h01:257 + h01],
                        start=True, stop=True))(),
                        reads=[b_diag4[1], b_diag4[2]], writes=[b_psM])
                    add("vector", (lambda h01=h01, psM=psM: lambda e:
                                   e.scalar_tensor_tensor(
                        negm[:, 16 + h01:17 + h01], psM[:, 0:1], 0.125,
                        negb16[:, 0:1], ALU.mult, ALU.add))(),
                        reads=[b_psM, b_negb], writes=[b_negmg])
                for h01 in range(2):
                    add("vector", (lambda h01=h01: lambda e: e.tensor_tensor(
                        negm[:, h01:h01 + 1], zeros_f[:, 0:1],
                        negm[:, 16 + h01:17 + h01], ALU.subtract))(),
                        reads=[b_zeros, b_negmg], writes=[b_negmg])

                pend_tail = [None, None]

                def emit_seg(h01, qg, pp=pp, p=p):
                    h = 2 * p + h01
                    rows = slice(64 * h01, 64 * h01 + 64)
                    m = 2 * pp + h01
                    seg = 8 * h01 + qg
                    jmax = 4 * qg + 3
                    psV, b_psV = (PE_, b_PE) if seg % 2 == 0 else (PF, b_PF)
                    tb = tmp_b[qg % 2]

                    def emit_s(j):
                        ii0 = max(0, j - 4 * qg)
                        r = j % 5
                        ps, b_ps = ((PA, b_PA), (PA, b_PA), (PB, b_PB),
                                    (PB, b_PB), (PC, b_PC))[r]
                        c0 = 512 * (r % 2) if r < 4 else 0
                        blks = []
                        if 0 <= j - 4 * qg <= 3:
                            blks.append((j - 4 * qg, bdiag, b_bdiag[h]))
                        if 0 <= j - 4 * qg + 1 <= 3:
                            blks.append((j - 4 * qg + 1, bsub, b_bsub[h]))
                        add("tensor", (lambda j=j, ii0=ii0, c0=c0, ps=ps,
                                       rows=rows, qg=qg,
                                       st=(len(blks) == 0):
                                       lambda e: e.matmul(
                            ps[:, c0 + 128 * ii0:c0 + 512],
                            kpair[rows, 128 * j:128 * j + 128],
                            qpair[rows, 512 * qg + 128 * ii0:512 * qg + 512],
                            start=True, stop=st))(),
                            reads=[b_qpair, b_kpair], writes=[b_ps])
                        for bi, (ii, btile, bbuf) in enumerate(blks):
                            add("tensor", (lambda ii=ii, c0=c0, ps=ps, h=h,
                                           btile=btile,
                                           st=(bi == len(blks) - 1):
                                           lambda e: e.matmul(
                                ps[:, c0 + 128 * ii:c0 + 128 * ii + 128],
                                btile[:, 128 * h:128 * h + 128], ident[:, :],
                                start=False, stop=st))(),
                                reads=[bbuf, b_ident], writes=[b_ps])

                    def emit_exp(j0, ii0):
                        r = j0 % 5
                        ps, b_ps = ((PA, b_PA), (PA, b_PA), (PB, b_PB),
                                    (PB, b_PB), (PC, b_PC))[r]
                        c0 = 512 * (r % 2) if r < 4 else 0
                        add("scalar", (lambda j0=j0, ii0=ii0, c0=c0,
                                       ps=ps, h01=h01: lambda e: e.activation(
                            pgrpT[:, 512 * j0 + 128 * ii0:512 * j0 + 512],
                            ps[:, c0 + 128 * ii0:c0 + 512], AF.Exp,
                            bias=negm[:, h01:h01 + 1], scale=0.125))(),
                            reads=[b_ps, b_negmg], writes=[b_pgT[j0]])

                    def emit_AV(j):
                        ii0 = max(0, j - 4 * qg)
                        add("tensor", (lambda j=j, ii0=ii0, m=m, psV=psV,
                                       jmax=jmax: lambda e: e.matmul(
                            psV[0:65, 128 * ii0:512],
                            vg4[:, j, m, :],
                            pgrpT[:, 512 * j + 128 * ii0:512 * j + 512],
                            start=(j == 0), stop=(j == jmax)))(),
                            reads=[b_vgrp, b_pgT[j]], writes=[b_psV])

                    def tail_a():
                        add("vector", (lambda psV=psV: lambda e: e.tensor_copy(
                            ysb[64:65, 0:512], psV[64:65, 0:512]))(),
                            reads=[b_psV], writes=[b_ysb[0]])
                        add("vector", lambda e: e.tensor_copy(
                            lparts[0:1, 20:21], lparts[0:1, 18:19]),
                            reads=[b_lparts], writes=[b_lparts])
                        add("vector", lambda e: e.reciprocal(
                            ysb[64:65, 0:512], ysb[64:65, 0:512]),
                            reads=[b_ysb[0]], writes=[b_ysb[0]])
                        add("vector", lambda e: e.tensor_copy(
                            lparts[0:1, 21:22], lparts[0:1, 18:19]),
                            reads=[b_lparts], writes=[b_lparts])
                        add("vector", lambda e: e.tensor_copy(
                            lrecip[64:65, 0:512], ysb[64:65, 0:512]),
                            reads=[b_ysb[0]],
                            writes=[b_lrec, b_xn1t[1]])

                    def tail_b():
                        add("tensor", lambda e: e.matmul(
                            PD[:, 0:512], diag4[64:65, 128:256],
                            lrecip[64:65, 0:512], start=True, stop=True),
                            reads=[b_diag4[1], b_lrec], writes=[b_PD])
                        add("vector", lambda e: e.tensor_copy(
                            ysb[:, 512:1024], PD[:, 0:512]),
                            reads=[b_PD], writes=[b_ysb[1]])
                        add("vector", lambda e: e.tensor_copy(
                            lparts[0:1, 22:23], lparts[0:1, 18:19]),
                            reads=[b_lparts], writes=[b_lparts])
                        if h01 == 0:
                            add("vector", (lambda qg=qg, psV=psV: lambda e:
                                           e.tensor_tensor(
                                attnT[0:64, 512 * qg:512 * qg + 512],
                                psV[0:64, 0:512], ysb[0:64, 512:1024],
                                ALU.mult))(),
                                reads=[b_psV, b_ysb[1]], writes=[b_attnT])
                        else:
                            add("vector", (lambda psV=psV, tb=tb: lambda e:
                                           e.tensor_tensor(
                                tb[0:64, 0:512], psV[0:64, 0:512],
                                ysb[0:64, 512:1024], ALU.mult))(),
                                reads=[b_psV, b_ysb[1]], writes=[b_xn1t[0]])
                            add("sync", (lambda qg=qg, tb=tb: lambda e:
                                           e.dma_start(
                                out=attnT[64:128, 512 * qg:512 * qg + 512],
                                in_=tb[0:64, 0:512]))(),
                                reads=[b_xn1t[0]], writes=[b_attnT], dma=True)

                    for j in range(jmax + 1):
                        emit_s(j)
                        emit_exp(j, max(0, j - 4 * qg))
                        if j == 3 and pend_tail[0] is not None:
                            for f in pend_tail[0]:
                                f()
                            pend_tail[0] = None
                        if j == 5 and pend_tail[1] is not None:
                            pend_tail[1]()
                            pend_tail[1] = None
                        if j >= 8:
                            emit_AV(j - 8)
                    if pend_tail[1] is not None:
                        pend_tail[1]()
                        pend_tail[1] = None
                    if qg == 0:
                        for j in range(4):
                            emit_AV(j)
                        pend_tail[0] = [tail_a]
                    else:
                        for j in range(max(0, jmax - 7), jmax - 1):
                            emit_AV(j)
                        pend_tail[0] = [
                            (lambda j=jmax - 1: emit_AV(j)),
                            (lambda j=jmax: emit_AV(j)), tail_a]
                    pend_tail[1] = tail_b

                for h01 in range(2):
                    for qg in range(8):
                        emit_seg(h01, qg)
                for f in pend_tail[0]:
                    f()
                pend_tail[0] = None
                pend_tail[1]()
                pend_tail[1] = None

                add("sync", (lambda p=p: lambda e: e.dma_start(
                    out=spillA_h.ap()[p], in_=attnT[:, :]))(),
                    reads=[b_attnT], writes=[b_spillA[p]], dma=True)
                if DEBUG_OUTPUTS:
                    add("sync", (lambda p=p: lambda e: e.dma_start(
                        out=dbg["attnT"].ap()[p], in_=attnT[:, :]))(),
                        reads=[b_attnT], dma=True)

        for p in range(NPAIR):
            add("gpsimd", (lambda p=p: lambda e: e.dma_start(
                out=wp_sb[:, C * p:C * p + C],
                in_=wp_h[128 * p:128 * p + 128, :]))(),
                writes=[b_wp, b_qpair, b_kpair], dma=True)

        aslab3 = aslab.rearrange("p (par pr x) -> p par pr x", par=2, pr=NPAIR)

        def emit_aslab_read(g, par):
            for p in range(NPAIR):
                add("sync", (lambda g=g, p=p, par=par: lambda e: e.dma_start(
                    out=aslab3[:, par, p, :],
                    in_=spillA_h.ap()[p][:, 512 * g:512 * g + 512]))(),
                    reads=[b_spillA[p]],
                    writes=[b_aslab[par], b_pgT[31],
                            b_attnT, b_wqg, b_wkg], dma=True)

        ffn_ws = [(wr_h, wrT, b_wrT, [b_vgrp]),
                  (wkf_h, wkT, b_wkTb, b_pgT[0:16] + [b_qabs, b_masks, b_vgrp]),
                  (wvf_h, wvT, b_wvTb, b_pgT[15:32] + [b_qabs])]

        def emit_ffn_wslab(s):
            wsrc, wdst, b_wdst, extra = ffn_ws[s // NCC]
            sl = s % NCC
            par = s % 2
            ps, b_ps = (PE_, b_PE) if par == 0 else (PF, b_PF)
            sb_ap = slab[:, C * par:C * par + C]
            add("gpsimd", (lambda sl=sl, wsrc=wsrc, sb_ap=sb_ap: lambda e:
                           e.dma_start(out=sb_ap,
                                       in_=wsrc[128 * sl:128 * sl + 128, :]))(),
                writes=[b_slab[par], b_wkg, b_wvg], dma=True)
            wdst3 = wdst.rearrange("p (cc x) -> p cc x", cc=NCC)
            for half in range(2):
                for cc in range(4 * half, 4 * half + 4):
                    add("tensor", (lambda cc=cc, half=half, sb_ap=sb_ap, ps=ps:
                                   lambda e: e.matmul(
                        ps[:, 128 * (cc - 4 * half):128 * (cc - 4 * half) + 128],
                        sb_ap[:, 128 * cc:128 * cc + 128], ident[:, :],
                        start=True, stop=True))(),
                        reads=[b_slab[par], b_ident], writes=[b_ps])
                add("vector", (lambda sl=sl, half=half, wdst3=wdst3, ps=ps:
                               lambda e: e.tensor_copy(
                    wdst3[:, 4 * half:4 * half + 4, 128 * sl:128 * sl + 128],
                    ps[:, 0:512].rearrange("p (cc x) -> p cc x", cc=4)))(),
                    reads=[b_ps], writes=[b_wdst] + extra)

        xn2T3 = xn2T.rearrange("p (cc t) -> p cc t", cc=NCC)
        add("vector", lambda e: e.memset(xn2T3[:, :, 0:1], 0.0),
            writes=[b_xn2T, b_xnT])

        emit_aslab_read(0, 0)
        _t3_pending = []

        for k in range(NT // 2):
            iA, iB = 2 * k, 2 * k + 1
            g = iA // 4
            if iA % 4 == 0 and g + 1 < 8:
                emit_aslab_read(g + 1, (g + 1) % 2)
            gpar = g % 2
            tiles = []
            for i, par, psY, b_psY in ((iA, 0, PA, b_PA), (iB, 1, PB, b_PB)):
                xin = xt[:, C * par:C * par + C]
                add("sync", (lambda i=i, xin=xin: lambda e: e.dma_start(
                    out=xin, in_=x_h[128 * i:128 * i + 128, :]))(),
                    writes=[b_xt[par]], dma=True)
                tiles.append((i, par, psY, b_psY, xin))
            for (i, par, psY, b_psY, xin) in tiles:
                for p in range(NPAIR):
                    for hf in range(2):
                        add("tensor", (lambda i=i, p=p, hf=hf, gpar=gpar,
                                       psY=psY: lambda e: e.matmul(
                            psY[:, 512 * hf:512 * hf + 512],
                            aslab3[:, gpar, p, 128 * (i % 4):128 * (i % 4) + 128],
                            wp_sb[:, C * p + 512 * hf:C * p + 512 * hf + 512],
                            start=(p == 0), stop=(p == NPAIR - 1)))(),
                            reads=[b_aslab[gpar], b_wp], writes=[b_psY])
            for (i, par, psY, b_psY, xin) in tiles:
                ysl = ysb[:, C * par:C * par + C]
                add("vector", (lambda psY=psY, ysl=ysl: lambda e:
                               e.tensor_tensor(ysl, psY[:, :], bp_b[:, :],
                                               ALU.add))(),
                    reads=[b_psY, b_bp], writes=[b_ysb[par]])
            ln_pair([(ysb[:, C * par:C * par + C], b_ysb[par],
                      y2[:, C * par:C * par + C], b_y2[par], par,
                      y2[:, C * par:C * par + C], b_y2[par])
                     for (i, par, psY, b_psY, xin) in tiles])
            for (i, par, psY, b_psY, xin) in tiles:
                y2l = y2[:, C * par:C * par + C]
                add("vector", (lambda xin=xin, y2l=y2l: lambda e:
                               e.tensor_tensor(xin, y2l, xin, ALU.add))(),
                    reads=[b_y2[par], b_xt[par]], writes=[b_xt[par]])
            for (i, par, psY, b_psY, xin) in tiles:
                add("gpsimd", (lambda i=i, xin=xin: lambda e: e.dma_start(
                    out=spillX_h[128 * i:128 * i + 128, :], in_=xin))(),
                    reads=[b_xt[par]], writes=[b_spillX[i]], dma=True)
                if DEBUG_OUTPUTS:
                    add("gpsimd", (lambda i=i, xin=xin: lambda e: e.dma_start(
                        out=dbg["xnew"][128 * i:128 * i + 128, :], in_=xin))(),
                        reads=[b_xt[par]], dma=True)
            while _t3_pending:
                _t3_pending.pop(0)()
            ln_pair([(xt[:, C * par:C * par + C], b_xt[par],
                      xn1t[:, C * par:C * par + C], b_xn1t[par], 2 + par,
                      y2[:, C * par:C * par + C], b_y2[par])
                     for (i, par, psY, b_psY, xin) in tiles])
            for (i, par, psY, b_psY, xin) in tiles:
                x2 = xn1t[:, C * par:C * par + C]

                def _t3(i=i, par=par, x2=x2):
                    psT, b_psT = (PC, b_PC) if par == 0 else (PD, b_PD)
                    for half in range(2):
                        for cc in range(4 * half, 4 * half + 4):
                            add("tensor", (lambda cc=cc, half=half, x2=x2,
                                           psT=psT: lambda e: e.matmul(
                                psT[:, 128 * (cc - 4 * half):
                                    128 * (cc - 4 * half) + 128],
                                x2[:, 128 * cc:128 * cc + 128], ident[:, :],
                                start=True, stop=True))(),
                                reads=[b_xn1t[par], b_ident], writes=[b_psT])
                        add("vector", (lambda i=i, half=half, psT=psT: lambda e:
                                       e.tensor_copy(
                            xn2T3[:, 4 * half:4 * half + 4,
                                  1 + 128 * i:1 + 128 * i + 128],
                            psT[:, 0:512].rearrange("p (cc x) -> p cc x",
                                                    cc=4)))(),
                            reads=[b_psT], writes=[b_xn2T])
                _t3_pending.append(_t3)
            if iA < 24:
                emit_ffn_wslab(iA)
            if iB < 24:
                emit_ffn_wslab(iB)
        for _fn in _t3_pending:
            _fn()
        _t3_pending = []

        xrT3 = xrT.rearrange("p (cc t) -> p cc t", cc=NCC)
        xkT3 = xkT.rearrange("p (cc t) -> p cc t", cc=NCC)
        kact3 = kact.rearrange("p (dd t) -> p dd t", dd=NCC)
        wrT3 = wrT.rearrange("p (cc x) -> p cc x", cc=NCC)
        wkT3 = wkT.rearrange("p (cc x) -> p cc x", cc=NCC)
        wvT3 = wvT.rearrange("p (dd x) -> p dd x", dd=NCC)

        def emit_mix(ch):
            t0 = 512 * ch
            for cc in range(NCC):
                prev = xn2T3[:, cc, t0:t0 + 512]
                cur = xn2T3[:, cc, t0 + 1:t0 + 513]
                add("vector", (lambda prev=prev, cur=cur: lambda e:
                               e.tensor_tensor(dscr[:, :], cur, prev,
                                               ALU.subtract))(),
                    reads=[b_xn2T], writes=[b_dscr, b_wvg])
                add("vector", (lambda cc=cc, prev=prev: lambda e:
                               e.scalar_tensor_tensor(
                    xrT3[:, cc, :], dscr[:, :], mu_sb[:, cc:cc + 1], prev,
                    ALU.mult, ALU.add))(),
                    reads=[b_dscr, b_mu, b_xn2T], writes=[b_xrT, b_wp])
                add("vector", (lambda cc=cc, prev=prev: lambda e:
                               e.scalar_tensor_tensor(
                    xkT3[:, cc, :], dscr[:, :], mu_sb[:, 8 + cc:8 + cc + 1], prev,
                    ALU.mult, ALU.add))(),
                    reads=[b_dscr, b_mu, b_xn2T], writes=[b_xkT, b_wp])

        def emit_rsig(ch):
            for sub in range(4):
                tpar = sub % 2
                psR, b_psR = (PA, b_PA) if tpar == 0 else (PB, b_PB)
                for hf in range(2):
                    for cc in range(NCC):
                        add("tensor", (lambda sub=sub, hf=hf, cc=cc, psR=psR:
                                       lambda e: e.matmul(
                            psR[:, 512 * hf:512 * hf + 512],
                            xrT3[:, cc, 128 * sub:128 * sub + 128],
                            wrT3[:, cc, 512 * hf:512 * hf + 512],
                            start=(cc == 0), stop=(cc == NCC - 1)))(),
                            reads=[b_xrT, b_wrT], writes=[b_psR])
                add("scalar", (lambda psR=psR: lambda e: e.activation(
                    sqd[:, :], psR[:, :], AF.Exp, bias=0.0, scale=-1.0))(),
                    reads=[b_psR], writes=[b_sqd])
                add("vector", lambda e: e.tensor_scalar(
                    sqd[:, :], sqd[:, :], 1.0, None, ALU.add),
                    reads=[b_sqd], writes=[b_sqd])
                add("vector", lambda e: e.reciprocal(sqd[:, :], sqd[:, :]),
                    reads=[b_sqd], writes=[b_sqd])
                add("scalar", (lambda sub=sub: lambda e: e.copy(
                    r_sb[:, C * sub:C * sub + C], sqd[:, :]))(),
                    reads=[b_sqd], writes=[b_rsb, b_aslab[0], b_aslab[1]])

        def emit_kpath(ch):
            for dd in range(NCC):
                ps, b_ps = (PC, b_PC) if dd % 2 == 0 else (PD, b_PD)
                for cc in range(NCC):
                    add("tensor", (lambda dd=dd, cc=cc, ps=ps: lambda e: e.matmul(
                        ps[:, 0:512], wkT3[:, cc, 128 * dd:128 * dd + 128],
                        xkT3[:, cc, :], start=(cc == 0), stop=(cc == NCC - 1)))(),
                        reads=[b_wkTb, b_xkT], writes=[b_ps])
                add("vector", (lambda ps=ps: lambda e: e.tensor_relu(
                    ps[:, 0:512], ps[:, 0:512]))(),
                    reads=[b_ps], writes=[b_ps])
                add("scalar", (lambda dd=dd, ps=ps: lambda e: e.activation(
                    kact3[:, dd, :], ps[:, 0:512], AF.Square, bias=0.0,
                    scale=1.0))(),
                    reads=[b_ps], writes=[b_kact, b_aslab[0], b_aslab[1]])

        def emit_vtail(ch):
            for sp in range(2):
                tl = []
                for sub in (2 * sp, 2 * sp + 1):
                    tpar = sub % 2
                    i = 4 * ch + sub
                    xin = xt[:, C * tpar:C * tpar + C]
                    add("gpsimd", (lambda i=i, xin=xin: lambda e: e.dma_start(
                        out=xin, in_=spillX_h[128 * i:128 * i + 128, :]))(),
                        reads=[b_spillX[i]], writes=[b_xt[tpar]], dma=True)
                    tl.append((sub, tpar, i, xin))
                for (sub, tpar, i, xin) in tl:
                    for hf in range(2):
                        psV, b_psV = (PE_, b_PE) if hf == 0 else (PF, b_PF)
                        for dd in range(NCC):
                            add("tensor", (lambda sub=sub, hf=hf, dd=dd, psV=psV:
                                           lambda e: e.matmul(
                                psV[:, 0:512],
                                kact3[:, dd, 128 * sub:128 * sub + 128],
                                wvT3[:, dd, 512 * hf:512 * hf + 512],
                                start=(dd == 0), stop=(dd == NCC - 1)))(),
                                reads=[b_kact, b_wvTb], writes=[b_psV])
                        add("vector", (lambda sub=sub, hf=hf, psV=psV, tpar=tpar:
                                       lambda e: e.tensor_tensor(
                            ysb[:, C * tpar + 512 * hf:C * tpar + 512 * hf + 512],
                            psV[:, 0:512],
                            r_sb[:, C * sub + 512 * hf:C * sub + 512 * hf + 512],
                            ALU.mult))(),
                            reads=[b_psV, b_rsb], writes=[b_ysb[tpar]])
                ln_pair([(ysb[:, C * tpar:C * tpar + C], b_ysb[tpar],
                          y2[:, C * tpar:C * tpar + C], b_y2[tpar], tpar,
                          y2[:, C * tpar:C * tpar + C], b_y2[tpar])
                         for (sub, tpar, i, xin) in tl])
                for (sub, tpar, i, xin) in tl:
                    y2l = y2[:, C * tpar:C * tpar + C]
                    add("vector", (lambda xin=xin, y2l=y2l: lambda e:
                                   e.tensor_tensor(
                        xin, y2l, xin, ALU.add))(),
                        reads=[b_y2[tpar], b_xt[tpar]], writes=[b_xt[tpar]])
                    add("sync", (lambda i=i, xin=xin: lambda e: e.dma_start(
                        out=out_h[128 * i:128 * i + 128, :], in_=xin))(),
                        reads=[b_xt[tpar]], writes=[b_out[i]], dma=True)

        emit_mix(0)
        for ch in range(8):
            emit_rsig(ch)
            emit_kpath(ch)
            if ch + 1 < 8:
                emit_mix(ch + 1)
            emit_vtail(ch)

        add("sync", lambda e: e.nop(), deps=[b_out[NT - 1].w])

        with nc.Block() as block:
            prog.emit(block, sems)

    return nc


_NC_CACHE = None


def kernel(L0_attn_x, L0_attn_Wq, L0_attn_Wk, L0_attn_Wv, L0_attn_Wp,
           L0_attn_bp, L0_attn_rel_table, L0_ffn_W_r, L0_ffn_W_k, L0_ffn_W_v,
           L0_ffn_mu_r, L0_ffn_mu_k):
    global _NC_CACHE
    cc = np.ascontiguousarray
    x = cc(np.asarray(L0_attn_x, dtype=np.float32))
    ins = {
        "wq": cc(np.asarray(L0_attn_Wq, dtype=np.float32)),
        "wk": cc(np.asarray(L0_attn_Wk, dtype=np.float32)),
        "wv": cc(np.asarray(L0_attn_Wv, dtype=np.float32)),
        "wp": cc(np.asarray(L0_attn_Wp, dtype=np.float32)),
        "bp": cc(np.asarray(L0_attn_bp, dtype=np.float32)),
        "rel": cc(np.asarray(L0_attn_rel_table, dtype=np.float32)).reshape(-1),
        "wr": cc(np.asarray(L0_ffn_W_r, dtype=np.float32)),
        "wkf": cc(np.asarray(L0_ffn_W_k, dtype=np.float32)),
        "wvf": cc(np.asarray(L0_ffn_W_v, dtype=np.float32)),
        "mur": cc(np.asarray(L0_ffn_mu_r, dtype=np.float32)),
        "muk": cc(np.asarray(L0_ffn_mu_k, dtype=np.float32)),
    }
    if _NC_CACHE is None:
        _NC_CACHE = build_program()
    nc = _NC_CACHE
    in_maps = [dict(ins, x=cc(x[b])) for b in range(8)]
    res = run_bass_kernel_spmd(nc, in_maps, core_ids=list(range(8)))
    return np.stack([res.results[b]["out"] for b in range(8)], axis=0).astype(np.float32)


# version 17
# speedup vs baseline: 1.1383x; 1.0558x over previous
from contextlib import ExitStack

import numpy as np

import concourse.bass as bass
import concourse.mybir as mybir
from concourse.bass_utils import run_bass_kernel_spmd

F32 = mybir.dt.float32
BF16 = mybir.dt.bfloat16
AF = mybir.ActivationFunctionType
ALU = mybir.AluOpType
AX = mybir.AxisListType

T = 4096
C = 1024
H = 16
NT = T // 128
NCC = C // 128
NPAIR = H // 2
EPS = 1e-5
NEG_BIG = -1.0e33

ENGINES = ("sync", "gpsimd", "scalar", "vector", "tensor")

DEBUG_OUTPUTS = False


class _Op:
    __slots__ = ("eng", "fn", "deps", "step", "idx", "val", "awaited")

    def __init__(self, eng, fn, deps, step, idx):
        self.eng = eng
        self.fn = fn
        self.deps = deps
        self.step = step
        self.idx = idx
        self.val = None
        self.awaited = False


class Buf:
    __slots__ = ("name", "w", "rs")

    def __init__(self, name):
        self.name = name
        self.w = None
        self.rs = []


class Prog:
    def __init__(self):
        self.ops = []
        self._last_dma = {}

    def add(self, eng, fn, reads=(), writes=(), deps=(), dma=False):
        d = []
        if dma:
            prev = self._last_dma.get(eng)
            if prev is not None:
                d.append(prev)
        for b in reads:
            if b.w is not None:
                d.append(b.w)
        for b in writes:
            d.extend(b.rs)
            if b.w is not None:
                d.append(b.w)
        d.extend(x for x in deps if x is not None)
        seen = set()
        dd = []
        for x in d:
            if id(x) not in seen:
                seen.add(id(x))
                dd.append(x)
        op = _Op(eng, fn, dd, 16 if dma else 1, len(self.ops))
        if dma:
            self._last_dma[eng] = op
        for x in dd:
            assert x.idx < op.idx
        for b in reads:
            b.rs.append(op)
        for b in writes:
            b.w = op
            b.rs = []
        self.ops.append(op)
        return op

    def emit(self, block, sems):
        for op in self.ops:
            if op.step == 16:
                op.awaited = True
            for dep in op.deps:
                if dep.eng != op.eng or dep.step == 16:
                    dep.awaited = True
        cnt = {e: 0 for e in ENGINES}
        for op in self.ops:
            if op.awaited:
                cnt[op.eng] += op.step
                op.val = cnt[op.eng]

        def make_runner(eng_name):
            def run(eng):
                waited = {}
                for op in self.ops:
                    if op.eng != eng_name:
                        continue
                    need = {}
                    for dep in op.deps:
                        if dep.eng == eng_name and dep.step != 16:
                            continue
                        if dep.val is None:
                            continue
                        s = dep.eng
                        if need.get(s, 0) < dep.val:
                            need[s] = dep.val
                    for s, v in need.items():
                        if waited.get(s, 0) < v:
                            eng.wait_ge(sems[s], v)
                            waited[s] = v
                    ins = op.fn(eng)
                    if op.awaited:
                        assert ins is not None, "awaited op returned None"
                        ins.then_inc(sems[op.eng], op.step)
            return run

        block.sync(make_runner("sync"))
        block.gpsimd(make_runner("gpsimd"))
        block.scalar(make_runner("scalar"))
        block.vector(make_runner("vector"))
        block.tensor(make_runner("tensor"))


def _ceil(a, b):
    return -(-a // b)


def build_program():
    nc = bass.Bass("TRN2", target_bir_lowering=False)

    x_h = nc.dram_tensor("x", [T, C], F32, kind="ExternalInput")
    wq_h = nc.dram_tensor("wq", [H, C, 64], F32, kind="ExternalInput")
    wk_h = nc.dram_tensor("wk", [H, C, 64], F32, kind="ExternalInput")
    wv_h = nc.dram_tensor("wv", [H, C, 64], F32, kind="ExternalInput")
    wp_h = nc.dram_tensor("wp", [C, C], F32, kind="ExternalInput")
    bp_h = nc.dram_tensor("bp", [C], F32, kind="ExternalInput")
    rel_h = nc.dram_tensor("rel", [33 * H], F32, kind="ExternalInput")
    wr_h = nc.dram_tensor("wr", [C, C], F32, kind="ExternalInput")
    wkf_h = nc.dram_tensor("wkf", [C, C], F32, kind="ExternalInput")
    wvf_h = nc.dram_tensor("wvf", [C, C], F32, kind="ExternalInput")
    mur_h = nc.dram_tensor("mur", [C], F32, kind="ExternalInput")
    muk_h = nc.dram_tensor("muk", [C], F32, kind="ExternalInput")

    spillA_h = nc.dram_tensor("spillA", [NPAIR, 128, T], BF16, kind="Internal")
    spillX_h = nc.dram_tensor("spillX", [T, C], F32, kind="Internal")
    out_h = nc.dram_tensor("out", [T, C], F32, kind="ExternalOutput")
    dbg = {}
    if DEBUG_OUTPUTS:
        dbg["xnT"] = nc.dram_tensor("dbg_xnT", [128, NCC * T], BF16, kind="ExternalOutput")
        dbg["attnT"] = nc.dram_tensor("dbg_attnT", [NPAIR, 128, T], BF16, kind="ExternalOutput")
        dbg["xnew"] = nc.dram_tensor("dbg_xnew", [T, C], F32, kind="ExternalOutput")
        dbg["st"] = nc.dram_tensor("dbg_st", [128, 64], F32, kind="ExternalOutput")
        dbg["x0"] = nc.dram_tensor("dbg_x0", [128, C], F32, kind="ExternalOutput")
        dbg["pg"] = nc.dram_tensor("dbg_pg", [128, 4 * T], BF16, kind="ExternalOutput")
        dbg["dg"] = nc.dram_tensor("dbg_dg", [128, 512], BF16, kind="ExternalOutput")
        dbg["nm"] = nc.dram_tensor("dbg_nm", [128, 64], F32, kind="ExternalOutput")
        dbg["bd"] = nc.dram_tensor("dbg_bd", [128, 256], BF16, kind="ExternalOutput")
        dbg["qa"] = nc.dram_tensor("dbg_qa", [128, T], BF16, kind="ExternalOutput")
        dbg["qp"] = nc.dram_tensor("dbg_qp", [128, T], BF16, kind="ExternalOutput")
        dbg["kp"] = nc.dram_tensor("dbg_kp", [128, T], BF16, kind="ExternalOutput")
        dbg["kr"] = nc.dram_tensor("dbg_kr", [128, 1], BF16, kind="ExternalOutput")

    prog = Prog()
    add = prog.add

    with ExitStack() as st:
        def sb(name, shape, dt):
            return st.enter_context(nc.sbuf_tensor(name, shape, dt))

        ident = sb("ident", [128, 128], BF16)
        ones_f = sb("ones_f", [128, 128], F32)
        zeros_f = sb("zeros_f", [128, 128], F32)
        causal_f = sb("causal_f", [128, 128], F32)
        rel8 = sb("rel8", [128, 528], F32)
        bdiag = sb("bdiag", [128, 16 * 128], BF16)
        bsub = sb("bsub", [128, 16 * 128], BF16)
        diag4 = sb("diag4", [128, 4 * 128], BF16)
        bp_b = sb("bp_b", [128, C], F32)
        mu_sb = sb("mu_sb", [128, 32], F32)
        negb16 = sb("negb16", [128, 16], F32)
        stats = sb("stats", [128, 64], F32)
        negm = sb("negm", [128, 2 * NT], F32)
        lparts = sb("lparts", [128, 24], F32)
        gtmp = sb("gtmp", [128, 2], F32)
        krb = sb("krb", [128, 1], BF16)
        epsc = sb("epsc", [128, 1], F32)
        gdiff = sb("gdiff", [128, 272], F32)
        gdiff2 = sb("gdiff2", [128, 272], F32)

        xt = sb("xt", [128, 2 * C], F32)
        ysb = sb("ysb", [128, 2 * C], F32)
        y2 = sb("y2", [128, 2 * C], F32)
        sqd = sb("sqd", [128, C], F32)

        U1 = sb("U1", [128, 32832], BF16)
        U2 = sb("U2", [128, 46592], BF16)

        def u1(off, n):
            return U1[:, off:off + n]

        def u2(off, n):
            return U2[:, off:off + n]

        qpair = u2(0, T)
        kpair = u2(T, T)
        vgrp = u2(2 * T, 2 * T + 128)
        pgrpT = u2(16512, 16384)
        masks = u2(16512, 33 * 128)
        qabs = u2(20608, T)
        attnT = u2(32896, T)
        wq_g = u2(36992, 2048)
        wk_g = u2(39040, 2048)
        wv_g = u2(41088, 2048)
        xn1t = u2(43136, 2048)
        tmp_b = [u2(43136, 512), u2(43648, 512)]
        lrecip = u2(44160, 512)
        wp_sb = u2(0, 8192)
        wrT = u2(8192, 8192)
        wkT = u2(16384, 8192)
        wvT = u2(24576, 8192)
        aslab = u2(32768, 8192)
        slab = u2(40960, 2048)
        dscr = u2(45696, 512)
        xrT = u2(0, T)
        xkT = u2(T, T)
        kact = u2(32768, T)
        r_sb = u2(36864, T)

        xnT = u1(0, NCC * T)
        xn2T = u1(0, NCC * (T + 8))

        PA = st.enter_context(nc.psum_tensor("PA", [128, 1024], F32))
        PB = st.enter_context(nc.psum_tensor("PB", [128, 1024], F32))
        PC = st.enter_context(nc.psum_tensor("PC", [128, 512], F32))
        PD = st.enter_context(nc.psum_tensor("PD", [128, 512], F32))
        PE_ = st.enter_context(nc.psum_tensor("PE", [128, 512], F32))
        PF = st.enter_context(nc.psum_tensor("PF", [128, 512], F32))

        sems = {e: st.enter_context(nc.semaphore("sem_" + e)) for e in ENGINES}

        b_ident = Buf("ident"); b_ones = Buf("ones"); b_zeros = Buf("zeros")
        b_causal = Buf("causal"); b_rel8 = Buf("rel8")
        b_bdiag = [Buf(f"bd{h}") for h in range(H)]
        b_bsub = [Buf(f"bs{h}") for h in range(H)]
        b_masks = Buf("masks"); b_bp = Buf("bp"); b_mu = Buf("mu")
        b_negb = Buf("negb")
        b_stats = [Buf(f"st{i}") for i in range(4)]
        b_negm = [Buf("negm0"), Buf("negm1")]
        b_lparts = Buf("lparts"); b_gtmp = Buf("gtmp")
        b_diag4 = [Buf(f"dg{i}") for i in range(4)]
        b_xnT = Buf("xnT"); b_qabs = Buf("qabs")
        b_pgT = [Buf(f"pgT{j}") for j in range(NT)]
        b_negmg = Buf("negmg"); b_linv = Buf("linv"); b_lrec = Buf("lrec")
        b_qpair = Buf("qpair"); b_kpair = Buf("kpair"); b_vgrp = Buf("vgrp")
        b_attnT = Buf("attnT")
        b_wqg = Buf("wqg"); b_wkg = Buf("wkg"); b_wvg = Buf("wvg")
        b_xn1t = [Buf("xn1t0"), Buf("xn1t1")]
        b_xn2T = Buf("xn2T"); b_wp = Buf("wp")
        b_wrT = Buf("wrT"); b_wkTb = Buf("wkT"); b_wvTb = Buf("wvT")
        b_slab = [Buf("slab0"), Buf("slab1")]
        b_aslab = [Buf("aslab0"), Buf("aslab1")]
        b_xrT = Buf("xrT"); b_xkT = Buf("xkT"); b_kact = Buf("kact")
        b_rsb = Buf("rsb"); b_dscr = Buf("dscr")
        b_xt = [Buf("xt0"), Buf("xt1")]
        b_ysb = [Buf("ysb0"), Buf("ysb1")]
        b_y2 = [Buf("y20"), Buf("y21")]; b_sqd = Buf("sqd")
        b_krmax = Buf("krmax"); b_eps = Buf("eps")
        b_gdiff = Buf("gdiff"); b_gdiff2 = Buf("gdiff2")
        b_PA = Buf("PA"); b_PB = Buf("PB"); b_PC = Buf("PC")
        b_PD = Buf("PD"); b_PE = Buf("PE"); b_PF = Buf("PF")
        b_spillA = [Buf(f"spA{p}") for p in range(NPAIR)]
        b_spillX = [Buf(f"spX{i}") for i in range(NT)]
        b_out = [Buf(f"out{i}") for i in range(NT)]

        add("vector", lambda e: e.memset(ones_f[:, :], 1.0), writes=[b_ones])
        add("vector", lambda e: e.memset(epsc[:, :], EPS), writes=[b_eps])
        add("vector", lambda e: e.memset(zeros_f[:, :], 0.0), writes=[b_zeros])
        add("vector", lambda e: e.memset(diag4[:, 0:128], 1.0), writes=[b_diag4[0]])
        add("gpsimd", lambda e: e.affine_select(
            ident[:, :], diag4[:, 0:128], pattern=[[1, 128]], base=0,
            channel_multiplier=-1, compare_op=ALU.is_equal, fill=0.0),
            reads=[b_diag4[0]], writes=[b_ident])
        add("vector", lambda e: e.memset(diag4[:, 128:256], 1.0),
            writes=[b_diag4[1]])
        add("vector", lambda e: e.memset(ysb[:, 0:128], NEG_BIG),
            writes=[b_ysb[0]])
        add("gpsimd", lambda e: e.affine_select(
            causal_f[:, :], ysb[:, 0:128], pattern=[[1, 128]], base=-1,
            channel_multiplier=-1, compare_op=ALU.is_ge, fill=0.0),
            reads=[b_ysb[0]], writes=[b_causal])

        def _mask_op(slot, base):
            def f(e):
                return e.affine_select(
                    masks[:, 128 * slot:128 * slot + 128], diag4[:, 0:128],
                    pattern=[[1, 128]], base=base, channel_multiplier=-1,
                    compare_op=ALU.is_equal, fill=0.0)
            return f
        for d in range(17):
            add("gpsimd", _mask_op(d, d), reads=[b_diag4[0]],
                writes=[b_masks] + b_pgT[0:9])
        for d in range(1, 17):
            add("gpsimd", _mask_op(17 + d - 1, d - 128), reads=[b_diag4[0]],
                writes=[b_masks] + b_pgT[0:9])

        add("sync", lambda e: e.dma_start(
            out=rel8[:, :], in_=rel_h.ap().rearrange("(one c) -> one c", one=1)
            .partition_broadcast(128)), writes=[b_rel8], dma=True)
        add("sync", lambda e: e.dma_start(
            out=bp_b[:, :], in_=bp_h.ap().rearrange("(one c) -> one c", one=1)
            .partition_broadcast(128)), writes=[b_bp], dma=True)
        add("gpsimd", lambda e: e.dma_start(
            out=mu_sb[:, 0:8], in_=mur_h.ap().rearrange("(cc p) -> p cc", p=128),
            allow_slow_non_contiguous=True), writes=[b_mu], dma=True)
        add("gpsimd", lambda e: e.dma_start(
            out=mu_sb[:, 8:16], in_=muk_h.ap().rearrange("(cc p) -> p cc", p=128),
            allow_slow_non_contiguous=True), writes=[b_mu], dma=True)
        add("vector", lambda e: e.tensor_tensor(
            mu_sb[:, 16:32], ones_f[:, 0:16], mu_sb[:, 0:16], ALU.subtract),
            reads=[b_mu, b_ones], writes=[b_mu])

        add("vector", lambda e: e.tensor_scalar(
            rel8[:, :], rel8[:, :], 8.0, None, ALU.mult),
            reads=[b_rel8], writes=[b_rel8])
        add("vector", lambda e: e.reduce_max(
            stats[:, 63:64], rel8[:, :], axis=AX.X, apply_absolute_value=True),
            reads=[b_rel8], writes=[b_stats[0]])
        add("scalar", lambda e: e.mul(
            stats[:, 62:63], stats[:, 63:64], 0.25),
            reads=[b_stats[0]], writes=[b_stats[0]])
        add("vector", lambda e: e.tensor_scalar(
            negb16[:, :], ones_f[:, 0:16], stats[:, 62:63], None, ALU.mult),
            reads=[b_ones, b_stats[0]], writes=[b_negb])

        for d in range(17):
            add("vector", (lambda d=d: lambda e: e.tensor_tensor(
                gdiff[:, 16 * d:16 * d + 16], rel8[:, (16 - d) * H:(16 - d) * H + 16],
                rel8[:, 0:16], ALU.subtract))(),
                reads=[b_rel8], writes=[b_gdiff])
        add("scalar", lambda e: e.copy(gdiff2[:, :], gdiff[:, :]),
            reads=[b_gdiff], writes=[b_gdiff2])
        for h in range(H):
            add("vector", (lambda h=h: lambda e: e.tensor_copy(
                bdiag[:, 128 * h:128 * h + 128], causal_f[:, :]))(),
                reads=[b_causal], writes=[b_bdiag[h]])
            add("vector", (lambda h=h: lambda e: e.memset(
                bsub[:, 128 * h:128 * h + 128], 0.0))(),
                writes=[b_bsub[h]])
            for d in range(17):
                add("vector", (lambda h=h, d=d: lambda e: e.scalar_tensor_tensor(
                    bdiag[:, 128 * h:128 * h + 128],
                    masks[:, 128 * d:128 * d + 128], gdiff2[:, 16 * d + h:16 * d + h + 1],
                    bdiag[:, 128 * h:128 * h + 128], ALU.mult, ALU.add))(),
                    reads=[b_masks, b_gdiff2, b_bdiag[h]] + b_pgT[0:9],
                    writes=[b_bdiag[h]])
                if d >= 1:
                    add("vector", (lambda h=h, d=d: lambda e: e.scalar_tensor_tensor(
                        bsub[:, 128 * h:128 * h + 128],
                        masks[:, 128 * (17 + d - 1):128 * (17 + d - 1) + 128],
                        gdiff2[:, 16 * d + h:16 * d + h + 1],
                        bsub[:, 128 * h:128 * h + 128], ALU.mult, ALU.add))(),
                        reads=[b_masks, b_gdiff2, b_bsub[h]] + b_pgT[0:9],
                        writes=[b_bsub[h]])

        def ln_stats_apply(xin_ap, b_xin, out_ap, b_out_buf, sidx,
                           dump_ap=None, b_dump=None):
            s0 = 12 * sidx
            add("vector", lambda e: e.reduce_sum(
                stats[:, s0:s0 + 1], xin_ap, axis=AX.X),
                reads=[b_xin], writes=[b_stats[sidx]])
            add("vector", lambda e: e.scalar_tensor_tensor(
                dump_ap, xin_ap, 1.0, xin_ap, ALU.mult, ALU.mult,
                accum_out=stats[:, s0 + 1:s0 + 2]),
                reads=[b_xin], writes=[b_dump, b_stats[sidx]])
            add("scalar", lambda e: e.mul(
                stats[:, s0 + 2:s0 + 3], stats[:, s0:s0 + 1], 1.0 / C),
                reads=[b_stats[sidx]], writes=[b_stats[sidx]])
            add("scalar", lambda e: e.mul(
                stats[:, s0 + 3:s0 + 4], stats[:, s0:s0 + 1], 1.0 / C),
                reads=[b_stats[sidx]], writes=[b_stats[sidx]])
            add("vector", lambda e: e.tensor_scalar(
                stats[:, s0 + 4:s0 + 5], stats[:, s0 + 2:s0 + 3],
                stats[:, s0 + 3:s0 + 4], None, ALU.mult),
                reads=[b_stats[sidx]], writes=[b_stats[sidx]])
            add("gpsimd", lambda e: e.tensor_copy(
                stats[:, s0 + 5:s0 + 6], stats[:, s0 + 4:s0 + 5]),
                reads=[b_stats[sidx]], writes=[b_stats[sidx]])
            add("vector", lambda e: e.scalar_tensor_tensor(
                stats[:, s0 + 6:s0 + 7], stats[:, s0 + 1:s0 + 2], 1.0 / C,
                stats[:, s0 + 5:s0 + 6], ALU.mult, ALU.subtract),
                reads=[b_stats[sidx]], writes=[b_stats[sidx]])
            add("scalar", lambda e: e.activation(
                stats[:, s0 + 7:s0 + 8], stats[:, s0 + 6:s0 + 7], AF.Ln,
                bias=epsc[:, :], scale=1.0),
                reads=[b_stats[sidx], b_eps], writes=[b_stats[sidx]])
            add("vector", lambda e: e.tensor_copy(
                stats[:, s0 + 8:s0 + 9], stats[:, s0 + 7:s0 + 8]),
                reads=[b_stats[sidx]], writes=[b_stats[sidx]])
            add("scalar", lambda e: e.activation(
                stats[:, s0 + 9:s0 + 10], stats[:, s0 + 8:s0 + 9], AF.Exp,
                bias=0.0, scale=-0.5),
                reads=[b_stats[sidx]], writes=[b_stats[sidx]])
            add("vector", lambda e: e.tensor_scalar(
                out_ap, xin_ap, stats[:, s0 + 2:s0 + 3], stats[:, s0 + 9:s0 + 10],
                ALU.subtract, ALU.mult),
                reads=[b_xin, b_stats[sidx]], writes=[b_out_buf])

        def ln_pair(specs):
            for (xin_ap, b_xin, _o, _bo, sidx, dump_ap, b_dump) in specs:
                s0 = 12 * sidx
                add("vector", (lambda xin_ap=xin_ap, s0=s0: lambda e:
                               e.reduce_sum(stats[:, s0:s0 + 1], xin_ap,
                                            axis=AX.X))(),
                    reads=[b_xin], writes=[b_stats[sidx]])
            for (xin_ap, b_xin, _o, _bo, sidx, dump_ap, b_dump) in specs:
                s0 = 12 * sidx
                add("vector", (lambda xin_ap=xin_ap, s0=s0, dump_ap=dump_ap:
                               lambda e: e.scalar_tensor_tensor(
                    dump_ap, xin_ap, 1.0, xin_ap, ALU.mult, ALU.mult,
                    accum_out=stats[:, s0 + 1:s0 + 2]))(),
                    reads=[b_xin], writes=[b_dump, b_stats[sidx]])
            for (_x, _bx, _o, _bo, sidx, _d, _bd) in specs:
                s0 = 12 * sidx
                add("vector", (lambda s0=s0: lambda e: e.tensor_scalar(
                    stats[:, s0 + 2:s0 + 3], stats[:, s0:s0 + 1], 1.0 / C,
                    None, ALU.mult))(),
                    reads=[b_stats[sidx]], writes=[b_stats[sidx]])
            for (_x, _bx, _o, _bo, sidx, _d, _bd) in specs:
                s0 = 12 * sidx
                add("vector", (lambda s0=s0: lambda e: e.scalar_tensor_tensor(
                    stats[:, s0 + 3:s0 + 4], stats[:, s0:s0 + 1], 1.0 / C,
                    stats[:, s0 + 2:s0 + 3], ALU.mult, ALU.mult))(),
                    reads=[b_stats[sidx]], writes=[b_stats[sidx]])
            for (_x, _bx, _o, _bo, sidx, _d, _bd) in specs:
                s0 = 12 * sidx
                add("gpsimd", (lambda s0=s0: lambda e: e.tensor_copy(
                    stats[:, s0 + 7:s0 + 8], stats[:, s0 + 1:s0 + 2]))(),
                    reads=[b_stats[sidx]], writes=[b_stats[sidx]])
            for (_x, _bx, _o, _bo, sidx, _d, _bd) in specs:
                s0 = 12 * sidx
                add("vector", (lambda s0=s0: lambda e: e.scalar_tensor_tensor(
                    stats[:, s0 + 4:s0 + 5], stats[:, s0 + 7:s0 + 8], 1.0 / C,
                    stats[:, s0 + 3:s0 + 4], ALU.mult, ALU.subtract))(),
                    reads=[b_stats[sidx]], writes=[b_stats[sidx]])
            for (_x, _bx, _o, _bo, sidx, _d, _bd) in specs:
                s0 = 12 * sidx
                add("scalar", (lambda s0=s0: lambda e: e.activation(
                    stats[:, s0 + 5:s0 + 6], stats[:, s0 + 4:s0 + 5], AF.Ln,
                    bias=epsc[:, :], scale=1.0))(),
                    reads=[b_stats[sidx], b_eps], writes=[b_stats[sidx]])
            for (_x, _bx, _o, _bo, sidx, _d, _bd) in specs:
                s0 = 12 * sidx
                add("scalar", (lambda s0=s0: lambda e: e.activation(
                    stats[:, s0 + 6:s0 + 7], stats[:, s0 + 5:s0 + 6], AF.Exp,
                    bias=0.0, scale=-0.5))(),
                    reads=[b_stats[sidx]], writes=[b_stats[sidx]])
            for (xin_ap, b_xin, out_ap, b_out_buf, sidx, _d, _bd) in specs:
                s0 = 12 * sidx
                add("vector", (lambda xin_ap=xin_ap, out_ap=out_ap, s0=s0:
                               lambda e: e.tensor_scalar(
                    out_ap, xin_ap, stats[:, s0 + 2:s0 + 3],
                    stats[:, s0 + 6:s0 + 7], ALU.subtract, ALU.mult))(),
                    reads=[b_xin, b_stats[sidx]], writes=[b_out_buf])


        xnT3 = xnT.rearrange("p (cc t) -> p cc t", cc=NCC)
        PSP = {0: (PA, b_PA), 1: (PB, b_PB)}

        for k2 in range(NT // 2):
            iA, iB = 2 * k2, 2 * k2 + 1
            tl = []
            for i, par in ((iA, 0), (iB, 1)):
                ps, b_ps = PSP[par]
                xin = xt[:, C * par:C * par + C]
                add("sync", (lambda i=i, xin=xin: lambda e: e.dma_start(
                    out=xin, in_=x_h[128 * i:128 * i + 128, :]))(),
                    writes=[b_xt[par]], dma=True)
                tl.append((i, par, ps, b_ps, xin))
            ln_pair([(xin, b_xt[par], xn1t[:, C * par:C * par + C],
                      b_xn1t[par], par, y2[:, C * par:C * par + C], b_y2[par])
                     for (i, par, ps, b_ps, xin) in tl])
            for (i, par, ps, b_ps, xin) in tl:
                x1 = xn1t[:, C * par:C * par + C]
                for cc in range(NCC):
                    add("tensor", (lambda cc=cc, x1=x1, ps=ps: lambda e: e.matmul(
                        ps[:, 128 * cc:128 * cc + 128],
                        x1[:, 128 * cc:128 * cc + 128], ident[:, :],
                        start=True, stop=True))(),
                        reads=[b_xn1t[par], b_ident], writes=[b_ps])
                add("vector", (lambda i=i, ps=ps: lambda e: e.tensor_copy(
                    xnT3[:, :, 128 * i:128 * i + 128],
                    ps[:, :].rearrange("p (cc t) -> p cc t", cc=NCC)))(),
                    reads=[b_ps], writes=[b_xnT])

        if DEBUG_OUTPUTS:
            add("sync", lambda e: e.dma_start(out=dbg["xnT"].ap(), in_=xnT),
                reads=[b_xnT], dma=True)

        wq3 = wq_g.rearrange("p (pp cc x) -> p pp cc x", pp=2, cc=NCC)
        wk3 = wk_g.rearrange("p (pp cc x) -> p pp cc x", pp=2, cc=NCC)
        wv3 = wv_g.rearrange("p (cc x) -> p cc x", cc=NCC)
        vg4 = vgrp.rearrange("p (t m e) -> p t m e", m=4, e=65)
        add("vector", lambda e: e.memset(vg4[:, :, :, 64:65], 1.0),
            writes=[b_vgrp])

        def wload(dst3_sel, src_h, head):
            def f(e):
                return e.dma_start(
                    out=dst3_sel,
                    in_=src_h.ap()[head].rearrange("(cc p) d -> p cc d", p=128))
            return f

        for g in range(4):
            pairs = (2 * g, 2 * g + 1)
            for pp, p in enumerate(pairs):
                for h01 in range(2):
                    h = 2 * p + h01
                    add("gpsimd", wload(wq3[:, pp, :, 64 * h01:64 * h01 + 64], wq_h, h),
                        writes=[b_wqg], dma=True)
                    add("gpsimd", wload(wk3[:, pp, :, 64 * h01:64 * h01 + 64], wk_h, h),
                        writes=[b_wkg], dma=True)
            for m in range(4):
                add("gpsimd", wload(wv3[:, :, 64 * m:64 * m + 64], wv_h, 4 * g + m),
                    writes=[b_wvg], dma=True)

            for ti in range(NT):
                ps, b_ps = (PC, b_PC) if ti % 2 == 0 else (PD, b_PD)
                for cc in range(NCC):
                    add("tensor", (lambda ti=ti, cc=cc, ps=ps: lambda e: e.matmul(
                        ps[:, 0:256],
                        xnT[:, T * cc + 128 * ti: T * cc + 128 * ti + 128],
                        wv_g[:, 256 * cc:256 * cc + 256],
                        start=(cc == 0), stop=(cc == NCC - 1)))(),
                        reads=[b_xnT, b_wvg], writes=[b_ps])
                add("vector", (lambda ti=ti, ps=ps: lambda e: e.tensor_copy(
                    vg4[:, ti, :, 0:64],
                    ps[:, 0:256].rearrange("p (m e) -> p m e", m=4)))(),
                    reads=[b_ps], writes=[b_vgrp])

            for pp, p in enumerate(pairs):
                for which, dst, b_dst, wsb3, b_w in (
                        (0, qpair, b_qpair, wq3, b_wqg),
                        (1, kpair, b_kpair, wk3, b_wkg)):
                    for tc in range(8):
                        ps, b_ps = (PA, b_PA) if tc % 2 == 0 else (PB, b_PB)
                        for cc in range(NCC):
                            add("tensor", (lambda pp=pp, tc=tc, cc=cc, ps=ps,
                                           wsb3=wsb3: lambda e: e.matmul(
                                ps[:, 0:512],
                                wsb3[:, pp, cc, :],
                                xnT[:, T * cc + 512 * tc: T * cc + 512 * tc + 512],
                                start=(cc == 0), stop=(cc == NCC - 1)))(),
                                reads=[b_xnT, b_w], writes=[b_ps])
                        add("vector", (lambda tc=tc, ps=ps, dst=dst: lambda e:
                            e.tensor_copy(dst[:, 512 * tc:512 * tc + 512],
                                          ps[:, 0:512]))(),
                            reads=[b_ps], writes=[b_dst])

                add("vector", lambda e: e.tensor_scalar(
                    qabs.bitcast(mybir.dt.uint16), qpair.bitcast(mybir.dt.uint16),
                    0x7FFF, None, ALU.bitwise_and),
                    reads=[b_qpair], writes=[b_qabs] + b_pgT[8:16])
                add("vector", lambda e: e.reduce_max(
                    stats[:, 61:62], kpair[:, :], axis=AX.X,
                    apply_absolute_value=True),
                    reads=[b_kpair], writes=[b_krmax])
                add("scalar", lambda e: e.copy(krb[:, :], stats[:, 61:62]),
                    reads=[b_krmax], writes=[b_krmax])
                if DEBUG_OUTPUTS and p == 0:
                    add("gpsimd", lambda e: e.dma_start(out=dbg["qa"].ap(), in_=qabs),
                        reads=[b_qabs], dma=True)
                    add("gpsimd", lambda e: e.dma_start(out=dbg["qp"].ap(), in_=qpair),
                        reads=[b_qpair], dma=True)
                    add("gpsimd", lambda e: e.dma_start(out=dbg["kp"].ap(), in_=kpair),
                        reads=[b_kpair], dma=True)
                    add("gpsimd", lambda e: e.dma_start(out=dbg["kr"].ap(), in_=krb[:, :]),
                        reads=[b_krmax], dma=True)
                for h01 in range(2):
                    rows = slice(64 * h01, 64 * h01 + 64)
                    for qg in range(8):
                        seg = 8 * h01 + qg
                        psM, b_psM = (PC, b_PC) if seg % 2 == 0 else (PD, b_PD)
                        add("tensor", (lambda rows=rows, qg=qg, psM=psM:
                                       lambda e: e.matmul(
                            psM[0:1, 0:512], krb[rows, 0:1],
                            qabs[rows, 512 * qg:512 * qg + 512],
                            start=True, stop=True))(),
                            reads=[b_krmax, b_qabs], writes=[b_psM])
                        add("vector", (lambda seg=seg, psM=psM: lambda e:
                                       e.reduce_max(
                            lparts[0:1, seg:seg + 1], psM[0:1, 0:512],
                            axis=AX.X))(),
                            reads=[b_psM], writes=[b_lparts])
                for h01 in range(2):
                    add("vector", (lambda h01=h01: lambda e: e.reduce_max(
                        lparts[0:1, 16 + h01:17 + h01],
                        lparts[0:1, 8 * h01:8 * h01 + 8], axis=AX.X))(),
                        reads=[b_lparts], writes=[b_lparts])
                for h01 in range(2):
                    add("scalar", (lambda h01=h01: lambda e: e.copy(
                        diag4[0:1, 256 + h01:257 + h01],
                        lparts[0:1, 16 + h01:17 + h01]))(),
                        reads=[b_lparts], writes=[b_diag4[2]])
                    psM, b_psM = (PC, b_PC) if h01 == 0 else (PD, b_PD)
                    add("tensor", (lambda h01=h01, psM=psM: lambda e: e.matmul(
                        psM[:, 0:1], diag4[0:1, 128:256],
                        diag4[0:1, 256 + h01:257 + h01],
                        start=True, stop=True))(),
                        reads=[b_diag4[1], b_diag4[2]], writes=[b_psM])
                    add("vector", (lambda h01=h01, psM=psM: lambda e:
                                   e.scalar_tensor_tensor(
                        negm[:, 16 + h01:17 + h01], psM[:, 0:1], 0.125,
                        negb16[:, 0:1], ALU.mult, ALU.add))(),
                        reads=[b_psM, b_negb], writes=[b_negmg])
                for h01 in range(2):
                    add("vector", (lambda h01=h01: lambda e: e.tensor_tensor(
                        negm[:, h01:h01 + 1], zeros_f[:, 0:1],
                        negm[:, 16 + h01:17 + h01], ALU.subtract))(),
                        reads=[b_zeros, b_negmg], writes=[b_negmg])

                pend_tail = [None, None]
                rot = [0]

                def emit_seg(h01, qg, pp=pp, p=p):
                    h = 2 * p + h01
                    rows = slice(64 * h01, 64 * h01 + 64)
                    m = 2 * pp + h01
                    seg = 8 * h01 + qg
                    jmax = 4 * qg + 3
                    psV, b_psV = (PE_, b_PE) if seg % 2 == 0 else (PF, b_PF)
                    tb = tmp_b[qg % 2]

                    def emit_s(j, r):
                        ii0 = max(0, j - 4 * qg)
                        ps, b_ps = ((PA, b_PA), (PA, b_PA), (PB, b_PB),
                                    (PB, b_PB), (PC, b_PC))[r]
                        c0 = 512 * (r % 2) if r < 4 else 0
                        blks = []
                        if 0 <= j - 4 * qg <= 3:
                            blks.append((j - 4 * qg, bdiag, b_bdiag[h]))
                        if 0 <= j - 4 * qg + 1 <= 3:
                            blks.append((j - 4 * qg + 1, bsub, b_bsub[h]))
                        add("tensor", (lambda j=j, ii0=ii0, c0=c0, ps=ps,
                                       rows=rows, qg=qg,
                                       st=(len(blks) == 0):
                                       lambda e: e.matmul(
                            ps[:, c0 + 128 * ii0:c0 + 512],
                            kpair[rows, 128 * j:128 * j + 128],
                            qpair[rows, 512 * qg + 128 * ii0:512 * qg + 512],
                            start=True, stop=st))(),
                            reads=[b_qpair, b_kpair], writes=[b_ps])
                        for bi, (ii, btile, bbuf) in enumerate(blks):
                            add("tensor", (lambda ii=ii, c0=c0, ps=ps, h=h,
                                           btile=btile,
                                           st=(bi == len(blks) - 1):
                                           lambda e: e.matmul(
                                ps[:, c0 + 128 * ii:c0 + 128 * ii + 128],
                                btile[:, 128 * h:128 * h + 128], ident[:, :],
                                start=False, stop=st))(),
                                reads=[bbuf, b_ident], writes=[b_ps])

                    def emit_exp(j0, ii0, r):
                        ps, b_ps = ((PA, b_PA), (PA, b_PA), (PB, b_PB),
                                    (PB, b_PB), (PC, b_PC))[r]
                        c0 = 512 * (r % 2) if r < 4 else 0
                        add("scalar", (lambda j0=j0, ii0=ii0, c0=c0,
                                       ps=ps, h01=h01: lambda e: e.activation(
                            pgrpT[:, 512 * j0 + 128 * ii0:512 * j0 + 512],
                            ps[:, c0 + 128 * ii0:c0 + 512], AF.Exp,
                            bias=negm[:, h01:h01 + 1], scale=0.125))(),
                            reads=[b_ps, b_negmg], writes=[b_pgT[j0]])

                    def emit_AV(j):
                        ii0 = max(0, j - 4 * qg)
                        add("tensor", (lambda j=j, ii0=ii0, m=m, psV=psV,
                                       jmax=jmax: lambda e: e.matmul(
                            psV[0:65, 128 * ii0:512],
                            vg4[:, j, m, :],
                            pgrpT[:, 512 * j + 128 * ii0:512 * j + 512],
                            start=(j == 0), stop=(j == jmax)))(),
                            reads=[b_vgrp, b_pgT[j]], writes=[b_psV])

                    def tail_a():
                        add("vector", (lambda psV=psV: lambda e: e.tensor_copy(
                            ysb[64:65, 0:512], psV[64:65, 0:512]))(),
                            reads=[b_psV], writes=[b_ysb[0]])
                        add("vector", lambda e: e.tensor_copy(
                            lparts[0:1, 20:21], lparts[0:1, 18:19]),
                            reads=[b_lparts], writes=[b_lparts])
                        add("vector", lambda e: e.reciprocal(
                            ysb[64:65, 0:512], ysb[64:65, 0:512]),
                            reads=[b_ysb[0]], writes=[b_ysb[0]])
                        add("vector", lambda e: e.tensor_copy(
                            lparts[0:1, 21:22], lparts[0:1, 18:19]),
                            reads=[b_lparts], writes=[b_lparts])
                        add("vector", lambda e: e.tensor_copy(
                            lrecip[64:65, 0:512], ysb[64:65, 0:512]),
                            reads=[b_ysb[0]],
                            writes=[b_lrec, b_xn1t[1]])

                    def tail_b():
                        add("tensor", lambda e: e.matmul(
                            PD[:, 0:512], diag4[64:65, 128:256],
                            lrecip[64:65, 0:512], start=True, stop=True),
                            reads=[b_diag4[1], b_lrec], writes=[b_PD])
                        add("vector", lambda e: e.tensor_copy(
                            ysb[:, 512:1024], PD[:, 0:512]),
                            reads=[b_PD], writes=[b_ysb[1]])
                        add("vector", lambda e: e.tensor_copy(
                            lparts[0:1, 22:23], lparts[0:1, 18:19]),
                            reads=[b_lparts], writes=[b_lparts])
                        if h01 == 0:
                            add("vector", (lambda qg=qg, psV=psV: lambda e:
                                           e.tensor_tensor(
                                attnT[0:64, 512 * qg:512 * qg + 512],
                                psV[0:64, 0:512], ysb[0:64, 512:1024],
                                ALU.mult))(),
                                reads=[b_psV, b_ysb[1]], writes=[b_attnT])
                        else:
                            add("vector", (lambda psV=psV, tb=tb: lambda e:
                                           e.tensor_tensor(
                                tb[0:64, 0:512], psV[0:64, 0:512],
                                ysb[0:64, 512:1024], ALU.mult))(),
                                reads=[b_psV, b_ysb[1]], writes=[b_xn1t[0]])
                            add("sync", (lambda qg=qg, tb=tb: lambda e:
                                           e.dma_start(
                                out=attnT[64:128, 512 * qg:512 * qg + 512],
                                in_=tb[0:64, 0:512]))(),
                                reads=[b_xn1t[0]], writes=[b_attnT], dma=True)

                    for j in range(jmax + 1):
                        r = rot[0] % 5
                        rot[0] += 1
                        emit_s(j, r)
                        emit_exp(j, max(0, j - 4 * qg), r)
                        if j == 3 and pend_tail[0] is not None:
                            for f in pend_tail[0]:
                                f()
                            pend_tail[0] = None
                        if j == 7 and pend_tail[1] is not None:
                            pend_tail[1]()
                            pend_tail[1] = None
                        if j >= 9:
                            emit_AV(j - 9)
                    if pend_tail[1] is not None:
                        pend_tail[1]()
                        pend_tail[1] = None
                    if qg == 0:
                        for j in range(4):
                            emit_AV(j)
                        pend_tail[0] = [tail_a]
                    else:
                        for j in range(max(0, jmax - 8), jmax - 1):
                            emit_AV(j)
                        pend_tail[0] = [
                            (lambda j=jmax - 1: emit_AV(j)),
                            (lambda j=jmax: emit_AV(j)), tail_a]
                    pend_tail[1] = tail_b

                for h01 in range(2):
                    for qg in range(8):
                        emit_seg(h01, qg)
                for f in pend_tail[0]:
                    f()
                pend_tail[0] = None
                pend_tail[1]()
                pend_tail[1] = None

                add("sync", (lambda p=p: lambda e: e.dma_start(
                    out=spillA_h.ap()[p], in_=attnT[:, :]))(),
                    reads=[b_attnT], writes=[b_spillA[p]], dma=True)
                if DEBUG_OUTPUTS:
                    add("sync", (lambda p=p: lambda e: e.dma_start(
                        out=dbg["attnT"].ap()[p], in_=attnT[:, :]))(),
                        reads=[b_attnT], dma=True)

        for p in range(NPAIR):
            add("gpsimd", (lambda p=p: lambda e: e.dma_start(
                out=wp_sb[:, C * p:C * p + C],
                in_=wp_h[128 * p:128 * p + 128, :]))(),
                writes=[b_wp, b_qpair, b_kpair], dma=True)

        aslab3 = aslab.rearrange("p (par pr x) -> p par pr x", par=2, pr=NPAIR)

        def emit_aslab_read(g, par):
            for p in range(NPAIR):
                add("sync", (lambda g=g, p=p, par=par: lambda e: e.dma_start(
                    out=aslab3[:, par, p, :],
                    in_=spillA_h.ap()[p][:, 512 * g:512 * g + 512]))(),
                    reads=[b_spillA[p]],
                    writes=[b_aslab[par], b_pgT[31],
                            b_attnT, b_wqg, b_wkg], dma=True)

        ffn_ws = [(wr_h, wrT, b_wrT, [b_vgrp]),
                  (wkf_h, wkT, b_wkTb, b_pgT[0:16] + [b_qabs, b_masks, b_vgrp]),
                  (wvf_h, wvT, b_wvTb, b_pgT[15:32] + [b_qabs])]

        def emit_ffn_wslab(s):
            wsrc, wdst, b_wdst, extra = ffn_ws[s // NCC]
            sl = s % NCC
            par = s % 2
            ps, b_ps = (PE_, b_PE) if par == 0 else (PF, b_PF)
            sb_ap = slab[:, C * par:C * par + C]
            add("gpsimd", (lambda sl=sl, wsrc=wsrc, sb_ap=sb_ap: lambda e:
                           e.dma_start(out=sb_ap,
                                       in_=wsrc[128 * sl:128 * sl + 128, :]))(),
                writes=[b_slab[par], b_wkg, b_wvg], dma=True)
            wdst3 = wdst.rearrange("p (cc x) -> p cc x", cc=NCC)
            for half in range(2):
                for cc in range(4 * half, 4 * half + 4):
                    add("tensor", (lambda cc=cc, half=half, sb_ap=sb_ap, ps=ps:
                                   lambda e: e.matmul(
                        ps[:, 128 * (cc - 4 * half):128 * (cc - 4 * half) + 128],
                        sb_ap[:, 128 * cc:128 * cc + 128], ident[:, :],
                        start=True, stop=True))(),
                        reads=[b_slab[par], b_ident], writes=[b_ps])
                add("vector", (lambda sl=sl, half=half, wdst3=wdst3, ps=ps:
                               lambda e: e.tensor_copy(
                    wdst3[:, 4 * half:4 * half + 4, 128 * sl:128 * sl + 128],
                    ps[:, 0:512].rearrange("p (cc x) -> p cc x", cc=4)))(),
                    reads=[b_ps], writes=[b_wdst] + extra)

        xn2T3 = xn2T.rearrange("p (cc t) -> p cc t", cc=NCC)
        add("vector", lambda e: e.memset(xn2T3[:, :, 0:1], 0.0),
            writes=[b_xn2T, b_xnT])

        emit_aslab_read(0, 0)
        _t3_pending = []

        for k in range(NT // 2):
            iA, iB = 2 * k, 2 * k + 1
            g = iA // 4
            if iA % 4 == 0 and g + 1 < 8:
                emit_aslab_read(g + 1, (g + 1) % 2)
            gpar = g % 2
            tiles = []
            for i, par, psY, b_psY in ((iA, 0, PA, b_PA), (iB, 1, PB, b_PB)):
                xin = xt[:, C * par:C * par + C]
                add("sync", (lambda i=i, xin=xin: lambda e: e.dma_start(
                    out=xin, in_=x_h[128 * i:128 * i + 128, :]))(),
                    writes=[b_xt[par]], dma=True)
                tiles.append((i, par, psY, b_psY, xin))
            for (i, par, psY, b_psY, xin) in tiles:
                for p in range(NPAIR):
                    for hf in range(2):
                        add("tensor", (lambda i=i, p=p, hf=hf, gpar=gpar,
                                       psY=psY: lambda e: e.matmul(
                            psY[:, 512 * hf:512 * hf + 512],
                            aslab3[:, gpar, p, 128 * (i % 4):128 * (i % 4) + 128],
                            wp_sb[:, C * p + 512 * hf:C * p + 512 * hf + 512],
                            start=(p == 0), stop=(p == NPAIR - 1)))(),
                            reads=[b_aslab[gpar], b_wp], writes=[b_psY])
            for (i, par, psY, b_psY, xin) in tiles:
                ysl = ysb[:, C * par:C * par + C]
                add("vector", (lambda psY=psY, ysl=ysl: lambda e:
                               e.tensor_tensor(ysl, psY[:, :], bp_b[:, :],
                                               ALU.add))(),
                    reads=[b_psY, b_bp], writes=[b_ysb[par]])
            ln_pair([(ysb[:, C * par:C * par + C], b_ysb[par],
                      y2[:, C * par:C * par + C], b_y2[par], par,
                      y2[:, C * par:C * par + C], b_y2[par])
                     for (i, par, psY, b_psY, xin) in tiles])
            for (i, par, psY, b_psY, xin) in tiles:
                y2l = y2[:, C * par:C * par + C]
                add("vector", (lambda xin=xin, y2l=y2l: lambda e:
                               e.tensor_tensor(xin, y2l, xin, ALU.add))(),
                    reads=[b_y2[par], b_xt[par]], writes=[b_xt[par]])
            for (i, par, psY, b_psY, xin) in tiles:
                add("gpsimd", (lambda i=i, xin=xin: lambda e: e.dma_start(
                    out=spillX_h[128 * i:128 * i + 128, :], in_=xin))(),
                    reads=[b_xt[par]], writes=[b_spillX[i]], dma=True)
                if DEBUG_OUTPUTS:
                    add("gpsimd", (lambda i=i, xin=xin: lambda e: e.dma_start(
                        out=dbg["xnew"][128 * i:128 * i + 128, :], in_=xin))(),
                        reads=[b_xt[par]], dma=True)
            while _t3_pending:
                _t3_pending.pop(0)()
            ln_pair([(xt[:, C * par:C * par + C], b_xt[par],
                      xn1t[:, C * par:C * par + C], b_xn1t[par], 2 + par,
                      y2[:, C * par:C * par + C], b_y2[par])
                     for (i, par, psY, b_psY, xin) in tiles])
            for (i, par, psY, b_psY, xin) in tiles:
                x2 = xn1t[:, C * par:C * par + C]

                def _t3(i=i, par=par, x2=x2):
                    psT, b_psT = (PC, b_PC) if par == 0 else (PD, b_PD)
                    for half in range(2):
                        for cc in range(4 * half, 4 * half + 4):
                            add("tensor", (lambda cc=cc, half=half, x2=x2,
                                           psT=psT: lambda e: e.matmul(
                                psT[:, 128 * (cc - 4 * half):
                                    128 * (cc - 4 * half) + 128],
                                x2[:, 128 * cc:128 * cc + 128], ident[:, :],
                                start=True, stop=True))(),
                                reads=[b_xn1t[par], b_ident], writes=[b_psT])
                        add("vector", (lambda i=i, half=half, psT=psT: lambda e:
                                       e.tensor_copy(
                            xn2T3[:, 4 * half:4 * half + 4,
                                  1 + 128 * i:1 + 128 * i + 128],
                            psT[:, 0:512].rearrange("p (cc x) -> p cc x",
                                                    cc=4)))(),
                            reads=[b_psT], writes=[b_xn2T])
                _t3_pending.append(_t3)
            if iA < 24:
                emit_ffn_wslab(iA)
            if iB < 24:
                emit_ffn_wslab(iB)
        for _fn in _t3_pending:
            _fn()
        _t3_pending = []

        xrT3 = xrT.rearrange("p (cc t) -> p cc t", cc=NCC)
        xkT3 = xkT.rearrange("p (cc t) -> p cc t", cc=NCC)
        kact3 = kact.rearrange("p (dd t) -> p dd t", dd=NCC)
        wrT3 = wrT.rearrange("p (cc x) -> p cc x", cc=NCC)
        wkT3 = wkT.rearrange("p (cc x) -> p cc x", cc=NCC)
        wvT3 = wvT.rearrange("p (dd x) -> p dd x", dd=NCC)

        def emit_mix(ch):
            t0 = 512 * ch
            for cc in range(NCC):
                prev = xn2T3[:, cc, t0:t0 + 512]
                cur = xn2T3[:, cc, t0 + 1:t0 + 513]
                add("vector", (lambda prev=prev, cur=cur: lambda e:
                               e.tensor_tensor(dscr[:, :], cur, prev,
                                               ALU.subtract))(),
                    reads=[b_xn2T], writes=[b_dscr, b_wvg])
                add("vector", (lambda cc=cc, prev=prev: lambda e:
                               e.scalar_tensor_tensor(
                    xrT3[:, cc, :], dscr[:, :], mu_sb[:, cc:cc + 1], prev,
                    ALU.mult, ALU.add))(),
                    reads=[b_dscr, b_mu, b_xn2T], writes=[b_xrT, b_wp])
                add("vector", (lambda cc=cc, prev=prev: lambda e:
                               e.scalar_tensor_tensor(
                    xkT3[:, cc, :], dscr[:, :], mu_sb[:, 8 + cc:8 + cc + 1], prev,
                    ALU.mult, ALU.add))(),
                    reads=[b_dscr, b_mu, b_xn2T], writes=[b_xkT, b_wp])

        def emit_rsig(ch):
            for sub in range(4):
                tpar = sub % 2
                psR, b_psR = (PA, b_PA) if tpar == 0 else (PB, b_PB)
                for hf in range(2):
                    for cc in range(NCC):
                        add("tensor", (lambda sub=sub, hf=hf, cc=cc, psR=psR:
                                       lambda e: e.matmul(
                            psR[:, 512 * hf:512 * hf + 512],
                            xrT3[:, cc, 128 * sub:128 * sub + 128],
                            wrT3[:, cc, 512 * hf:512 * hf + 512],
                            start=(cc == 0), stop=(cc == NCC - 1)))(),
                            reads=[b_xrT, b_wrT], writes=[b_psR])
                add("scalar", (lambda psR=psR: lambda e: e.activation(
                    sqd[:, :], psR[:, :], AF.Exp, bias=0.0, scale=-1.0))(),
                    reads=[b_psR], writes=[b_sqd])
                add("vector", lambda e: e.tensor_scalar(
                    sqd[:, :], sqd[:, :], 1.0, None, ALU.add),
                    reads=[b_sqd], writes=[b_sqd])
                add("vector", lambda e: e.reciprocal(sqd[:, :], sqd[:, :]),
                    reads=[b_sqd], writes=[b_sqd])
                add("scalar", (lambda sub=sub: lambda e: e.copy(
                    r_sb[:, C * sub:C * sub + C], sqd[:, :]))(),
                    reads=[b_sqd], writes=[b_rsb, b_aslab[0], b_aslab[1]])

        def emit_kpath(ch):
            for dd in range(NCC):
                ps, b_ps = (PC, b_PC) if dd % 2 == 0 else (PD, b_PD)
                for cc in range(NCC):
                    add("tensor", (lambda dd=dd, cc=cc, ps=ps: lambda e: e.matmul(
                        ps[:, 0:512], wkT3[:, cc, 128 * dd:128 * dd + 128],
                        xkT3[:, cc, :], start=(cc == 0), stop=(cc == NCC - 1)))(),
                        reads=[b_wkTb, b_xkT], writes=[b_ps])
                add("vector", (lambda ps=ps: lambda e: e.tensor_relu(
                    ps[:, 0:512], ps[:, 0:512]))(),
                    reads=[b_ps], writes=[b_ps])
                add("scalar", (lambda dd=dd, ps=ps: lambda e: e.activation(
                    kact3[:, dd, :], ps[:, 0:512], AF.Square, bias=0.0,
                    scale=1.0))(),
                    reads=[b_ps], writes=[b_kact, b_aslab[0], b_aslab[1]])

        def emit_vtail(ch):
            for sp in range(2):
                tl = []
                for sub in (2 * sp, 2 * sp + 1):
                    tpar = sub % 2
                    i = 4 * ch + sub
                    xin = xt[:, C * tpar:C * tpar + C]
                    add("gpsimd", (lambda i=i, xin=xin: lambda e: e.dma_start(
                        out=xin, in_=spillX_h[128 * i:128 * i + 128, :]))(),
                        reads=[b_spillX[i]], writes=[b_xt[tpar]], dma=True)
                    tl.append((sub, tpar, i, xin))
                for (sub, tpar, i, xin) in tl:
                    for hf in range(2):
                        psV, b_psV = (PE_, b_PE) if hf == 0 else (PF, b_PF)
                        for dd in range(NCC):
                            add("tensor", (lambda sub=sub, hf=hf, dd=dd, psV=psV:
                                           lambda e: e.matmul(
                                psV[:, 0:512],
                                kact3[:, dd, 128 * sub:128 * sub + 128],
                                wvT3[:, dd, 512 * hf:512 * hf + 512],
                                start=(dd == 0), stop=(dd == NCC - 1)))(),
                                reads=[b_kact, b_wvTb], writes=[b_psV])
                        add("vector", (lambda sub=sub, hf=hf, psV=psV, tpar=tpar:
                                       lambda e: e.tensor_tensor(
                            ysb[:, C * tpar + 512 * hf:C * tpar + 512 * hf + 512],
                            psV[:, 0:512],
                            r_sb[:, C * sub + 512 * hf:C * sub + 512 * hf + 512],
                            ALU.mult))(),
                            reads=[b_psV, b_rsb], writes=[b_ysb[tpar]])
                ln_pair([(ysb[:, C * tpar:C * tpar + C], b_ysb[tpar],
                          y2[:, C * tpar:C * tpar + C], b_y2[tpar], tpar,
                          y2[:, C * tpar:C * tpar + C], b_y2[tpar])
                         for (sub, tpar, i, xin) in tl])
                for (sub, tpar, i, xin) in tl:
                    y2l = y2[:, C * tpar:C * tpar + C]
                    add("vector", (lambda xin=xin, y2l=y2l: lambda e:
                                   e.tensor_tensor(
                        xin, y2l, xin, ALU.add))(),
                        reads=[b_y2[tpar], b_xt[tpar]], writes=[b_xt[tpar]])
                    add("sync", (lambda i=i, xin=xin: lambda e: e.dma_start(
                        out=out_h[128 * i:128 * i + 128, :], in_=xin))(),
                        reads=[b_xt[tpar]], writes=[b_out[i]], dma=True)

        emit_mix(0)
        for ch in range(8):
            emit_rsig(ch)
            emit_kpath(ch)
            if ch + 1 < 8:
                emit_mix(ch + 1)
            emit_vtail(ch)

        add("sync", lambda e: e.nop(), deps=[b_out[NT - 1].w])

        with nc.Block() as block:
            prog.emit(block, sems)

    return nc


_NC_CACHE = None


def kernel(L0_attn_x, L0_attn_Wq, L0_attn_Wk, L0_attn_Wv, L0_attn_Wp,
           L0_attn_bp, L0_attn_rel_table, L0_ffn_W_r, L0_ffn_W_k, L0_ffn_W_v,
           L0_ffn_mu_r, L0_ffn_mu_k):
    global _NC_CACHE
    cc = np.ascontiguousarray
    x = cc(np.asarray(L0_attn_x, dtype=np.float32))
    ins = {
        "wq": cc(np.asarray(L0_attn_Wq, dtype=np.float32)),
        "wk": cc(np.asarray(L0_attn_Wk, dtype=np.float32)),
        "wv": cc(np.asarray(L0_attn_Wv, dtype=np.float32)),
        "wp": cc(np.asarray(L0_attn_Wp, dtype=np.float32)),
        "bp": cc(np.asarray(L0_attn_bp, dtype=np.float32)),
        "rel": cc(np.asarray(L0_attn_rel_table, dtype=np.float32)).reshape(-1),
        "wr": cc(np.asarray(L0_ffn_W_r, dtype=np.float32)),
        "wkf": cc(np.asarray(L0_ffn_W_k, dtype=np.float32)),
        "wvf": cc(np.asarray(L0_ffn_W_v, dtype=np.float32)),
        "mur": cc(np.asarray(L0_ffn_mu_r, dtype=np.float32)),
        "muk": cc(np.asarray(L0_ffn_mu_k, dtype=np.float32)),
    }
    if _NC_CACHE is None:
        _NC_CACHE = build_program()
    nc = _NC_CACHE
    in_maps = [dict(ins, x=cc(x[b])) for b in range(8)]
    res = run_bass_kernel_spmd(nc, in_maps, core_ids=list(range(8)))
    return np.stack([res.results[b]["out"] for b in range(8)], axis=0).astype(np.float32)


# version 18 (final)
# speedup vs baseline: 1.1425x; 1.0037x over previous
from contextlib import ExitStack

import numpy as np

import concourse.bass as bass
import concourse.mybir as mybir
from concourse.bass_utils import run_bass_kernel_spmd

F32 = mybir.dt.float32
BF16 = mybir.dt.bfloat16
AF = mybir.ActivationFunctionType
ALU = mybir.AluOpType
AX = mybir.AxisListType

T = 4096
C = 1024
H = 16
NT = T // 128
NCC = C // 128
NPAIR = H // 2
EPS = 1e-5
NEG_BIG = -1.0e33

ENGINES = ("sync", "gpsimd", "scalar", "vector", "tensor")

DEBUG_OUTPUTS = False


class _Op:
    __slots__ = ("eng", "fn", "deps", "step", "idx", "val", "awaited")

    def __init__(self, eng, fn, deps, step, idx):
        self.eng = eng
        self.fn = fn
        self.deps = deps
        self.step = step
        self.idx = idx
        self.val = None
        self.awaited = False


class Buf:
    __slots__ = ("name", "w", "rs")

    def __init__(self, name):
        self.name = name
        self.w = None
        self.rs = []


class Prog:
    def __init__(self):
        self.ops = []
        self._last_dma = {}

    def add(self, eng, fn, reads=(), writes=(), deps=(), dma=False):
        d = []
        if dma:
            prev = self._last_dma.get(eng)
            if prev is not None:
                d.append(prev)
        for b in reads:
            if b.w is not None:
                d.append(b.w)
        for b in writes:
            d.extend(b.rs)
            if b.w is not None:
                d.append(b.w)
        d.extend(x for x in deps if x is not None)
        seen = set()
        dd = []
        for x in d:
            if id(x) not in seen:
                seen.add(id(x))
                dd.append(x)
        op = _Op(eng, fn, dd, 16 if dma else 1, len(self.ops))
        if dma:
            self._last_dma[eng] = op
        for x in dd:
            assert x.idx < op.idx
        for b in reads:
            b.rs.append(op)
        for b in writes:
            b.w = op
            b.rs = []
        self.ops.append(op)
        return op

    def emit(self, block, sems):
        for op in self.ops:
            if op.step == 16:
                op.awaited = True
            for dep in op.deps:
                if dep.eng != op.eng or dep.step == 16:
                    dep.awaited = True
        cnt = {e: 0 for e in ENGINES}
        for op in self.ops:
            if op.awaited:
                cnt[op.eng] += op.step
                op.val = cnt[op.eng]

        def make_runner(eng_name):
            def run(eng):
                waited = {}
                for op in self.ops:
                    if op.eng != eng_name:
                        continue
                    need = {}
                    for dep in op.deps:
                        if dep.eng == eng_name and dep.step != 16:
                            continue
                        if dep.val is None:
                            continue
                        s = dep.eng
                        if need.get(s, 0) < dep.val:
                            need[s] = dep.val
                    for s, v in need.items():
                        if waited.get(s, 0) < v:
                            eng.wait_ge(sems[s], v)
                            waited[s] = v
                    ins = op.fn(eng)
                    if op.awaited:
                        assert ins is not None, "awaited op returned None"
                        ins.then_inc(sems[op.eng], op.step)
            return run

        block.sync(make_runner("sync"))
        block.gpsimd(make_runner("gpsimd"))
        block.scalar(make_runner("scalar"))
        block.vector(make_runner("vector"))
        block.tensor(make_runner("tensor"))


def _ceil(a, b):
    return -(-a // b)


def build_program():
    nc = bass.Bass("TRN2", target_bir_lowering=False)

    x_h = nc.dram_tensor("x", [T, C], F32, kind="ExternalInput")
    wq_h = nc.dram_tensor("wq", [H, C, 64], F32, kind="ExternalInput")
    wk_h = nc.dram_tensor("wk", [H, C, 64], F32, kind="ExternalInput")
    wv_h = nc.dram_tensor("wv", [H, C, 64], F32, kind="ExternalInput")
    wp_h = nc.dram_tensor("wp", [C, C], F32, kind="ExternalInput")
    bp_h = nc.dram_tensor("bp", [C], F32, kind="ExternalInput")
    rel_h = nc.dram_tensor("rel", [33 * H], F32, kind="ExternalInput")
    wr_h = nc.dram_tensor("wr", [C, C], F32, kind="ExternalInput")
    wkf_h = nc.dram_tensor("wkf", [C, C], F32, kind="ExternalInput")
    wvf_h = nc.dram_tensor("wvf", [C, C], F32, kind="ExternalInput")
    mur_h = nc.dram_tensor("mur", [C], F32, kind="ExternalInput")
    muk_h = nc.dram_tensor("muk", [C], F32, kind="ExternalInput")

    spillA_h = nc.dram_tensor("spillA", [NPAIR, 128, T], BF16, kind="Internal")
    spillX_h = nc.dram_tensor("spillX", [T, C], F32, kind="Internal")
    out_h = nc.dram_tensor("out", [T, C], F32, kind="ExternalOutput")
    dbg = {}
    if DEBUG_OUTPUTS:
        dbg["xnT"] = nc.dram_tensor("dbg_xnT", [128, NCC * T], BF16, kind="ExternalOutput")
        dbg["attnT"] = nc.dram_tensor("dbg_attnT", [NPAIR, 128, T], BF16, kind="ExternalOutput")
        dbg["xnew"] = nc.dram_tensor("dbg_xnew", [T, C], F32, kind="ExternalOutput")
        dbg["st"] = nc.dram_tensor("dbg_st", [128, 64], F32, kind="ExternalOutput")
        dbg["x0"] = nc.dram_tensor("dbg_x0", [128, C], F32, kind="ExternalOutput")
        dbg["pg"] = nc.dram_tensor("dbg_pg", [128, 4 * T], BF16, kind="ExternalOutput")
        dbg["dg"] = nc.dram_tensor("dbg_dg", [128, 512], BF16, kind="ExternalOutput")
        dbg["nm"] = nc.dram_tensor("dbg_nm", [128, 64], F32, kind="ExternalOutput")
        dbg["bd"] = nc.dram_tensor("dbg_bd", [128, 256], BF16, kind="ExternalOutput")
        dbg["qa"] = nc.dram_tensor("dbg_qa", [128, T], BF16, kind="ExternalOutput")
        dbg["qp"] = nc.dram_tensor("dbg_qp", [128, T], BF16, kind="ExternalOutput")
        dbg["kp"] = nc.dram_tensor("dbg_kp", [128, T], BF16, kind="ExternalOutput")
        dbg["kr"] = nc.dram_tensor("dbg_kr", [128, 1], BF16, kind="ExternalOutput")

    prog = Prog()
    add = prog.add

    with ExitStack() as st:
        def sb(name, shape, dt):
            return st.enter_context(nc.sbuf_tensor(name, shape, dt))

        ident = sb("ident", [128, 128], BF16)
        ones_f = sb("ones_f", [128, 128], F32)
        zeros_f = sb("zeros_f", [128, 128], F32)
        causal_f = sb("causal_f", [128, 128], F32)
        rel8 = sb("rel8", [128, 528], F32)
        bdiag = sb("bdiag", [128, 16 * 128], BF16)
        bsub = sb("bsub", [128, 16 * 128], BF16)
        diag4 = sb("diag4", [128, 4 * 128], BF16)
        bp_b = sb("bp_b", [128, C], F32)
        mu_sb = sb("mu_sb", [128, 32], F32)
        negb16 = sb("negb16", [128, 16], F32)
        stats = sb("stats", [128, 64], F32)
        negm = sb("negm", [128, 2 * NT], F32)
        lparts = sb("lparts", [128, 24], F32)
        gtmp = sb("gtmp", [128, 2], F32)
        krb = sb("krb", [128, 1], BF16)
        epsc = sb("epsc", [128, 1], F32)
        gdiff = sb("gdiff", [128, 272], F32)
        gdiff2 = sb("gdiff2", [128, 272], F32)

        xt = sb("xt", [128, 2 * C], F32)
        ysb = sb("ysb", [128, 2 * C], F32)
        y2 = sb("y2", [128, 2 * C], F32)
        sqd = sb("sqd", [128, C], F32)

        U1 = sb("U1", [128, 32832], BF16)
        U2 = sb("U2", [128, 46592], BF16)

        def u1(off, n):
            return U1[:, off:off + n]

        def u2(off, n):
            return U2[:, off:off + n]

        qpair = u2(0, T)
        kpair = u2(T, T)
        vgrp = u2(2 * T, 2 * T + 128)
        pgrpT = u2(16512, 16384)
        masks = u2(16512, 33 * 128)
        qabs = u2(20608, T)
        attnT = u2(32896, T)
        wq_g = u2(36992, 2048)
        wk_g = u2(39040, 2048)
        wv_g = u2(41088, 2048)
        xn1t = u2(43136, 2048)
        tmp_b = [u2(43136, 512), u2(43648, 512)]
        lrecip = u2(44160, 512)
        wp_sb = u2(0, 8192)
        wrT = u2(8192, 8192)
        wkT = u2(16384, 8192)
        wvT = u2(24576, 8192)
        aslab = u2(32768, 8192)
        slab = u2(40960, 2048)
        dscr = u2(45696, 512)
        xrT = u2(0, T)
        xkT = u2(T, T)
        kact = u2(32768, T)
        r_sb = u2(36864, T)

        xnT = u1(0, NCC * T)
        xn2T = u1(0, NCC * (T + 8))

        PA = st.enter_context(nc.psum_tensor("PA", [128, 1024], F32))
        PB = st.enter_context(nc.psum_tensor("PB", [128, 1024], F32))
        PC = st.enter_context(nc.psum_tensor("PC", [128, 512], F32))
        PD = st.enter_context(nc.psum_tensor("PD", [128, 512], F32))
        PE_ = st.enter_context(nc.psum_tensor("PE", [128, 512], F32))
        PF = st.enter_context(nc.psum_tensor("PF", [128, 512], F32))

        sems = {e: st.enter_context(nc.semaphore("sem_" + e)) for e in ENGINES}

        b_ident = Buf("ident"); b_ones = Buf("ones"); b_zeros = Buf("zeros")
        b_causal = Buf("causal"); b_rel8 = Buf("rel8")
        b_bdiag = [Buf(f"bd{h}") for h in range(H)]
        b_bsub = [Buf(f"bs{h}") for h in range(H)]
        b_masks = Buf("masks"); b_bp = Buf("bp"); b_mu = Buf("mu")
        b_negb = Buf("negb")
        b_stats = [Buf(f"st{i}") for i in range(4)]
        b_negm = [Buf("negm0"), Buf("negm1")]
        b_lparts = Buf("lparts"); b_gtmp = Buf("gtmp")
        b_diag4 = [Buf(f"dg{i}") for i in range(4)]
        b_xnT = Buf("xnT"); b_qabs = Buf("qabs")
        b_pgT = [Buf(f"pgT{j}") for j in range(NT)]
        b_negmg = Buf("negmg"); b_linv = Buf("linv"); b_lrec = Buf("lrec")
        b_qpair = Buf("qpair"); b_kpair = Buf("kpair"); b_vgrp = Buf("vgrp")
        b_attnT = Buf("attnT")
        b_wqg = Buf("wqg"); b_wkg = Buf("wkg"); b_wvg = Buf("wvg")
        b_xn1t = [Buf("xn1t0"), Buf("xn1t1")]
        b_xn2T = Buf("xn2T"); b_wp = Buf("wp")
        b_wrT = Buf("wrT"); b_wkTb = Buf("wkT"); b_wvTb = Buf("wvT")
        b_slab = [Buf("slab0"), Buf("slab1")]
        b_aslab = [Buf("aslab0"), Buf("aslab1")]
        b_xrT = Buf("xrT"); b_xkT = Buf("xkT"); b_kact = Buf("kact")
        b_rsb = Buf("rsb"); b_dscr = Buf("dscr")
        b_xt = [Buf("xt0"), Buf("xt1")]
        b_ysb = [Buf("ysb0"), Buf("ysb1")]
        b_y2 = [Buf("y20"), Buf("y21")]; b_sqd = Buf("sqd")
        b_krmax = Buf("krmax"); b_eps = Buf("eps")
        b_gdiff = Buf("gdiff"); b_gdiff2 = Buf("gdiff2")
        b_PA = Buf("PA"); b_PB = Buf("PB"); b_PC = Buf("PC")
        b_PD = Buf("PD"); b_PE = Buf("PE"); b_PF = Buf("PF")
        b_spillA = [Buf(f"spA{p}") for p in range(NPAIR)]
        b_spillX = [Buf(f"spX{i}") for i in range(NT)]
        b_out = [Buf(f"out{i}") for i in range(NT)]

        add("vector", lambda e: e.memset(ones_f[:, :], 1.0), writes=[b_ones])
        add("vector", lambda e: e.memset(epsc[:, :], EPS), writes=[b_eps])
        add("vector", lambda e: e.memset(zeros_f[:, :], 0.0), writes=[b_zeros])
        add("vector", lambda e: e.memset(diag4[:, 0:128], 1.0), writes=[b_diag4[0]])
        add("gpsimd", lambda e: e.affine_select(
            ident[:, :], diag4[:, 0:128], pattern=[[1, 128]], base=0,
            channel_multiplier=-1, compare_op=ALU.is_equal, fill=0.0),
            reads=[b_diag4[0]], writes=[b_ident])
        add("vector", lambda e: e.memset(diag4[:, 128:256], 1.0),
            writes=[b_diag4[1]])
        add("vector", lambda e: e.memset(ysb[:, 0:128], NEG_BIG),
            writes=[b_ysb[0]])
        add("gpsimd", lambda e: e.affine_select(
            causal_f[:, :], ysb[:, 0:128], pattern=[[1, 128]], base=-1,
            channel_multiplier=-1, compare_op=ALU.is_ge, fill=0.0),
            reads=[b_ysb[0]], writes=[b_causal])

        def _mask_op(slot, base):
            def f(e):
                return e.affine_select(
                    masks[:, 128 * slot:128 * slot + 128], diag4[:, 0:128],
                    pattern=[[1, 128]], base=base, channel_multiplier=-1,
                    compare_op=ALU.is_equal, fill=0.0)
            return f
        for d in range(17):
            add("gpsimd", _mask_op(d, d), reads=[b_diag4[0]],
                writes=[b_masks] + b_pgT[0:9])
        for d in range(1, 17):
            add("gpsimd", _mask_op(17 + d - 1, d - 128), reads=[b_diag4[0]],
                writes=[b_masks] + b_pgT[0:9])

        add("sync", lambda e: e.dma_start(
            out=rel8[:, :], in_=rel_h.ap().rearrange("(one c) -> one c", one=1)
            .partition_broadcast(128)), writes=[b_rel8], dma=True)
        add("sync", lambda e: e.dma_start(
            out=bp_b[:, :], in_=bp_h.ap().rearrange("(one c) -> one c", one=1)
            .partition_broadcast(128)), writes=[b_bp], dma=True)
        add("gpsimd", lambda e: e.dma_start(
            out=mu_sb[:, 0:8], in_=mur_h.ap().rearrange("(cc p) -> p cc", p=128),
            allow_slow_non_contiguous=True), writes=[b_mu], dma=True)
        add("gpsimd", lambda e: e.dma_start(
            out=mu_sb[:, 8:16], in_=muk_h.ap().rearrange("(cc p) -> p cc", p=128),
            allow_slow_non_contiguous=True), writes=[b_mu], dma=True)
        add("vector", lambda e: e.tensor_tensor(
            mu_sb[:, 16:32], ones_f[:, 0:16], mu_sb[:, 0:16], ALU.subtract),
            reads=[b_mu, b_ones], writes=[b_mu])

        add("vector", lambda e: e.tensor_scalar(
            rel8[:, :], rel8[:, :], 8.0, None, ALU.mult),
            reads=[b_rel8], writes=[b_rel8])
        add("vector", lambda e: e.reduce_max(
            stats[:, 63:64], rel8[:, :], axis=AX.X, apply_absolute_value=True),
            reads=[b_rel8], writes=[b_stats[0]])
        add("scalar", lambda e: e.mul(
            stats[:, 62:63], stats[:, 63:64], 0.25),
            reads=[b_stats[0]], writes=[b_stats[0]])
        add("vector", lambda e: e.tensor_scalar(
            negb16[:, :], ones_f[:, 0:16], stats[:, 62:63], None, ALU.mult),
            reads=[b_ones, b_stats[0]], writes=[b_negb])

        for d in range(17):
            add("vector", (lambda d=d: lambda e: e.tensor_tensor(
                gdiff[:, 16 * d:16 * d + 16], rel8[:, (16 - d) * H:(16 - d) * H + 16],
                rel8[:, 0:16], ALU.subtract))(),
                reads=[b_rel8], writes=[b_gdiff])
        add("scalar", lambda e: e.copy(gdiff2[:, :], gdiff[:, :]),
            reads=[b_gdiff], writes=[b_gdiff2])
        for h in range(H):
            add("vector", (lambda h=h: lambda e: e.tensor_copy(
                bdiag[:, 128 * h:128 * h + 128], causal_f[:, :]))(),
                reads=[b_causal], writes=[b_bdiag[h]])
            add("vector", (lambda h=h: lambda e: e.memset(
                bsub[:, 128 * h:128 * h + 128], 0.0))(),
                writes=[b_bsub[h]])
            for d in range(17):
                add("vector", (lambda h=h, d=d: lambda e: e.scalar_tensor_tensor(
                    bdiag[:, 128 * h:128 * h + 128],
                    masks[:, 128 * d:128 * d + 128], gdiff2[:, 16 * d + h:16 * d + h + 1],
                    bdiag[:, 128 * h:128 * h + 128], ALU.mult, ALU.add))(),
                    reads=[b_masks, b_gdiff2, b_bdiag[h]] + b_pgT[0:9],
                    writes=[b_bdiag[h]])
                if d >= 1:
                    add("vector", (lambda h=h, d=d: lambda e: e.scalar_tensor_tensor(
                        bsub[:, 128 * h:128 * h + 128],
                        masks[:, 128 * (17 + d - 1):128 * (17 + d - 1) + 128],
                        gdiff2[:, 16 * d + h:16 * d + h + 1],
                        bsub[:, 128 * h:128 * h + 128], ALU.mult, ALU.add))(),
                        reads=[b_masks, b_gdiff2, b_bsub[h]] + b_pgT[0:9],
                        writes=[b_bsub[h]])

        def ln_stats_apply(xin_ap, b_xin, out_ap, b_out_buf, sidx,
                           dump_ap=None, b_dump=None):
            s0 = 12 * sidx
            add("vector", lambda e: e.reduce_sum(
                stats[:, s0:s0 + 1], xin_ap, axis=AX.X),
                reads=[b_xin], writes=[b_stats[sidx]])
            add("vector", lambda e: e.scalar_tensor_tensor(
                dump_ap, xin_ap, 1.0, xin_ap, ALU.mult, ALU.mult,
                accum_out=stats[:, s0 + 1:s0 + 2]),
                reads=[b_xin], writes=[b_dump, b_stats[sidx]])
            add("scalar", lambda e: e.mul(
                stats[:, s0 + 2:s0 + 3], stats[:, s0:s0 + 1], 1.0 / C),
                reads=[b_stats[sidx]], writes=[b_stats[sidx]])
            add("scalar", lambda e: e.mul(
                stats[:, s0 + 3:s0 + 4], stats[:, s0:s0 + 1], 1.0 / C),
                reads=[b_stats[sidx]], writes=[b_stats[sidx]])
            add("vector", lambda e: e.tensor_scalar(
                stats[:, s0 + 4:s0 + 5], stats[:, s0 + 2:s0 + 3],
                stats[:, s0 + 3:s0 + 4], None, ALU.mult),
                reads=[b_stats[sidx]], writes=[b_stats[sidx]])
            add("gpsimd", lambda e: e.tensor_copy(
                stats[:, s0 + 5:s0 + 6], stats[:, s0 + 4:s0 + 5]),
                reads=[b_stats[sidx]], writes=[b_stats[sidx]])
            add("vector", lambda e: e.scalar_tensor_tensor(
                stats[:, s0 + 6:s0 + 7], stats[:, s0 + 1:s0 + 2], 1.0 / C,
                stats[:, s0 + 5:s0 + 6], ALU.mult, ALU.subtract),
                reads=[b_stats[sidx]], writes=[b_stats[sidx]])
            add("scalar", lambda e: e.activation(
                stats[:, s0 + 7:s0 + 8], stats[:, s0 + 6:s0 + 7], AF.Ln,
                bias=epsc[:, :], scale=1.0),
                reads=[b_stats[sidx], b_eps], writes=[b_stats[sidx]])
            add("vector", lambda e: e.tensor_copy(
                stats[:, s0 + 8:s0 + 9], stats[:, s0 + 7:s0 + 8]),
                reads=[b_stats[sidx]], writes=[b_stats[sidx]])
            add("scalar", lambda e: e.activation(
                stats[:, s0 + 9:s0 + 10], stats[:, s0 + 8:s0 + 9], AF.Exp,
                bias=0.0, scale=-0.5),
                reads=[b_stats[sidx]], writes=[b_stats[sidx]])
            add("vector", lambda e: e.tensor_scalar(
                out_ap, xin_ap, stats[:, s0 + 2:s0 + 3], stats[:, s0 + 9:s0 + 10],
                ALU.subtract, ALU.mult),
                reads=[b_xin, b_stats[sidx]], writes=[b_out_buf])

        def ln_pair(specs):
            for (xin_ap, b_xin, _o, _bo, sidx, dump_ap, b_dump) in specs:
                s0 = 12 * sidx
                add("vector", (lambda xin_ap=xin_ap, s0=s0: lambda e:
                               e.reduce_sum(stats[:, s0:s0 + 1], xin_ap,
                                            axis=AX.X))(),
                    reads=[b_xin], writes=[b_stats[sidx]])
            for (xin_ap, b_xin, _o, _bo, sidx, dump_ap, b_dump) in specs:
                s0 = 12 * sidx
                add("vector", (lambda xin_ap=xin_ap, s0=s0, dump_ap=dump_ap:
                               lambda e: e.scalar_tensor_tensor(
                    dump_ap, xin_ap, 1.0, xin_ap, ALU.mult, ALU.mult,
                    accum_out=stats[:, s0 + 1:s0 + 2]))(),
                    reads=[b_xin], writes=[b_dump, b_stats[sidx]])
            for (_x, _bx, _o, _bo, sidx, _d, _bd) in specs:
                s0 = 12 * sidx
                add("vector", (lambda s0=s0: lambda e: e.tensor_scalar(
                    stats[:, s0 + 2:s0 + 3], stats[:, s0:s0 + 1], 1.0 / C,
                    None, ALU.mult))(),
                    reads=[b_stats[sidx]], writes=[b_stats[sidx]])
            for (_x, _bx, _o, _bo, sidx, _d, _bd) in specs:
                s0 = 12 * sidx
                add("vector", (lambda s0=s0: lambda e: e.scalar_tensor_tensor(
                    stats[:, s0 + 3:s0 + 4], stats[:, s0:s0 + 1], 1.0 / C,
                    stats[:, s0 + 2:s0 + 3], ALU.mult, ALU.mult))(),
                    reads=[b_stats[sidx]], writes=[b_stats[sidx]])
            for (_x, _bx, _o, _bo, sidx, _d, _bd) in specs:
                s0 = 12 * sidx
                add("gpsimd", (lambda s0=s0: lambda e: e.tensor_copy(
                    stats[:, s0 + 7:s0 + 8], stats[:, s0 + 1:s0 + 2]))(),
                    reads=[b_stats[sidx]], writes=[b_stats[sidx]])
            for (_x, _bx, _o, _bo, sidx, _d, _bd) in specs:
                s0 = 12 * sidx
                add("vector", (lambda s0=s0: lambda e: e.scalar_tensor_tensor(
                    stats[:, s0 + 4:s0 + 5], stats[:, s0 + 7:s0 + 8], 1.0 / C,
                    stats[:, s0 + 3:s0 + 4], ALU.mult, ALU.subtract))(),
                    reads=[b_stats[sidx]], writes=[b_stats[sidx]])
            for (_x, _bx, _o, _bo, sidx, _d, _bd) in specs:
                s0 = 12 * sidx
                add("scalar", (lambda s0=s0: lambda e: e.activation(
                    stats[:, s0 + 5:s0 + 6], stats[:, s0 + 4:s0 + 5], AF.Ln,
                    bias=epsc[:, :], scale=1.0))(),
                    reads=[b_stats[sidx], b_eps], writes=[b_stats[sidx]])
            for (_x, _bx, _o, _bo, sidx, _d, _bd) in specs:
                s0 = 12 * sidx
                add("scalar", (lambda s0=s0: lambda e: e.activation(
                    stats[:, s0 + 6:s0 + 7], stats[:, s0 + 5:s0 + 6], AF.Exp,
                    bias=0.0, scale=-0.5))(),
                    reads=[b_stats[sidx]], writes=[b_stats[sidx]])
            for (xin_ap, b_xin, out_ap, b_out_buf, sidx, _d, _bd) in specs:
                s0 = 12 * sidx
                add("vector", (lambda xin_ap=xin_ap, out_ap=out_ap, s0=s0:
                               lambda e: e.tensor_scalar(
                    out_ap, xin_ap, stats[:, s0 + 2:s0 + 3],
                    stats[:, s0 + 6:s0 + 7], ALU.subtract, ALU.mult))(),
                    reads=[b_xin, b_stats[sidx]], writes=[b_out_buf])


        xnT3 = xnT.rearrange("p (cc t) -> p cc t", cc=NCC)
        PSP = {0: (PA, b_PA), 1: (PB, b_PB)}

        ln1_out = [xn1t[:, 0:C], xn1t[:, C:2 * C], qpair[:, 0:C],
                   qpair[:, C:2 * C]]
        ln1_ob = [b_xn1t[0], b_xn1t[1], b_qpair, b_qpair]
        for g4 in range(NT // 4):
            tl = []
            for s4 in range(4):
                i = 4 * g4 + s4
                xin = (xt if s4 < 2 else ysb)[:, C * (s4 % 2):
                                              C * (s4 % 2) + C]
                b_xin = (b_xt[0], b_xt[1], b_ysb[0], b_ysb[1])[s4]
                add("sync", (lambda i=i, xin=xin: lambda e: e.dma_start(
                    out=xin, in_=x_h[128 * i:128 * i + 128, :]))(),
                    writes=[b_xin], dma=True)
                tl.append((i, s4, xin, b_xin))
            ln_pair([(xin, b_xin, ln1_out[s4], ln1_ob[s4], s4,
                      y2[:, 0:C], b_y2[0])
                     for (i, s4, xin, b_xin) in tl])
            for (i, s4, xin, b_xin) in tl:
                ps, b_ps = PSP[s4 % 2]
                x1 = ln1_out[s4]
                for cc in range(NCC):
                    add("tensor", (lambda cc=cc, x1=x1, ps=ps: lambda e: e.matmul(
                        ps[:, 128 * cc:128 * cc + 128],
                        x1[:, 128 * cc:128 * cc + 128], ident[:, :],
                        start=True, stop=True))(),
                        reads=[ln1_ob[s4], b_ident], writes=[b_ps])
                add("vector", (lambda i=i, ps=ps: lambda e: e.tensor_copy(
                    xnT3[:, :, 128 * i:128 * i + 128],
                    ps[:, :].rearrange("p (cc t) -> p cc t", cc=NCC)))(),
                    reads=[b_ps], writes=[b_xnT])

        if DEBUG_OUTPUTS:
            add("sync", lambda e: e.dma_start(out=dbg["xnT"].ap(), in_=xnT),
                reads=[b_xnT], dma=True)

        wq3 = wq_g.rearrange("p (pp cc x) -> p pp cc x", pp=2, cc=NCC)
        wk3 = wk_g.rearrange("p (pp cc x) -> p pp cc x", pp=2, cc=NCC)
        wv3 = wv_g.rearrange("p (cc x) -> p cc x", cc=NCC)
        vg4 = vgrp.rearrange("p (t m e) -> p t m e", m=4, e=65)
        add("vector", lambda e: e.memset(vg4[:, :, :, 64:65], 1.0),
            writes=[b_vgrp])

        def wload(dst3_sel, src_h, head):
            def f(e):
                return e.dma_start(
                    out=dst3_sel,
                    in_=src_h.ap()[head].rearrange("(cc p) d -> p cc d", p=128))
            return f

        for g in range(4):
            pairs = (2 * g, 2 * g + 1)
            for pp, p in enumerate(pairs):
                for h01 in range(2):
                    h = 2 * p + h01
                    add("gpsimd", wload(wq3[:, pp, :, 64 * h01:64 * h01 + 64], wq_h, h),
                        writes=[b_wqg], dma=True)
                    add("gpsimd", wload(wk3[:, pp, :, 64 * h01:64 * h01 + 64], wk_h, h),
                        writes=[b_wkg], dma=True)
            for m in range(4):
                add("gpsimd", wload(wv3[:, :, 64 * m:64 * m + 64], wv_h, 4 * g + m),
                    writes=[b_wvg], dma=True)

            for ti in range(NT):
                ps, b_ps = (PC, b_PC) if ti % 2 == 0 else (PD, b_PD)
                for cc in range(NCC):
                    add("tensor", (lambda ti=ti, cc=cc, ps=ps: lambda e: e.matmul(
                        ps[:, 0:256],
                        xnT[:, T * cc + 128 * ti: T * cc + 128 * ti + 128],
                        wv_g[:, 256 * cc:256 * cc + 256],
                        start=(cc == 0), stop=(cc == NCC - 1)))(),
                        reads=[b_xnT, b_wvg], writes=[b_ps])
                add("vector", (lambda ti=ti, ps=ps: lambda e: e.tensor_copy(
                    vg4[:, ti, :, 0:64],
                    ps[:, 0:256].rearrange("p (m e) -> p m e", m=4)))(),
                    reads=[b_ps], writes=[b_vgrp])

            for pp, p in enumerate(pairs):
                for which, dst, b_dst, wsb3, b_w in (
                        (0, qpair, b_qpair, wq3, b_wqg),
                        (1, kpair, b_kpair, wk3, b_wkg)):
                    for tc in range(8):
                        ps, b_ps = (PA, b_PA) if tc % 2 == 0 else (PB, b_PB)
                        for cc in range(NCC):
                            add("tensor", (lambda pp=pp, tc=tc, cc=cc, ps=ps,
                                           wsb3=wsb3: lambda e: e.matmul(
                                ps[:, 0:512],
                                wsb3[:, pp, cc, :],
                                xnT[:, T * cc + 512 * tc: T * cc + 512 * tc + 512],
                                start=(cc == 0), stop=(cc == NCC - 1)))(),
                                reads=[b_xnT, b_w], writes=[b_ps])
                        add("vector", (lambda tc=tc, ps=ps, dst=dst: lambda e:
                            e.tensor_copy(dst[:, 512 * tc:512 * tc + 512],
                                          ps[:, 0:512]))(),
                            reads=[b_ps], writes=[b_dst])

                add("vector", lambda e: e.tensor_scalar(
                    qabs.bitcast(mybir.dt.uint16), qpair.bitcast(mybir.dt.uint16),
                    0x7FFF, None, ALU.bitwise_and),
                    reads=[b_qpair], writes=[b_qabs] + b_pgT[8:16])
                add("vector", lambda e: e.reduce_max(
                    stats[:, 61:62], kpair[:, :], axis=AX.X,
                    apply_absolute_value=True),
                    reads=[b_kpair], writes=[b_krmax])
                add("scalar", lambda e: e.copy(krb[:, :], stats[:, 61:62]),
                    reads=[b_krmax], writes=[b_krmax])
                if DEBUG_OUTPUTS and p == 0:
                    add("gpsimd", lambda e: e.dma_start(out=dbg["qa"].ap(), in_=qabs),
                        reads=[b_qabs], dma=True)
                    add("gpsimd", lambda e: e.dma_start(out=dbg["qp"].ap(), in_=qpair),
                        reads=[b_qpair], dma=True)
                    add("gpsimd", lambda e: e.dma_start(out=dbg["kp"].ap(), in_=kpair),
                        reads=[b_kpair], dma=True)
                    add("gpsimd", lambda e: e.dma_start(out=dbg["kr"].ap(), in_=krb[:, :]),
                        reads=[b_krmax], dma=True)
                for h01 in range(2):
                    rows = slice(64 * h01, 64 * h01 + 64)
                    for qg in range(8):
                        seg = 8 * h01 + qg
                        psM, b_psM = (PC, b_PC) if seg % 2 == 0 else (PD, b_PD)
                        add("tensor", (lambda rows=rows, qg=qg, psM=psM:
                                       lambda e: e.matmul(
                            psM[0:1, 0:512], krb[rows, 0:1],
                            qabs[rows, 512 * qg:512 * qg + 512],
                            start=True, stop=True))(),
                            reads=[b_krmax, b_qabs], writes=[b_psM])
                        add("vector", (lambda seg=seg, psM=psM: lambda e:
                                       e.reduce_max(
                            lparts[0:1, seg:seg + 1], psM[0:1, 0:512],
                            axis=AX.X))(),
                            reads=[b_psM], writes=[b_lparts])
                for h01 in range(2):
                    add("vector", (lambda h01=h01: lambda e: e.reduce_max(
                        lparts[0:1, 16 + h01:17 + h01],
                        lparts[0:1, 8 * h01:8 * h01 + 8], axis=AX.X))(),
                        reads=[b_lparts], writes=[b_lparts])
                for h01 in range(2):
                    add("scalar", (lambda h01=h01: lambda e: e.copy(
                        diag4[0:1, 256 + h01:257 + h01],
                        lparts[0:1, 16 + h01:17 + h01]))(),
                        reads=[b_lparts], writes=[b_diag4[2]])
                    psM, b_psM = (PC, b_PC) if h01 == 0 else (PD, b_PD)
                    add("tensor", (lambda h01=h01, psM=psM: lambda e: e.matmul(
                        psM[:, 0:1], diag4[0:1, 128:256],
                        diag4[0:1, 256 + h01:257 + h01],
                        start=True, stop=True))(),
                        reads=[b_diag4[1], b_diag4[2]], writes=[b_psM])
                    add("vector", (lambda h01=h01, psM=psM: lambda e:
                                   e.scalar_tensor_tensor(
                        negm[:, 16 + h01:17 + h01], psM[:, 0:1], 0.125,
                        negb16[:, 0:1], ALU.mult, ALU.add))(),
                        reads=[b_psM, b_negb], writes=[b_negmg])
                for h01 in range(2):
                    add("vector", (lambda h01=h01: lambda e: e.tensor_tensor(
                        negm[:, h01:h01 + 1], zeros_f[:, 0:1],
                        negm[:, 16 + h01:17 + h01], ALU.subtract))(),
                        reads=[b_zeros, b_negmg], writes=[b_negmg])

                pend_tail = [None, None]
                rot = [0]

                def emit_seg(h01, qg, pp=pp, p=p):
                    h = 2 * p + h01
                    rows = slice(64 * h01, 64 * h01 + 64)
                    m = 2 * pp + h01
                    seg = 8 * h01 + qg
                    jmax = 4 * qg + 3
                    psV, b_psV = (PE_, b_PE) if seg % 2 == 0 else (PF, b_PF)
                    tb = tmp_b[qg % 2]

                    def emit_s(j, r):
                        ii0 = max(0, j - 4 * qg)
                        ps, b_ps = ((PA, b_PA), (PA, b_PA), (PB, b_PB),
                                    (PB, b_PB), (PC, b_PC))[r]
                        c0 = 512 * (r % 2) if r < 4 else 0
                        blks = []
                        if 0 <= j - 4 * qg <= 3:
                            blks.append((j - 4 * qg, bdiag, b_bdiag[h]))
                        if 0 <= j - 4 * qg + 1 <= 3:
                            blks.append((j - 4 * qg + 1, bsub, b_bsub[h]))
                        add("tensor", (lambda j=j, ii0=ii0, c0=c0, ps=ps,
                                       rows=rows, qg=qg,
                                       st=(len(blks) == 0):
                                       lambda e: e.matmul(
                            ps[:, c0 + 128 * ii0:c0 + 512],
                            kpair[rows, 128 * j:128 * j + 128],
                            qpair[rows, 512 * qg + 128 * ii0:512 * qg + 512],
                            start=True, stop=st))(),
                            reads=[b_qpair, b_kpair], writes=[b_ps])
                        for bi, (ii, btile, bbuf) in enumerate(blks):
                            add("tensor", (lambda ii=ii, c0=c0, ps=ps, h=h,
                                           btile=btile,
                                           st=(bi == len(blks) - 1):
                                           lambda e: e.matmul(
                                ps[:, c0 + 128 * ii:c0 + 128 * ii + 128],
                                btile[:, 128 * h:128 * h + 128], ident[:, :],
                                start=False, stop=st))(),
                                reads=[bbuf, b_ident], writes=[b_ps])

                    def emit_exp(j0, ii0, r):
                        ps, b_ps = ((PA, b_PA), (PA, b_PA), (PB, b_PB),
                                    (PB, b_PB), (PC, b_PC))[r]
                        c0 = 512 * (r % 2) if r < 4 else 0
                        add("scalar", (lambda j0=j0, ii0=ii0, c0=c0,
                                       ps=ps, h01=h01: lambda e: e.activation(
                            pgrpT[:, 512 * j0 + 128 * ii0:512 * j0 + 512],
                            ps[:, c0 + 128 * ii0:c0 + 512], AF.Exp,
                            bias=negm[:, h01:h01 + 1], scale=0.125))(),
                            reads=[b_ps, b_negmg], writes=[b_pgT[j0]])

                    def emit_AV(j):
                        ii0 = max(0, j - 4 * qg)
                        add("tensor", (lambda j=j, ii0=ii0, m=m, psV=psV,
                                       jmax=jmax: lambda e: e.matmul(
                            psV[0:65, 128 * ii0:512],
                            vg4[:, j, m, :],
                            pgrpT[:, 512 * j + 128 * ii0:512 * j + 512],
                            start=(j == 0), stop=(j == jmax)))(),
                            reads=[b_vgrp, b_pgT[j]], writes=[b_psV])

                    def tail_a():
                        add("vector", (lambda psV=psV: lambda e: e.tensor_copy(
                            ysb[64:65, 0:512], psV[64:65, 0:512]))(),
                            reads=[b_psV], writes=[b_ysb[0]])
                        add("vector", lambda e: e.tensor_copy(
                            lparts[0:1, 20:21], lparts[0:1, 18:19]),
                            reads=[b_lparts], writes=[b_lparts])
                        add("vector", lambda e: e.reciprocal(
                            ysb[64:65, 0:512], ysb[64:65, 0:512]),
                            reads=[b_ysb[0]], writes=[b_ysb[0]])
                        add("vector", lambda e: e.tensor_copy(
                            lparts[0:1, 21:22], lparts[0:1, 18:19]),
                            reads=[b_lparts], writes=[b_lparts])
                        add("vector", lambda e: e.tensor_copy(
                            lrecip[64:65, 0:512], ysb[64:65, 0:512]),
                            reads=[b_ysb[0]],
                            writes=[b_lrec, b_xn1t[1]])

                    def tail_b():
                        add("tensor", lambda e: e.matmul(
                            PD[:, 0:512], diag4[64:65, 128:256],
                            lrecip[64:65, 0:512], start=True, stop=True),
                            reads=[b_diag4[1], b_lrec], writes=[b_PD])
                        add("vector", lambda e: e.tensor_copy(
                            ysb[:, 512:1024], PD[:, 0:512]),
                            reads=[b_PD], writes=[b_ysb[1]])
                        add("vector", lambda e: e.tensor_copy(
                            lparts[0:1, 22:23], lparts[0:1, 18:19]),
                            reads=[b_lparts], writes=[b_lparts])
                        if h01 == 0:
                            add("vector", (lambda qg=qg, psV=psV: lambda e:
                                           e.tensor_tensor(
                                attnT[0:64, 512 * qg:512 * qg + 512],
                                psV[0:64, 0:512], ysb[0:64, 512:1024],
                                ALU.mult))(),
                                reads=[b_psV, b_ysb[1]], writes=[b_attnT])
                        else:
                            add("vector", (lambda psV=psV, tb=tb: lambda e:
                                           e.tensor_tensor(
                                tb[0:64, 0:512], psV[0:64, 0:512],
                                ysb[0:64, 512:1024], ALU.mult))(),
                                reads=[b_psV, b_ysb[1]], writes=[b_xn1t[0]])
                            add("sync", (lambda qg=qg, tb=tb: lambda e:
                                           e.dma_start(
                                out=attnT[64:128, 512 * qg:512 * qg + 512],
                                in_=tb[0:64, 0:512]))(),
                                reads=[b_xn1t[0]], writes=[b_attnT], dma=True)

                    for j in range(jmax + 1):
                        r = rot[0] % 5
                        rot[0] += 1
                        emit_s(j, r)
                        emit_exp(j, max(0, j - 4 * qg), r)
                        if j == 3 and pend_tail[0] is not None:
                            for f in pend_tail[0]:
                                f()
                            pend_tail[0] = None
                        if j == 7 and pend_tail[1] is not None:
                            pend_tail[1]()
                            pend_tail[1] = None
                        if j >= 9:
                            emit_AV(j - 9)
                    if pend_tail[1] is not None:
                        pend_tail[1]()
                        pend_tail[1] = None
                    if qg == 0:
                        for j in range(4):
                            emit_AV(j)
                        pend_tail[0] = [tail_a]
                    else:
                        for j in range(max(0, jmax - 8), jmax - 1):
                            emit_AV(j)
                        pend_tail[0] = [
                            (lambda j=jmax - 1: emit_AV(j)),
                            (lambda j=jmax: emit_AV(j)), tail_a]
                    pend_tail[1] = tail_b

                for h01 in range(2):
                    for qg in range(8):
                        emit_seg(h01, qg)
                for f in pend_tail[0]:
                    f()
                pend_tail[0] = None
                pend_tail[1]()
                pend_tail[1] = None

                add("sync", (lambda p=p: lambda e: e.dma_start(
                    out=spillA_h.ap()[p], in_=attnT[:, :]))(),
                    reads=[b_attnT], writes=[b_spillA[p]], dma=True)
                if DEBUG_OUTPUTS:
                    add("sync", (lambda p=p: lambda e: e.dma_start(
                        out=dbg["attnT"].ap()[p], in_=attnT[:, :]))(),
                        reads=[b_attnT], dma=True)

        for p in range(NPAIR):
            add("gpsimd", (lambda p=p: lambda e: e.dma_start(
                out=wp_sb[:, C * p:C * p + C],
                in_=wp_h[128 * p:128 * p + 128, :]))(),
                writes=[b_wp, b_qpair, b_kpair], dma=True)

        aslab3 = aslab.rearrange("p (par pr x) -> p par pr x", par=2, pr=NPAIR)

        def emit_aslab_read(g, par):
            for p in range(NPAIR):
                add("sync", (lambda g=g, p=p, par=par: lambda e: e.dma_start(
                    out=aslab3[:, par, p, :],
                    in_=spillA_h.ap()[p][:, 512 * g:512 * g + 512]))(),
                    reads=[b_spillA[p]],
                    writes=[b_aslab[par], b_pgT[31],
                            b_attnT, b_wqg, b_wkg], dma=True)

        ffn_ws = [(wr_h, wrT, b_wrT, [b_vgrp]),
                  (wkf_h, wkT, b_wkTb, b_pgT[0:16] + [b_qabs, b_masks, b_vgrp]),
                  (wvf_h, wvT, b_wvTb, b_pgT[15:32] + [b_qabs])]

        def emit_ffn_wslab(s):
            wsrc, wdst, b_wdst, extra = ffn_ws[s // NCC]
            sl = s % NCC
            par = s % 2
            ps, b_ps = (PE_, b_PE) if par == 0 else (PF, b_PF)
            sb_ap = slab[:, C * par:C * par + C]
            add("gpsimd", (lambda sl=sl, wsrc=wsrc, sb_ap=sb_ap: lambda e:
                           e.dma_start(out=sb_ap,
                                       in_=wsrc[128 * sl:128 * sl + 128, :]))(),
                writes=[b_slab[par], b_wkg, b_wvg], dma=True)
            wdst3 = wdst.rearrange("p (cc x) -> p cc x", cc=NCC)
            for half in range(2):
                for cc in range(4 * half, 4 * half + 4):
                    add("tensor", (lambda cc=cc, half=half, sb_ap=sb_ap, ps=ps:
                                   lambda e: e.matmul(
                        ps[:, 128 * (cc - 4 * half):128 * (cc - 4 * half) + 128],
                        sb_ap[:, 128 * cc:128 * cc + 128], ident[:, :],
                        start=True, stop=True))(),
                        reads=[b_slab[par], b_ident], writes=[b_ps])
                add("vector", (lambda sl=sl, half=half, wdst3=wdst3, ps=ps:
                               lambda e: e.tensor_copy(
                    wdst3[:, 4 * half:4 * half + 4, 128 * sl:128 * sl + 128],
                    ps[:, 0:512].rearrange("p (cc x) -> p cc x", cc=4)))(),
                    reads=[b_ps], writes=[b_wdst] + extra)

        xn2T3 = xn2T.rearrange("p (cc t) -> p cc t", cc=NCC)
        add("vector", lambda e: e.memset(xn2T3[:, :, 0:1], 0.0),
            writes=[b_xn2T, b_xnT])

        emit_aslab_read(0, 0)
        _t3_pending = []

        for k in range(NT // 2):
            iA, iB = 2 * k, 2 * k + 1
            g = iA // 4
            if iA % 4 == 0 and g + 1 < 8:
                emit_aslab_read(g + 1, (g + 1) % 2)
            gpar = g % 2
            tiles = []
            for i, par, psY, b_psY in ((iA, 0, PA, b_PA), (iB, 1, PB, b_PB)):
                xin = xt[:, C * par:C * par + C]
                add("sync", (lambda i=i, xin=xin: lambda e: e.dma_start(
                    out=xin, in_=x_h[128 * i:128 * i + 128, :]))(),
                    writes=[b_xt[par]], dma=True)
                tiles.append((i, par, psY, b_psY, xin))
            for (i, par, psY, b_psY, xin) in tiles:
                for p in range(NPAIR):
                    for hf in range(2):
                        add("tensor", (lambda i=i, p=p, hf=hf, gpar=gpar,
                                       psY=psY: lambda e: e.matmul(
                            psY[:, 512 * hf:512 * hf + 512],
                            aslab3[:, gpar, p, 128 * (i % 4):128 * (i % 4) + 128],
                            wp_sb[:, C * p + 512 * hf:C * p + 512 * hf + 512],
                            start=(p == 0), stop=(p == NPAIR - 1)))(),
                            reads=[b_aslab[gpar], b_wp], writes=[b_psY])
            for (i, par, psY, b_psY, xin) in tiles:
                ysl = ysb[:, C * par:C * par + C]
                add("vector", (lambda psY=psY, ysl=ysl: lambda e:
                               e.tensor_tensor(ysl, psY[:, :], bp_b[:, :],
                                               ALU.add))(),
                    reads=[b_psY, b_bp], writes=[b_ysb[par]])
            ln_pair([(ysb[:, C * par:C * par + C], b_ysb[par],
                      y2[:, C * par:C * par + C], b_y2[par], par,
                      y2[:, C * par:C * par + C], b_y2[par])
                     for (i, par, psY, b_psY, xin) in tiles])
            for (i, par, psY, b_psY, xin) in tiles:
                y2l = y2[:, C * par:C * par + C]
                add("vector", (lambda xin=xin, y2l=y2l: lambda e:
                               e.tensor_tensor(xin, y2l, xin, ALU.add))(),
                    reads=[b_y2[par], b_xt[par]], writes=[b_xt[par]])
            for (i, par, psY, b_psY, xin) in tiles:
                add("gpsimd", (lambda i=i, xin=xin: lambda e: e.dma_start(
                    out=spillX_h[128 * i:128 * i + 128, :], in_=xin))(),
                    reads=[b_xt[par]], writes=[b_spillX[i]], dma=True)
                if DEBUG_OUTPUTS:
                    add("gpsimd", (lambda i=i, xin=xin: lambda e: e.dma_start(
                        out=dbg["xnew"][128 * i:128 * i + 128, :], in_=xin))(),
                        reads=[b_xt[par]], dma=True)
            while _t3_pending:
                _t3_pending.pop(0)()
            ln_pair([(xt[:, C * par:C * par + C], b_xt[par],
                      xn1t[:, C * par:C * par + C], b_xn1t[par], 2 + par,
                      y2[:, C * par:C * par + C], b_y2[par])
                     for (i, par, psY, b_psY, xin) in tiles])
            for (i, par, psY, b_psY, xin) in tiles:
                x2 = xn1t[:, C * par:C * par + C]

                def _t3(i=i, par=par, x2=x2):
                    psT, b_psT = (PC, b_PC) if par == 0 else (PD, b_PD)
                    for half in range(2):
                        for cc in range(4 * half, 4 * half + 4):
                            add("tensor", (lambda cc=cc, half=half, x2=x2,
                                           psT=psT: lambda e: e.matmul(
                                psT[:, 128 * (cc - 4 * half):
                                    128 * (cc - 4 * half) + 128],
                                x2[:, 128 * cc:128 * cc + 128], ident[:, :],
                                start=True, stop=True))(),
                                reads=[b_xn1t[par], b_ident], writes=[b_psT])
                        add("vector", (lambda i=i, half=half, psT=psT: lambda e:
                                       e.tensor_copy(
                            xn2T3[:, 4 * half:4 * half + 4,
                                  1 + 128 * i:1 + 128 * i + 128],
                            psT[:, 0:512].rearrange("p (cc x) -> p cc x",
                                                    cc=4)))(),
                            reads=[b_psT], writes=[b_xn2T])
                _t3_pending.append(_t3)
            if iA < 24:
                emit_ffn_wslab(iA)
            if iB < 24:
                emit_ffn_wslab(iB)
        for _fn in _t3_pending:
            _fn()
        _t3_pending = []

        xrT3 = xrT.rearrange("p (cc t) -> p cc t", cc=NCC)
        xkT3 = xkT.rearrange("p (cc t) -> p cc t", cc=NCC)
        kact3 = kact.rearrange("p (dd t) -> p dd t", dd=NCC)
        wrT3 = wrT.rearrange("p (cc x) -> p cc x", cc=NCC)
        wkT3 = wkT.rearrange("p (cc x) -> p cc x", cc=NCC)
        wvT3 = wvT.rearrange("p (dd x) -> p dd x", dd=NCC)

        def emit_mix(ch):
            t0 = 512 * ch
            for cc in range(NCC):
                prev = xn2T3[:, cc, t0:t0 + 512]
                cur = xn2T3[:, cc, t0 + 1:t0 + 513]
                add("vector", (lambda prev=prev, cur=cur: lambda e:
                               e.tensor_tensor(dscr[:, :], cur, prev,
                                               ALU.subtract))(),
                    reads=[b_xn2T], writes=[b_dscr, b_wvg])
                add("vector", (lambda cc=cc, prev=prev: lambda e:
                               e.scalar_tensor_tensor(
                    xrT3[:, cc, :], dscr[:, :], mu_sb[:, cc:cc + 1], prev,
                    ALU.mult, ALU.add))(),
                    reads=[b_dscr, b_mu, b_xn2T], writes=[b_xrT, b_wp])
                add("vector", (lambda cc=cc, prev=prev: lambda e:
                               e.scalar_tensor_tensor(
                    xkT3[:, cc, :], dscr[:, :], mu_sb[:, 8 + cc:8 + cc + 1], prev,
                    ALU.mult, ALU.add))(),
                    reads=[b_dscr, b_mu, b_xn2T], writes=[b_xkT, b_wp])

        def emit_rsig(ch):
            for sub in range(4):
                tpar = sub % 2
                psR, b_psR = (PA, b_PA) if tpar == 0 else (PB, b_PB)
                for hf in range(2):
                    for cc in range(NCC):
                        add("tensor", (lambda sub=sub, hf=hf, cc=cc, psR=psR:
                                       lambda e: e.matmul(
                            psR[:, 512 * hf:512 * hf + 512],
                            xrT3[:, cc, 128 * sub:128 * sub + 128],
                            wrT3[:, cc, 512 * hf:512 * hf + 512],
                            start=(cc == 0), stop=(cc == NCC - 1)))(),
                            reads=[b_xrT, b_wrT], writes=[b_psR])
                add("scalar", (lambda psR=psR: lambda e: e.activation(
                    sqd[:, :], psR[:, :], AF.Exp, bias=0.0, scale=-1.0))(),
                    reads=[b_psR], writes=[b_sqd])
                add("vector", lambda e: e.tensor_scalar(
                    sqd[:, :], sqd[:, :], 1.0, None, ALU.add),
                    reads=[b_sqd], writes=[b_sqd])
                add("vector", lambda e: e.reciprocal(sqd[:, :], sqd[:, :]),
                    reads=[b_sqd], writes=[b_sqd])
                add("scalar", (lambda sub=sub: lambda e: e.copy(
                    r_sb[:, C * sub:C * sub + C], sqd[:, :]))(),
                    reads=[b_sqd], writes=[b_rsb, b_aslab[0], b_aslab[1]])

        def emit_kpath(ch):
            for dd in range(NCC):
                ps, b_ps = (PC, b_PC) if dd % 2 == 0 else (PD, b_PD)
                for cc in range(NCC):
                    add("tensor", (lambda dd=dd, cc=cc, ps=ps: lambda e: e.matmul(
                        ps[:, 0:512], wkT3[:, cc, 128 * dd:128 * dd + 128],
                        xkT3[:, cc, :], start=(cc == 0), stop=(cc == NCC - 1)))(),
                        reads=[b_wkTb, b_xkT], writes=[b_ps])
                add("vector", (lambda ps=ps: lambda e: e.tensor_relu(
                    ps[:, 0:512], ps[:, 0:512]))(),
                    reads=[b_ps], writes=[b_ps])
                add("scalar", (lambda dd=dd, ps=ps: lambda e: e.activation(
                    kact3[:, dd, :], ps[:, 0:512], AF.Square, bias=0.0,
                    scale=1.0))(),
                    reads=[b_ps], writes=[b_kact, b_aslab[0], b_aslab[1]])

        def emit_vtail(ch):
            for sp in range(2):
                tl = []
                for sub in (2 * sp, 2 * sp + 1):
                    tpar = sub % 2
                    i = 4 * ch + sub
                    xin = xt[:, C * tpar:C * tpar + C]
                    add("gpsimd", (lambda i=i, xin=xin: lambda e: e.dma_start(
                        out=xin, in_=spillX_h[128 * i:128 * i + 128, :]))(),
                        reads=[b_spillX[i]], writes=[b_xt[tpar]], dma=True)
                    tl.append((sub, tpar, i, xin))
                for (sub, tpar, i, xin) in tl:
                    for hf in range(2):
                        psV, b_psV = (PE_, b_PE) if hf == 0 else (PF, b_PF)
                        for dd in range(NCC):
                            add("tensor", (lambda sub=sub, hf=hf, dd=dd, psV=psV:
                                           lambda e: e.matmul(
                                psV[:, 0:512],
                                kact3[:, dd, 128 * sub:128 * sub + 128],
                                wvT3[:, dd, 512 * hf:512 * hf + 512],
                                start=(dd == 0), stop=(dd == NCC - 1)))(),
                                reads=[b_kact, b_wvTb], writes=[b_psV])
                        add("vector", (lambda sub=sub, hf=hf, psV=psV, tpar=tpar:
                                       lambda e: e.tensor_tensor(
                            ysb[:, C * tpar + 512 * hf:C * tpar + 512 * hf + 512],
                            psV[:, 0:512],
                            r_sb[:, C * sub + 512 * hf:C * sub + 512 * hf + 512],
                            ALU.mult))(),
                            reads=[b_psV, b_rsb], writes=[b_ysb[tpar]])
                ln_pair([(ysb[:, C * tpar:C * tpar + C], b_ysb[tpar],
                          y2[:, C * tpar:C * tpar + C], b_y2[tpar], tpar,
                          y2[:, C * tpar:C * tpar + C], b_y2[tpar])
                         for (sub, tpar, i, xin) in tl])
                for (sub, tpar, i, xin) in tl:
                    y2l = y2[:, C * tpar:C * tpar + C]
                    add("vector", (lambda xin=xin, y2l=y2l: lambda e:
                                   e.tensor_tensor(
                        xin, y2l, xin, ALU.add))(),
                        reads=[b_y2[tpar], b_xt[tpar]], writes=[b_xt[tpar]])
                    add("sync", (lambda i=i, xin=xin: lambda e: e.dma_start(
                        out=out_h[128 * i:128 * i + 128, :], in_=xin))(),
                        reads=[b_xt[tpar]], writes=[b_out[i]], dma=True)

        emit_mix(0)
        for ch in range(8):
            emit_rsig(ch)
            emit_kpath(ch)
            if ch + 1 < 8:
                emit_mix(ch + 1)
            emit_vtail(ch)

        add("sync", lambda e: e.nop(), deps=[b_out[NT - 1].w])

        with nc.Block() as block:
            prog.emit(block, sems)

    return nc


_NC_CACHE = None


def kernel(L0_attn_x, L0_attn_Wq, L0_attn_Wk, L0_attn_Wv, L0_attn_Wp,
           L0_attn_bp, L0_attn_rel_table, L0_ffn_W_r, L0_ffn_W_k, L0_ffn_W_v,
           L0_ffn_mu_r, L0_ffn_mu_k):
    global _NC_CACHE
    cc = np.ascontiguousarray
    x = cc(np.asarray(L0_attn_x, dtype=np.float32))
    ins = {
        "wq": cc(np.asarray(L0_attn_Wq, dtype=np.float32)),
        "wk": cc(np.asarray(L0_attn_Wk, dtype=np.float32)),
        "wv": cc(np.asarray(L0_attn_Wv, dtype=np.float32)),
        "wp": cc(np.asarray(L0_attn_Wp, dtype=np.float32)),
        "bp": cc(np.asarray(L0_attn_bp, dtype=np.float32)),
        "rel": cc(np.asarray(L0_attn_rel_table, dtype=np.float32)).reshape(-1),
        "wr": cc(np.asarray(L0_ffn_W_r, dtype=np.float32)),
        "wkf": cc(np.asarray(L0_ffn_W_k, dtype=np.float32)),
        "wvf": cc(np.asarray(L0_ffn_W_v, dtype=np.float32)),
        "mur": cc(np.asarray(L0_ffn_mu_r, dtype=np.float32)),
        "muk": cc(np.asarray(L0_ffn_mu_k, dtype=np.float32)),
    }
    if _NC_CACHE is None:
        _NC_CACHE = build_program()
    nc = _NC_CACHE
    in_maps = [dict(ins, x=cc(x[b])) for b in range(8)]
    res = run_bass_kernel_spmd(nc, in_maps, core_ids=list(range(8)))
    return np.stack([res.results[b]["out"] for b in range(8)], axis=0).astype(np.float32)
